# version 02
# baseline (speedup 1.0000x reference)
import contextlib
import numpy as np
import ml_dtypes
import concourse.bass as bass
import concourse.mybir as mybir
from concourse.bass_utils import run_bass_kernel_spmd

F32 = mybir.dt.float32
BF16 = mybir.dt.bfloat16
I32 = mybir.dt.int32
AF = mybir.ActivationFunctionType
ALU = mybir.AluOpType
AX = mybir.AxisListType
BFNP = ml_dtypes.bfloat16

NEG = -3.0e37
NEGW0 = -1.0e37
DIM = 1024
DI = 2048
NST = 16
DTR = 64
HALO = 2560
TOK = 4096
TOTC = HALO + TOK
NCH = TOTC // 512
L1Q0 = HALO - 2048
HID2 = 2816
MHID = 4096
LN32 = float(np.log(32.0))

CKVALS = [0.0, 1.0, -1.0, 1e-5, -0.5, LN32, 1.0 / 1024.0, 1e-24, 8.0, -1e30]
CK = {v: i for i, v in enumerate(CKVALS)}
NCK = len(CKVALS)


class S:
    def __init__(self, h):
        self.h = h
        self.n = 0

    def inc(self, instr, k=1):
        instr.then_inc(self.h, k)
        self.n += k
        return self.n


class K:
    def __init__(self, dbg=(), phases=("p1", "ex1", "ffn", "scan", "ex2",
                                       "tail", "moe", "comb")):
        self.dbg = set(dbg)
        self.phases = set(phases)
        self.nc = bass.Bass("TRN2", target_bir_lowering=False, debug=False)
        self.st = contextlib.ExitStack()

    def dram(self, name, shape, dt, kind=None):
        kk = {}
        if kind is not None:
            kk["kind"] = kind
        elif name in self.dbg:
            kk["kind"] = "ExternalOutput"
        return self.nc.dram_tensor(name, shape, dt, **kk).ap()

    def sb(self, n, sh, dt):
        return self.st.enter_context(self.nc.sbuf_tensor(n, sh, dt))

    def psumt(self, n, sh, dt):
        return self.st.enter_context(self.nc.psum_tensor(n, sh, dt))

    def sem(self, n):
        return S(self.st.enter_context(self.nc.semaphore(n)))

    def w(self, eng, *deps):
        for s_, c_ in deps:
            if c_ > 0:
                eng.wait_ge(s_.h, c_)

    def ck(self, val, row=False, p=None):
        j = CK[val]
        if p is not None:
            return self.c_ck[0:p, j:j + 1]
        return self.c_ck[0:1, j:j + 1] if row else self.c_ck[:, j:j + 1]

    def bank_wait(self, eng, nm):
        for dep in self.bfree[nm]:
            self.w(eng, dep)
        self.bfree[nm] = []

    def bank_used(self, nm, dep):
        self.bfree[nm].append(dep)

    def enter_phase(self):
        deps = []
        for s_ in ("dS", "dG1", "dG2", "d_wt", "d_u", "d_bc", "d_ym", "d_sp2",
                   "dZg", "d_sl", "d_ug", "d_ffs", "d_tl", "d_mx", "d_cl",
                   "dZs", "d_ffs2", "d_mx2", "d_sl2", "d_ym2",
                   "d_xn", "d_xr", "d_res", "d_dt", "d_y", "d_xa",
                   "d_x1", "d_g", "d_xm", "d_sp", "d_out", "cC",
                   "pe", "ve", "ac", "po"):
            s = getattr(self, s_)
            if s.n > 0:
                deps.append((s, s.n))
        for eng in (self.T, self.V, self.A, self.G, self.SP):
            self.w(eng, *deps)

    def declare(self):
        d = self.dram
        EI = "ExternalInput"
        self.xfm = d("xfm", [DIM, TOTC], F32, EI)
        self.tqc = d("tqc", [128, 6144], BF16, EI)
        self.tqs = d("tqs", [128, 6144], BF16, EI)
        self.tkc = d("tkc", [128, 6144], BF16, EI)
        self.tks = d("tks", [128, 6144], BF16, EI)
        self.m_triu = d("m_triu", [128, 128], F32, EI)
        self.m_tril = d("m_tril", [128, 128], F32, EI)
        self.m_w0 = d("m_w0", [128, 512], F32, EI)
        self.x1zv = d("x1zv", [128, 1], F32, EI)
        self.parv = d("parv", [1, 1], I32, EI)
        self.identv = d("identv", [128, 128], BF16, EI)
        self.ident32v = d("ident32", [128, 128], F32, EI)
        self.onesrv = d("onesrv", [1, 128], BF16, EI)
        self.ind2v = d("ind2v", [128, 2], BF16, EI)
        self.rep2v = d("rep2v", [2, 128], BF16, EI)
        self.rotptv = d("rotptv", [128, 128], BF16, EI)
        self.ckv = d("ckv", [128, NCK], F32, EI)
        self.convwv = d("convwv", [128, 64], F32, EI)
        self.convbv = d("convbv", [128, 16], F32, EI)
        self.dvecv = d("dvecv", [128, 16], F32, EI)
        self.dtbv = d("dtbv", [128, 8], F32, EI)
        self.qscv = d("qscv", [128, 2], F32, EI)
        self.kscv = d("kscv", [128, 2], F32, EI)
        self.lngv = d("lngv", [128, 16], F32, EI)
        self.lnbv = d("lnbv", [128, 16], F32, EI)
        self.b1av = d("b1av", [128, 44], F32, EI)
        self.b1gv = d("b1gv", [128, 44], F32, EI)
        self.fb2v = d("fb2v", [128, 16], F32, EI)
        self.alogv = d("alogv", [1024, 16], F32, EI)
        self.wqkv = d("wqkv", [2, DIM, 1536], F32, EI)
        self.wo_w = d("wo_w", [2, 512, DIM], F32, EI)
        self.w1av = d("w1av", [2, DIM, HID2], F32, EI)
        self.w1gv = d("w1gv", [2, DIM, HID2], F32, EI)
        self.w2pv = d("w2pv", [2, HID2, DIM], F32, EI)
        self.inp_t = d("inp_t", [DIM, 2 * DI], F32, EI)
        self.xproj_t = d("xproj_t", [DI, 128], F32, EI)
        self.dtproj_t = d("dtproj_t", [DTR, 1024], F32, EI)
        self.outp_t = d("outp_t", [DI, DIM], F32, EI)
        self.gate_t = d("gate_t", [DIM, 8], F32, EI)
        self.mw1 = d("mw1", [8, DIM, MHID], F32, EI)
        self.mw2 = d("mw2", [8, MHID, DIM], F32, EI)
        self.xn_d = d("xn_d", [DIM, TOTC], BF16)
        self.u_d = d("u_d", [DI, TOK], BF16)
        self.res_d = d("res_d", [DI, TOK], BF16)
        self.xr1_d = d("xr1_d", [DIM, TOK], BF16)
        self.dt_own = d("dt_own", [DTR, TOK], F32)
        self.bc_own = d("bc_own", [32, TOK], BF16)
        self.uoth_c = d("uoth_c", [1024, TOK], BF16)
        self.uoth_g = d("uoth_g", [2048, TOK], BF16)
        self.dt_gt = d("dt_gt", [2 * DTR, TOK], F32)
        self.bc_gt = d("bc_gt", [64, TOK], BF16)
        self.u_sc = d("u_sc", [1024, 8192], BF16)
        self.dt_sc = d("dt_sc", [DTR, 8192], F32)
        self.bc_sc = d("bc_sc", [32, 8192], BF16)
        self.y_dd = d("y_dd", [1024, 8192], BF16)
        self.y_cc = d("y_cc", [1024, TOK], BF16)
        self.y_gg = d("y_gg", [2048, TOK], BF16)
        self.y_alld = d("y_alld", [DI, TOK], BF16)
        self.xatt_d = d("xatt_d", [DIM, 5632], BF16)
        self.x1_d = d("x1_d", [DIM, 5632], BF16)
        self.g_dd = d("g_dd", [DIM, TOK], BF16)
        self.xm_d = d("xm_d", [DIM, TOK], BF16)
        self.sp_d = d("sp_d", [8, TOK], F32)
        self.ymoe_d = d("ymoe_d", [DIM, TOK], BF16)
        self.out_fm = d("out_fm", [DIM, TOK], F32, "ExternalOutput")

    def alloc(self):
        sb, ps = self.sb, self.psumt
        self.c_id = sb("c_id", [128, 128], BF16)
        self.c_id32 = sb("c_id32", [128, 128], F32)
        self.c_ones = sb("c_ones", [1, 128], BF16)
        self.c_onec = sb("c_onec", [128, 1], BF16)
        self.c_ind2 = sb("c_ind2", [128, 2], BF16)
        self.c_rep2 = sb("c_rep2", [2, 128], BF16)
        self.c_rotp = sb("c_rotp", [128, 128], BF16)
        self.c_triu = sb("c_triu", [128, 128], F32)
        self.c_tril = sb("c_tril", [128, 128], F32)
        self.c_w0 = sb("c_w0", [128, 512], F32)
        self.c_x1z = sb("c_x1z", [128, 1], F32)
        self.c_ck = sb("c_ck", [128, NCK], F32)
        self.c_convw = sb("c_convw", [128, 64], F32)
        self.c_convb = sb("c_convb", [128, 16], F32)
        self.c_dvec = sb("c_dvec", [128, 16], F32)
        self.c_dtb = sb("c_dtb", [128, 8], F32)
        self.c_qsc = sb("c_qsc", [128, 2], F32)
        self.c_ksc = sb("c_ksc", [128, 2], F32)
        self.c_lng = sb("c_lng", [128, 16], F32)
        self.c_lnb = sb("c_lnb", [128, 16], F32)
        self.c_b1a = sb("c_b1a", [128, 44], F32)
        self.c_b1g = sb("c_b1g", [128, 44], F32)
        self.c_fb2 = sb("c_fb2", [128, 16], F32)
        self.c_A = sb("c_A", [128, 128], F32)
        self.c_par = sb("c_par", [1, 1], I32)
        self.g0 = ps("g0", [128, 512], F32)
        self.g1 = ps("g1", [128, 512], F32)
        self.g2 = ps("g2", [128, 512], F32)
        self.g3 = ps("g3", [128, 512], F32)
        self.g4 = ps("g4", [128, 1024], BF16)
        self.g5 = ps("g5", [128, 512], F32)
        self.g6 = ps("g6", [128, 512], F32)
        self.g7 = ps("g7", [128, 512], F32)
        self.bfree = {nm: [] for nm in
                      ("g0", "g1", "g2", "g3", "g4", "g5", "g6", "g7")}
        for nm in ("dS", "dG1", "dG2", "d_wt", "d_u", "d_bc", "d_ym",
                   "d_sp2", "dZg", "d_sl", "d_ug", "d_ffs", "d_tl", "d_mx",
                   "d_cl", "dZs", "d_ffs2", "d_mx2", "d_sl2", "d_ym2",
                   "d_xn", "d_xr", "d_res", "d_dt", "d_y",
                   "d_xa", "d_x1", "d_g", "d_xm", "d_sp", "d_out", "cC",
                   "pe", "ve", "ac", "po"):
            setattr(self, nm, self.sem(nm))
        nc = self.nc
        self.T, self.V, self.A, self.G, self.SP = (
            nc.tensor, nc.vector, nc.scalar, nc.gpsimd, nc.sync)

    def p0_consts(self):
        V, A, SP = self.V, self.A, self.SP
        w, dS, ve, ac = self.w, self.dS, self.ve, self.ac
        pairs = [
            (self.c_id, self.identv), (self.c_id32, self.ident32v),
            (self.c_ones, self.onesrv),
            (self.c_ind2, self.ind2v), (self.c_rep2, self.rep2v),
            (self.c_rotp, self.rotptv), (self.c_triu, self.m_triu),
            (self.c_tril, self.m_tril), (self.c_w0, self.m_w0),
            (self.c_x1z, self.x1zv), (self.c_ck, self.ckv),
            (self.c_convw, self.convwv), (self.c_convb, self.convbv),
            (self.c_dvec, self.dvecv), (self.c_dtb, self.dtbv),
            (self.c_qsc, self.qscv), (self.c_ksc, self.kscv),
            (self.c_lng, self.lngv), (self.c_lnb, self.lnbv),
            (self.c_b1a, self.b1av), (self.c_b1g, self.b1gv),
            (self.c_fb2, self.fb2v), (self.c_par, self.parv),
        ]
        for s_, d_ in pairs:
            dS.inc(SP.dma_start(out=s_[:], in_=d_), 16)
        for gi in range(8):
            dS.inc(SP.dma_start(out=self.c_A[:, gi * 16:(gi + 1) * 16],
                                in_=self.alogv[gi * 128:(gi + 1) * 128, :]), 16)
        ve.inc(self.V.memset(self.c_onec[:], 1.0))
        w(A, (dS, dS.n))
        ac.inc(A.activation(self.c_A[:], self.c_A[:], AF.Exp,
                            bias=self.ck(0.0), scale=self.ck(1.0)))
        w(V, (ac, ac.n))
        ve.inc(V.tensor_scalar_mul(self.c_A[:], self.c_A[:], -1.0))
        self.cr = (ve, ve.n)
        self.crd = (dS, dS.n)

    def p1_front(self):
        nc = self.nc
        T, V, A, G, SP = self.T, self.V, self.A, self.G, self.SP
        w = self.w
        dS = self.dS
        pe, ve, ac = self.pe, self.ve, self.ac
        g0, g1, g2, g3 = self.g0, self.g1, self.g2, self.g3
        self.enter_phase()
        est = contextlib.ExitStack()
        sbl = lambda n, sh, dt: est.enter_context(nc.sbuf_tensor(n, sh, dt))
        with est:
            xb = [sbl(f"p1xb{i}", [128, 8 * 512], BF16) for i in range(2)]
            sq = sbl("p1sq", [128, 8 * 512], BF16)
            rmu = sbl("p1rmu", [1, 512], F32)
            rva = sbl("p1rva", [1, 512], F32)
            rtm = sbl("p1rtm", [1, 512], F32)
            rmub = sbl("p1rmub", [1, 512], BF16)
            rvab = sbl("p1rvab", [1, 512], BF16)
            xnt = sbl("p1xn", [128, 8 * 512], BF16)
            xr1t = sbl("p1xr1", [128, 8 * 512], BF16)
            xmb = sbl("p1xm", [128, 16 * 515], F32)
            resb = [sbl(f"p1res{i}", [128, 512], BF16) for i in range(2)]
            xct = sbl("p1xc", [128, 16 * 512], F32)
            ubt = sbl("p1u", [128, 16 * 512], BF16)
            xdb = sbl("p1xdb", [128, 512], F32)
            wip = sbl("p1wip", [128, 8 * 4096], BF16)
            wxp = sbl("p1wxp", [128, 16 * 128], BF16)

            t8 = lambda buf, i: buf[:, i * 512:(i + 1) * 512]

            self.d_wt.inc(G.dma_start(
                out=wip.rearrange("p (a n) -> p a n", a=8),
                in_=self.inp_t.rearrange("(a p) n -> p a n", p=128)), 16)
            self.d_wt.inc(G.dma_start(
                out=wxp.rearrange("p (a n) -> p a n", a=16),
                in_=self.xproj_t.rearrange("(a p) n -> p a n", p=128)), 16)
            wready = (self.d_wt, self.d_wt.n)

            ldsem = [self.dG1, self.dG2]
            xb_ready = [(dS, 0), (dS, 0)]
            xb_free = [(ve, 0), (ve, 0)]
            sq_free = (pe, 0)
            xnt_free = (ac, 0)
            xct_free = (ac, 0)
            xm_conv_done = (ve, 0)
            res_free_v = [(ve, 0), (ve, 0)]

            for c in range(NCH):
                pb = c % 2
                w(G, xb_free[pb])
                ldsem[pb].inc(G.dma_start(
                    out=xb[pb].rearrange("p (a n) -> p a n", a=8),
                    in_=self.xfm[:, c * 512:(c + 1) * 512]
                        .rearrange("(a p) n -> p a n", p=128)), 16)
                xb_ready[pb] = (ldsem[pb], ldsem[pb].n)

                w(A, xb_ready[pb], sq_free, self.crd)
                for i in range(8):
                    ac.inc(A.activation(t8(sq, i), t8(xb[pb], i), AF.Square,
                                        bias=self.ck(0.0), scale=self.ck(1.0)))
                self.bank_wait(T, "g2")
                self.bank_wait(T, "g3")
                w(T, xb_ready[pb], (ac, ac.n), self.cr)
                for i in range(8):
                    T.matmul(g2[0:1, :], self.c_onec[:], t8(xb[pb], i),
                             start=(i == 0), stop=(i == 7))
                for i in range(8):
                    mm = T.matmul(g3[0:1, :], self.c_onec[:], t8(sq, i),
                                  start=(i == 0), stop=(i == 7))
                pe.inc(mm)
                sq_free = (pe, pe.n)
                w(V, (pe, pe.n), (ac, ac.n))
                ve.inc(V.tensor_scalar_mul(rmu[:], g2[0:1, :], 1.0 / 1024.0))
                ve.inc(V.tensor_scalar_mul(rva[:], g3[0:1, :], 1.0 / 1024.0))
                w(V, (ve, ve.n))
                ve.inc(V.tensor_mul(rtm[:], rmu[:], rmu[:]))
                w(V, (ve, ve.n))
                ve.inc(V.tensor_sub(rva[:], rva[:], rtm[:]))
                self.bank_used("g2", (ve, ve.n))
                self.bank_used("g3", (ve, ve.n))
                w(A, (ve, ve.n))
                ac.inc(A.activation(rva[:], rva[:], AF.Ln,
                                    bias=self.ck(1e-5, True), scale=self.ck(1.0, True)))
                w(A, (ac, ac.n))
                ac.inc(A.activation(rva[:], rva[:], AF.Exp,
                                    bias=self.ck(0.0, True), scale=self.ck(-0.5, True)))
                w(V, (ac, ac.n))
                ve.inc(V.tensor_copy(rmub[:], rmu[:]))
                ve.inc(V.tensor_copy(rvab[:], rva[:]))
                self.bank_wait(T, "g2")
                self.bank_wait(T, "g3")
                w(T, (ve, ve.n))
                m1 = T.matmul(g2[:], self.c_ones[:], rmub[:], start=True, stop=True)
                m2 = T.matmul(g3[:], self.c_ones[:], rvab[:], start=True, stop=True)
                pe.inc(m2)
                w(V, (pe, pe.n), (self.d_xn, self.d_xn.n), xnt_free)
                for i in range(8):
                    ve.inc(V.tensor_sub(t8(xnt, i), t8(xb[pb], i), g2[:]))
                w(V, (ve, ve.n))
                for i in range(8):
                    ve.inc(V.tensor_mul(t8(xnt, i), t8(xnt, i), g3[:]))
                xb_free[pb] = (ve, ve.n)
                self.bank_used("g2", (ve, ve.n))
                self.bank_used("g3", (ve, ve.n))
                xn_done = (ve, ve.n)
                w(A, xn_done)
                self.d_xn.inc(A.dma_start(
                    out=self.xn_d[:, c * 512:(c + 1) * 512]
                        .rearrange("(a p) n -> p a n", p=128),
                    in_=xnt.rearrange("p (a n) -> p a n", a=8)), 16)

                if c < 4:
                    continue

                w(A, xn_done, sq_free)
                for i in range(8):
                    ac.inc(A.activation(t8(sq, i), t8(xnt, i), AF.Square,
                                        bias=self.ck(0.0), scale=self.ck(1.0)))
                xnt_free = (ac, ac.n)
                self.bank_wait(T, "g2")
                w(T, (ac, ac.n))
                for i in range(8):
                    mm = T.matmul(g2[0:1, :], self.c_onec[:], t8(sq, i),
                                  start=(i == 0), stop=(i == 7))
                pe.inc(mm)
                sq_free = (pe, pe.n)
                w(V, (pe, pe.n))
                ve.inc(V.tensor_scalar_max(rtm[:], g2[0:1, :], 1e-24))
                self.bank_used("g2", (ve, ve.n))
                w(A, (ve, ve.n))
                ac.inc(A.activation(rtm[:], rtm[:], AF.Ln,
                                    bias=self.ck(0.0, True), scale=self.ck(1.0, True)))
                w(A, (ac, ac.n))
                ac.inc(A.activation(rtm[:], rtm[:], AF.Exp,
                                    bias=self.ck(LN32, True), scale=self.ck(-0.5, True)))
                w(V, (ac, ac.n), (pe, pe.n))
                ve.inc(V.tensor_copy(rvab[:], rtm[:]))
                self.bank_wait(T, "g2")
                w(T, (ve, ve.n))
                m1 = T.matmul(g2[:], self.c_ones[:], rvab[:], start=True, stop=True)
                pe.inc(m1)
                w(V, (pe, pe.n), (self.d_xr, self.d_xr.n))
                for i in range(8):
                    ve.inc(V.tensor_mul(t8(xr1t, i), t8(xnt, i), g2[:]))
                self.bank_used("g2", (ve, ve.n))
                xr1_done = (ve, ve.n)
                if c >= 5:
                    w(A, xr1_done)
                    self.d_xr.inc(A.dma_start(
                        out=self.xr1_d[:, (c - 5) * 512:(c - 4) * 512]
                            .rearrange("(a p) n -> p a n", p=128),
                        in_=xr1t.rearrange("p (a n) -> p a n", a=8)), 16)

                w(V, xm_conv_done, xct_free)
                if c >= 5:
                    ve.inc(V.tensor_copy(
                        xmb.rearrange("p (a n) -> p a n", a=16)[:, :, 0:3],
                        xmb.rearrange("p (a n) -> p a n", a=16)[:, :, 512:515]))
                carry_done = (ve, ve.n)

                banks = [g0, g1, self.g5, self.g6]
                bnames = ["g0", "g1", "g5", "g6"]
                w(T, xr1_done, wready)
                for mi in range(32):
                    bk = banks[mi % 4]
                    bn = bnames[mi % 4]
                    self.bank_wait(T, bn)
                    for kk in range(8):
                        mm = T.matmul(bk[:], wip[:, kk * 4096 + mi * 128:
                                                 kk * 4096 + (mi + 1) * 128],
                                      t8(xr1t, kk), start=(kk == 0), stop=(kk == 7))
                    pe.inc(mm)
                    mmdep = (pe, pe.n)
                    if mi < 16:
                        w(V, mmdep, carry_done)
                        ve.inc(V.tensor_copy(
                            xmb[:, mi * 515 + 3:mi * 515 + 515], bk[:]))
                        self.bank_used(bn, (ve, ve.n))
                    else:
                        rb = resb[mi % 2]
                        w(V, mmdep, (self.d_res, self.d_res.n),
                          res_free_v[mi % 2])
                        ve.inc(V.tensor_copy(rb[:], bk[:]))
                        res_free_v[mi % 2] = (ve, ve.n)
                        self.bank_used(bn, (ve, ve.n))
                        if c >= 5:
                            w(A, (ve, ve.n))
                            self.d_res.inc(A.dma_start(
                                out=self.res_d[(mi - 16) * 128:(mi - 15) * 128,
                                               (c - 5) * 512:(c - 4) * 512],
                                in_=rb[:]), 16)
                xm_ready = (ve, ve.n)

                if c < 5:
                    xm_conv_done = (ve, ve.n)
                    continue

                w(V, xm_ready, xct_free)
                for j in range(16):
                    base = j * 515
                    ve.inc(V.tensor_scalar(
                        xct[:, j * 512:(j + 1) * 512],
                        xmb[:, base:base + 512],
                        self.c_convw[:, j * 4 + 0:j * 4 + 1], None, ALU.mult))
                w(V, (ve, ve.n))
                for kk in range(1, 4):
                    for j in range(16):
                        base = j * 515
                        ve.inc(V.scalar_tensor_tensor(
                            xct[:, j * 512:(j + 1) * 512],
                            xmb[:, base + kk:base + kk + 512],
                            self.c_convw[:, j * 4 + kk:j * 4 + kk + 1],
                            xct[:, j * 512:(j + 1) * 512], ALU.mult, ALU.add))
                    w(V, (ve, ve.n))
                xm_conv_done = (ve, ve.n)
                w(A, (ve, ve.n), (pe, pe.n), (self.d_u, self.d_u.n))
                for j in range(16):
                    ac.inc(A.activation(ubt[:, j * 512:(j + 1) * 512],
                                        xct[:, j * 512:(j + 1) * 512], AF.Silu,
                                        bias=self.c_convb[:, j:j + 1],
                                        scale=self.ck(1.0)))
                u_done = (ac, ac.n)
                xct_free = (ac, ac.n)
                w(G, u_done)
                self.d_u.inc(G.dma_start(
                    out=self.u_d[:, (c - 5) * 512:(c - 4) * 512]
                        .rearrange("(a p) n -> p a n", p=128),
                    in_=ubt.rearrange("p (a n) -> p a n", a=16)), 16)

                self.bank_wait(T, "g3")
                w(T, u_done, wready)
                for kk in range(16):
                    mm = T.matmul(g3[:], wxp[:, kk * 128:(kk + 1) * 128],
                                  ubt[:, kk * 512:(kk + 1) * 512],
                                  start=(kk == 0), stop=(kk == 15))
                pe.inc(mm)
                w(V, (pe, pe.n), (self.d_dt, self.d_dt.n),
                  (self.d_bc, self.d_bc.n))
                ve.inc(V.tensor_copy(xdb[:], g3[:]))
                self.bank_used("g3", (ve, ve.n))
                w(A, (ve, ve.n))
                self.d_dt.inc(A.dma_start(
                    out=self.dt_own[:, (c - 5) * 512:(c - 4) * 512],
                    in_=xdb[0:64, :]), 16)
                w(G, (ve, ve.n))
                self.d_bc.inc(G.dma_start(
                    out=self.bc_own[:, (c - 5) * 512:(c - 4) * 512],
                    in_=xdb[64:96, :]), 16)
            self.p1_done = ((self.d_xn, self.d_xn.n), (self.d_xr, self.d_xr.n),
                            (self.d_res, self.d_res.n),
                            (self.d_u, self.d_u.n), (self.d_dt, self.d_dt.n),
                            (self.d_bc, self.d_bc.n))

    def ffn_branch(self):
        for lay in (0, 1):
            self.stream_dep = ((self.d_xn, self.d_xn.n) if lay == 0
                               else (self.d_x1, self.d_x1.n))
            self.ffn_passA(lay)
            self.xatt_dep = (self.d_xa, self.d_xa.n)
            self.ffn_passB(lay)
        self.ffn_done = (self.d_g, self.d_g.n)

    def ffn_passA(self, lay):
        nc = self.nc
        T, V, A, G, SP = self.T, self.V, self.A, self.G, self.SP
        w = self.w
        pe, ve, ac = self.pe, self.ve, self.ac
        g0, g1, g2, g3, g4, g5, g6, g7 = (self.g0, self.g1, self.g2, self.g3,
                                          self.g4, self.g5, self.g6, self.g7)
        stream = self.xn_d if lay == 0 else self.x1_d
        base = L1Q0 if lay == 0 else 512
        nqb = 12 if lay == 0 else 10
        tab0 = 0 if lay == 0 else 1024
        w0qb = 4 if lay == 0 else 2
        self.enter_phase()
        est = contextlib.ExitStack()
        sbl = lambda n, sh, dt: est.enter_context(
            nc.sbuf_tensor(f"{n}_{lay}", sh, dt))
        with est:
            wqb = sbl("a_wq", [128, 8 * 1536], BF16)
            wob = sbl("a_wo", [128, 4 * 1024], BF16)
            xin = [sbl(f"a_xi{i}", [128, 8 * 512], BF16) for i in range(2)]
            xsh = [sbl(f"a_xs{i}", [128, 4 * 512], BF16) for i in range(2)]
            tbq = sbl("a_tbq", [128, 2 * 512], BF16)
            tbk = sbl("a_tbk", [128, 2 * 512], BF16)
            qkvb = sbl("a_qkv", [128, 12 * 512], BF16)
            sq2 = sbl("a_sq2", [128, 512], BF16)
            rr2 = sbl("a_rr2", [2, 512], F32)
            rr2b = sbl("a_rr2b", [2, 512], BF16)
            qn = sbl("a_qn", [128, 512], BF16)
            t1b = sbl("a_t1", [128, 512], BF16)
            t2b = sbl("a_t2", [128, 512], BF16)
            qf = [sbl(f"a_qf{i}", [128, 8 * 512], BF16) for i in range(2)]
            vTc = [sbl(f"a_vT{i}", [128, 4 * 512], BF16) for i in range(2)]
            eAp = [sbl(f"a_eA{i}", [128, 512], BF16) for i in range(2)]
            eBp = [sbl(f"a_eB{i}", [128, 128], BF16) for i in range(2)]
            aTp = [sbl(f"a_aT{i}", [128, 640], BF16) for i in range(2)]
            rA = sbl("a_rA", [128, 1], F32)
            rB = sbl("a_rB", [128, 1], F32)
            r1s = sbl("a_r1s", [128, 1], F32)
            sAp = [sbl(f"a_sA{i}", [128, 1], F32) for i in range(2)]
            sBp = [sbl(f"a_sB{i}", [128, 1], F32) for i in range(2)]
            osb = sbl("a_osb", [128, 4 * 512], BF16)
            xatt = sbl("a_xatt", [128, 8 * 512], BF16)

            pdone = getattr(self, "p1_done", ())
            w(G, *pdone)
            self.d_wt.inc(G.dma_start(
                out=wqb.rearrange("p (a n) -> p a n", a=8),
                in_=self.wqkv[lay].rearrange("(a p) n -> p a n", p=128)), 16)
            self.d_wt.inc(G.dma_start(
                out=wob.rearrange("p (a n) -> p a n", a=4),
                in_=self.wo_w[lay].rearrange("(a p) n -> p a n", p=128)), 16)
            wrdy = (self.d_wt, self.d_wt.n)

            xin_free = [(pe, 0), (pe, 0)]
            tb_free = (ve, 0)
            qf_free = [(pe, 0), (pe, 0)]
            vT_free = [(pe, 0), (pe, 0)]
            qkvb_free = (ve, 0)
            osb_free = (pe, 0)
            xatt_free = (self.d_xa, 0)
            eA_free = [(pe, 0), (pe, 0)]
            eB_free = [(pe, 0), (pe, 0)]
            aT_free = [(pe, 0), (pe, 0)]
            t8 = lambda b, i: b[:, i * 512:(i + 1) * 512]

            for bi in range(nqb):
                cur = bi % 2
                prv = 1 - cur
                dffs = self.d_ffs if cur == 0 else self.d_ffs2
                sc0 = base + bi * 512
                w(SP, xin_free[cur], self.stream_dep, *pdone)
                dffs.inc(SP.dma_start(
                    out=xin[cur].rearrange("p (a n) -> p a n", a=8),
                    in_=stream[:, sc0:sc0 + 512]
                        .rearrange("(a p) n -> p a n", p=128)), 16)
                dffs.inc(SP.dma_start(
                    out=xsh[cur].rearrange("p (a n) -> p a n", a=4),
                    in_=stream[512:1024, sc0 - 1:sc0 + 511]
                        .rearrange("(a p) n -> p a n", p=128)), 16)
                w(SP, tb_free)
                dffs.inc(SP.dma_start(
                    out=tbq[:, 0:512], in_=self.tqc[:, tab0 + bi * 512:
                                                    tab0 + (bi + 1) * 512]), 16)
                dffs.inc(SP.dma_start(
                    out=tbq[:, 512:1024], in_=self.tqs[:, tab0 + bi * 512:
                                                       tab0 + (bi + 1) * 512]), 16)
                dffs.inc(SP.dma_start(
                    out=tbk[:, 0:512], in_=self.tkc[:, tab0 + bi * 512:
                                                    tab0 + (bi + 1) * 512]), 16)
                dffs.inc(SP.dma_start(
                    out=tbk[:, 512:1024], in_=self.tks[:, tab0 + bi * 512:
                                                       tab0 + (bi + 1) * 512]), 16)
                ldrdy = (dffs, dffs.n)
                banks = [g0, g1]
                bn = ["g0", "g1"]
                w(T, ldrdy, wrdy, qkvb_free)
                for mi in range(12):
                    bk = banks[mi % 2]
                    self.bank_wait(T, bn[mi % 2])
                    for kk in range(8):
                        rhs = t8(xin[cur], kk) if kk < 4 else t8(xsh[cur], kk - 4)
                        mm = T.matmul(bk[:], wqb[:, kk * 1536 + mi * 128:
                                                  kk * 1536 + (mi + 1) * 128],
                                      rhs, start=(kk == 0), stop=(kk == 7))
                    pe.inc(mm)
                    w(V, (pe, pe.n))
                    ve.inc(V.tensor_copy(t8(qkvb, mi), bk[:]))
                    self.bank_used(bn[mi % 2], (ve, ve.n))
                qkv_rdy = (ve, ve.n)
                for i in range(8):
                    w(A, qkv_rdy, (pe, pe.n))
                    ac.inc(A.activation(sq2[:], t8(qkvb, i), AF.Square,
                                        bias=self.ck(0.0), scale=self.ck(1.0)))
                    self.bank_wait(T, "g2")
                    w(T, (ac, ac.n))
                    mm = T.matmul(g2[0:2, :], self.c_ind2[:], sq2[:],
                                  start=True, stop=True)
                    pe.inc(mm)
                    w(V, (pe, pe.n), (ac, ac.n))
                    ve.inc(V.tensor_scalar_max(rr2[:], g2[0:2, :], 1e-24))
                    self.bank_used("g2", (ve, ve.n))
                    w(A, (ve, ve.n))
                    ac.inc(A.activation(rr2[:], rr2[:], AF.Ln,
                                        bias=self.ck(0.0, p=2),
                                        scale=self.ck(1.0, p=2)))
                    w(A, (ac, ac.n))
                    ac.inc(A.activation(rr2[:], rr2[:], AF.Exp,
                                        bias=self.ck(0.0, p=2),
                                        scale=self.ck(-0.5, p=2)))
                    w(V, (ac, ac.n), (pe, pe.n))
                    ve.inc(V.tensor_copy(rr2b[:], rr2[:]))
                    self.bank_wait(T, "g3")
                    w(T, (ve, ve.n))
                    mm = T.matmul(g3[:], self.c_rep2[:], rr2b[:],
                                  start=True, stop=True)
                    pe.inc(mm)
                    w(V, (pe, pe.n), (ac, ac.n))
                    ve.inc(V.tensor_mul(qn[:], t8(qkvb, i), g3[:]))
                    self.bank_used("g3", (ve, ve.n))
                    sc_col = self.c_qsc if i < 4 else self.c_ksc
                    w(V, (ve, ve.n))
                    ve.inc(V.tensor_scalar(qn[:], qn[:],
                                           sc_col[:, lay:lay + 1], None, ALU.mult))
                    self.bank_wait(T, "g2")
                    w(T, (ve, ve.n))
                    mm = T.matmul(g2[:], self.c_rotp[:], qn[:],
                                  start=True, stop=True)
                    pe.inc(mm)
                    tb = tbq if i < 4 else tbk
                    w(V, ldrdy, (ve, ve.n))
                    ve.inc(V.tensor_mul(t1b[:], qn[:], tb[:, 0:512]))
                    w(V, (pe, pe.n), (ve, ve.n))
                    ve.inc(V.tensor_mul(t2b[:], g2[:], tb[:, 512:1024]))
                    self.bank_used("g2", (ve, ve.n))
                    w(V, (ve, ve.n), qf_free[cur] if i == 0 else (ve, 0))
                    ve.inc(V.tensor_add(t8(qf[cur], i), t1b[:], t2b[:]))
                qf_rdy = (ve, ve.n)
                tb_free = (ve, ve.n)
                self.bank_wait(T, "g4")
                w(T, qkv_rdy, vT_free[cur])
                for vt in range(4):
                    for j in range(4):
                        mm = T.transpose(g4[:, j * 128:(j + 1) * 128],
                                         qkvb[:, (8 + vt) * 512 + j * 128:
                                              (8 + vt) * 512 + (j + 1) * 128],
                                         self.c_id[:])
                    pe.inc(mm)
                    w(V, (pe, pe.n))
                    ve.inc(V.tensor_copy(vTc[cur][:, vt * 512:(vt + 1) * 512],
                                         g4[:, 0:512]))
                    self.bank_used("g4", (ve, ve.n))
                    self.bank_wait(T, "g4")
                vT_rdy = (ve, ve.n)
                qkvb_free = (ve, ve.n)

                if bi == 0:
                    xin_free[cur] = (pe, pe.n)
                    qf_free[cur] = (pe, pe.n)
                    vT_free[cur] = (ve, ve.n)
                    continue

                it = 0
                for hp in range(4):
                    for hh in range(2):
                        h = 2 * hp + hh
                        avbank, avbn = (g7, "g7") if hh == 0 else (g0, "g0")
                        self.bank_wait(T, avbn)
                        for qt in range(4):
                            pa_ = it % 2
                            it += 1
                            sA_, sB_ = (sAp[pa_], sBp[pa_])
                            eA_, eB_ = (eAp[pa_], eBp[pa_])
                            aT_ = aTp[pa_]
                            if pa_ == 0:
                                bA, bB, nA, nB = g5, g6, "g5", "g6"
                            else:
                                bA, bB, nA, nB = g2, g3, "g2", "g3"

                            rowsl = slice(hh * 64, hh * 64 + 64)
                            lhsT = qf[cur][rowsl, (h // 2) * 512 + qt * 128:
                                           (h // 2) * 512 + (qt + 1) * 128]
                            self.bank_wait(T, nA)
                            self.bank_wait(T, nB)
                            w(T, qf_rdy)
                            for jt in range(5):
                                ja = qt + jt
                                slot = prv if ja < 4 else cur
                                rhs = qf[slot][rowsl,
                                               (4 + h // 2) * 512 + (ja % 4) * 128:
                                               (4 + h // 2) * 512 + (ja % 4 + 1) * 128]
                                if jt < 4:
                                    mm = T.matmul(bA[:, jt * 128:(jt + 1) * 128],
                                                  lhsT, rhs, start=True, stop=True)
                                else:
                                    mm = T.matmul(bB[:, 0:128], lhsT, rhs,
                                                  start=True, stop=True)
                            pe.inc(mm)
                            w(V, (pe, pe.n))
                            ve.inc(V.tensor_add(bA[:, 0:128], bA[:, 0:128],
                                                self.c_triu[:]))
                            ve.inc(V.tensor_add(bB[:, 0:128], bB[:, 0:128],
                                                self.c_tril[:]))
                            if bi == w0qb:
                                cw = (4 - qt) * 128
                                w(V, (ve, ve.n))
                                ve.inc(V.tensor_add(bA[:, 0:cw], bA[:, 0:cw],
                                                    self.c_w0[:, 0:cw]))
                            w(V, (ve, ve.n))
                            ve.inc(V.reduce_max(rA[:], bA[:], AX.X, negate=True))
                            ve.inc(V.reduce_max(rB[:], bB[:, 0:128], AX.X,
                                                negate=True))
                            w(V, (ve, ve.n))
                            ve.inc(V.tensor_tensor(r1s[:], rA[:], rB[:], ALU.min))
                            w(V, (ve, ve.n))
                            ve.inc(V.tensor_scalar_mul(r1s[:], r1s[:], 8.0))
                            w(A, (ve, ve.n), eA_free[pa_], eB_free[pa_])
                            ac.inc(A.activation(eA_[:], bA[:], AF.Exp,
                                                bias=r1s[:, 0:1],
                                                scale=self.ck(8.0),
                                                accum_out=sA_[:, 0:1]))
                            ac.inc(A.activation(eB_[:], bB[:, 0:128], AF.Exp,
                                                bias=r1s[:, 0:1],
                                                scale=self.ck(8.0),
                                                accum_out=sB_[:, 0:1]))
                            self.bank_used(nA, (ac, ac.n))
                            self.bank_used(nB, (ac, ac.n))
                            w(V, (ac, ac.n))
                            ve.inc(V.tensor_add(sA_[:], sA_[:], sB_[:]))
                            w(V, (ve, ve.n))
                            ve.inc(V.reciprocal(sB_[:], sA_[:]))
                            w(V, (ve, ve.n))
                            ve.inc(V.tensor_scalar(eA_[:], eA_[:], sB_[:, 0:1],
                                                   None, ALU.mult))
                            ve.inc(V.tensor_scalar(eB_[:], eB_[:], sB_[:, 0:1],
                                                   None, ALU.mult))
                            e_rdy = (ve, ve.n)
                            self.bank_wait(T, "g4")
                            w(T, e_rdy, aT_free[pa_])
                            for jt in range(4):
                                mm = T.transpose(g4[:, jt * 128:(jt + 1) * 128],
                                                 eA_[:, jt * 128:(jt + 1) * 128],
                                                 self.c_id[:])
                            mm = T.transpose(g4[:, 512:640], eB_[:], self.c_id[:])
                            pe.inc(mm)
                            eA_free[pa_] = (pe, pe.n)
                            eB_free[pa_] = (pe, pe.n)
                            w(V, (pe, pe.n))
                            ve.inc(V.tensor_copy(aT_[:, 0:640], g4[:, 0:640]))
                            self.bank_used("g4", (ve, ve.n))
                            w(T, (ve, ve.n), vT_rdy)
                            for jt in range(5):
                                ja = qt + jt
                                slot = prv if ja < 4 else cur
                                lhsT = vTc[slot][:, (h // 2) * 512 +
                                                 (ja % 4) * 128 + hh * 64:
                                                 (h // 2) * 512 +
                                                 (ja % 4) * 128 + hh * 64 + 64]
                                mm = T.matmul(avbank[0:64, qt * 128:(qt + 1) * 128],
                                              lhsT, aT_[:, jt * 128:(jt + 1) * 128],
                                              start=(jt == 0), stop=(jt == 4))
                            pe.inc(mm)
                            aT_free[pa_] = (pe, pe.n)
                    w(V, (pe, pe.n), osb_free)
                    ve.inc(V.tensor_copy(osb[0:64, hp * 512:(hp + 1) * 512],
                                         g7[0:64, :]))
                    ve.inc(V.tensor_copy(osb[64:128, hp * 512:(hp + 1) * 512],
                                         g0[0:64, :]))
                    self.bank_used("g7", (ve, ve.n))
                    self.bank_used("g0", (ve, ve.n))
                o_rdy = (ve, ve.n)
                qf_free[prv] = (pe, pe.n)
                vT_free[prv] = (pe, pe.n)
                qf_free[cur] = (pe, pe.n)
                vT_free[cur] = (pe, pe.n)
                w(T, o_rdy)
                for mi in range(8):
                    bk = banks[mi % 2]
                    self.bank_wait(T, bn[mi % 2])
                    for kk in range(4):
                        mm = T.matmul(bk[:], wob[:, kk * 1024 + mi * 128:
                                                  kk * 1024 + (mi + 1) * 128],
                                      t8(osb, kk), start=(kk == 0), stop=(kk == 3))
                    pe.inc(mm)
                    w(V, (pe, pe.n), xatt_free)
                    ve.inc(V.tensor_add(t8(xatt, mi), bk[:], t8(xin[cur], mi)))
                    self.bank_used(bn[mi % 2], (ve, ve.n))
                osb_free = (pe, pe.n)
                xin_free[cur] = (ve, ve.n)
                xin_free[prv] = (pe, pe.n)
                w(A, (ve, ve.n))
                self.d_xa.inc(A.dma_start(
                    out=self.xatt_d[:, (bi - 1) * 512:bi * 512]
                        .rearrange("(a p) n -> p a n", p=128),
                    in_=xatt.rearrange("p (a n) -> p a n", a=8)), 16)
                xatt_free = (self.d_xa, self.d_xa.n)

    def ffn_passB(self, lay):
        nc = self.nc
        T, V, A, G, SP = self.T, self.V, self.A, self.G, self.SP
        w = self.w
        pe, ve, ac = self.pe, self.ve, self.ac
        g0, g1, g2, g3 = self.g0, self.g1, self.g2, self.g3
        nfb = 11 if lay == 0 else 9
        self.enter_phase()
        est = contextlib.ExitStack()
        sbl = lambda n, sh, dt: est.enter_context(
            nc.sbuf_tensor(f"{n}_{lay}", sh, dt))
        with est:
            w1ab = sbl("b_w1a", [128, 8 * HID2], BF16)
            w1gb = sbl("b_w1g", [128, 8 * HID2], BF16)
            w2pb = sbl("b_w2p", [128, 22 * 1024], BF16)
            xat = sbl("b_xat", [128, 8 * 512], BF16)
            xshb = sbl("b_xsh", [128, 4 * 512], BF16)
            sqf = sbl("b_sqf", [128, 8 * 512], BF16)
            rmu2 = sbl("b_rmu", [1, 512], F32)
            rva2 = sbl("b_rva", [1, 512], F32)
            rtm2 = sbl("b_rtm", [1, 512], F32)
            rmub2 = sbl("b_rmub", [1, 512], BF16)
            rvab2 = sbl("b_rvab", [1, 512], BF16)
            xh = sbl("b_xh", [128, 8 * 512], BF16)
            ab = sbl("b_ab", [128, 22 * 512], BF16)
            ggt = sbl("b_ggt", [128, 512], BF16)
            x2b = sbl("b_x2", [128, 8 * 512], BF16)
            t8 = lambda b, i: b[:, i * 512:(i + 1) * 512]

            w(G, (self.d_xa, self.d_xa.n))
            self.d_wt.inc(G.dma_start(
                out=w1ab.rearrange("p (a n) -> p a n", a=8),
                in_=self.w1av[lay].rearrange("(a p) n -> p a n", p=128)), 16)
            self.d_wt.inc(G.dma_start(
                out=w1gb.rearrange("p (a n) -> p a n", a=8),
                in_=self.w1gv[lay].rearrange("(a p) n -> p a n", p=128)), 16)
            self.d_wt.inc(G.dma_start(
                out=w2pb.rearrange("p (a n) -> p a n", a=22),
                in_=self.w2pv[lay].rearrange("(a p) n -> p a n", p=128)), 16)
            wrdy = (self.d_wt, self.d_wt.n)

            ld_free = (pe, 0)
            ab_free = (pe, 0)
            x2_free = (self.d_x1, 0)
            xh_free = (pe, 0)
            for fb in range(nfb):
                dffs = self.d_ffs if fb % 2 == 0 else self.d_ffs2
                c0 = fb * 512
                csh = max(c0 - 1, 0)
                w(SP, ld_free, self.xatt_dep)
                dffs.inc(SP.dma_start(
                    out=xat.rearrange("p (a n) -> p a n", a=8),
                    in_=self.xatt_d[:, c0:c0 + 512]
                        .rearrange("(a p) n -> p a n", p=128)), 16)
                dffs.inc(SP.dma_start(
                    out=xshb.rearrange("p (a n) -> p a n", a=4),
                    in_=self.xatt_d[512:1024, csh:csh + 512]
                        .rearrange("(a p) n -> p a n", p=128)), 16)
                ldrdy = (dffs, dffs.n)
                hhs = lambda i: t8(xat, i) if i < 4 else t8(xshb, i - 4)
                w(A, ldrdy, (pe, pe.n))
                for i in range(8):
                    ac.inc(A.activation(t8(sqf, i), hhs(i), AF.Square,
                                        bias=self.ck(0.0), scale=self.ck(1.0)))
                self.bank_wait(T, "g2")
                self.bank_wait(T, "g3")
                w(T, ldrdy, (ac, ac.n))
                for i in range(8):
                    T.matmul(g2[0:1, :], self.c_onec[:], hhs(i),
                             start=(i == 0), stop=(i == 7))
                for i in range(8):
                    mm = T.matmul(g3[0:1, :], self.c_onec[:], t8(sqf, i),
                                  start=(i == 0), stop=(i == 7))
                pe.inc(mm)
                w(V, (pe, pe.n), (ac, ac.n))
                ve.inc(V.tensor_scalar_mul(rmu2[:], g2[0:1, :], 1.0 / 1024.0))
                ve.inc(V.tensor_scalar_mul(rva2[:], g3[0:1, :], 1.0 / 1024.0))
                w(V, (ve, ve.n))
                ve.inc(V.tensor_mul(rtm2[:], rmu2[:], rmu2[:]))
                w(V, (ve, ve.n))
                ve.inc(V.tensor_sub(rva2[:], rva2[:], rtm2[:]))
                w(A, (ve, ve.n))
                ac.inc(A.activation(rva2[:], rva2[:], AF.Ln,
                                    bias=self.ck(1e-5, True),
                                    scale=self.ck(1.0, True)))
                w(A, (ac, ac.n))
                ac.inc(A.activation(rva2[:], rva2[:], AF.Exp,
                                    bias=self.ck(0.0, True),
                                    scale=self.ck(-0.5, True)))
                w(V, (ac, ac.n), (pe, pe.n))
                ve.inc(V.tensor_copy(rmub2[:], rmu2[:]))
                ve.inc(V.tensor_copy(rvab2[:], rva2[:]))
                w(T, (ve, ve.n))
                T.matmul(g2[:], self.c_ones[:], rmub2[:], start=True, stop=True)
                mm = T.matmul(g3[:], self.c_ones[:], rvab2[:],
                              start=True, stop=True)
                pe.inc(mm)
                w(V, (pe, pe.n), xh_free)
                for i in range(8):
                    ve.inc(V.tensor_sub(t8(xh, i), hhs(i), g2[:]))
                w(V, (ve, ve.n))
                for i in range(8):
                    ve.inc(V.tensor_mul(t8(xh, i), t8(xh, i), g3[:]))
                w(V, (ve, ve.n))
                for i in range(8):
                    ve.inc(V.tensor_scalar(
                        t8(xh, i), t8(xh, i), self.c_lng[:, lay * 8 + i:
                                                         lay * 8 + i + 1],
                        self.c_lnb[:, lay * 8 + i:lay * 8 + i + 1],
                        ALU.mult, ALU.add))
                self.bank_used("g2", (ve, ve.n))
                self.bank_used("g3", (ve, ve.n))
                xh_rdy = (ve, ve.n)
                banks = [g0, g1, self.g5, self.g6]
                bn = ["g0", "g1", "g5", "g6"]
                w(T, xh_rdy, wrdy, ab_free)
                for hc in range(22):
                    bk = banks[hc % 4]
                    self.bank_wait(T, bn[hc % 4])
                    for kk in range(8):
                        mm = T.matmul(bk[:], w1ab[:, kk * HID2 + hc * 128:
                                                  kk * HID2 + (hc + 1) * 128],
                                      t8(xh, kk), start=(kk == 0), stop=(kk == 7))
                    pe.inc(mm)
                    w(V, (pe, pe.n))
                    ve.inc(V.tensor_scalar(
                        ab[:, hc * 512:(hc + 1) * 512], bk[:],
                        self.c_b1a[:, lay * 22 + hc:lay * 22 + hc + 1],
                        None, ALU.add))
                    self.bank_used(bn[hc % 4], (ve, ve.n))
                for hc in range(22):
                    bk = banks[hc % 4]
                    self.bank_wait(T, bn[hc % 4])
                    for kk in range(8):
                        mm = T.matmul(bk[:], w1gb[:, kk * HID2 + hc * 128:
                                                  kk * HID2 + (hc + 1) * 128],
                                      t8(xh, kk), start=(kk == 0), stop=(kk == 7))
                    pe.inc(mm)
                    w(A, (pe, pe.n), (ve, ve.n))
                    ac.inc(A.activation(
                        ggt[:], bk[:], AF.Gelu,
                        bias=self.c_b1g[:, lay * 22 + hc:lay * 22 + hc + 1],
                        scale=self.ck(1.0)))
                    self.bank_used(bn[hc % 4], (ac, ac.n))
                    w(V, (ac, ac.n))
                    ve.inc(V.tensor_mul(ab[:, hc * 512:(hc + 1) * 512],
                                        ab[:, hc * 512:(hc + 1) * 512], ggt[:]))
                hf_rdy = (ve, ve.n)
                w(T, hf_rdy, x2_free)
                for mi in range(8):
                    bk = banks[mi % 4]
                    self.bank_wait(T, bn[mi % 4])
                    for kk in range(22):
                        mm = T.matmul(bk[:], w2pb[:, kk * 1024 + mi * 128:
                                                  kk * 1024 + (mi + 1) * 128],
                                      ab[:, kk * 512:(kk + 1) * 512],
                                      start=(kk == 0), stop=(kk == 21))
                    pe.inc(mm)
                    w(V, (pe, pe.n))
                    ve.inc(V.scalar_tensor_tensor(
                        t8(x2b, mi), bk[:],
                        self.c_fb2[:, lay * 8 + mi:lay * 8 + mi + 1],
                        t8(xat, mi), ALU.add, ALU.add))
                    self.bank_used(bn[mi % 4], (ve, ve.n))
                ab_free = (pe, pe.n)
                xh_free = (pe, pe.n)
                ld_free = (ve, ve.n)
                if lay == 0:
                    if fb < 3:
                        w(V, (ve, ve.n))
                        for i in range(8):
                            ve.inc(V.tensor_scalar(
                                t8(x2b, i), t8(x2b, i),
                                self.c_x1z[:, 0:1], None, ALU.mult))
                    w(A, (ve, ve.n))
                    self.d_x1.inc(A.dma_start(
                        out=self.x1_d[:, c0:c0 + 512]
                            .rearrange("(a p) n -> p a n", p=128),
                        in_=x2b.rearrange("p (a n) -> p a n", a=8)), 16)
                    x2_free = (self.d_x1, self.d_x1.n)
                else:
                    w(A, (ve, ve.n))
                    for i in range(8):
                        ac.inc(A.activation(t8(x2b, i), t8(x2b, i), AF.Silu,
                                            bias=self.ck(0.0),
                                            scale=self.ck(1.0)))
                    if fb >= 1:
                        w(A, (ac, ac.n))
                        self.d_g.inc(A.dma_start(
                            out=self.g_dd[:, (fb - 1) * 512:fb * 512]
                                .rearrange("(a p) n -> p a n", p=128),
                            in_=x2b.rearrange("p (a n) -> p a n", a=8)), 16)
                        x2_free = (self.d_g, self.d_g.n)
                    else:
                        x2_free = (ac, ac.n)

    def ex1(self):
        G, w = self.G, self.w
        w(G, (self.d_u, self.d_u.n), (self.d_dt, self.d_dt.n),
          (self.d_bc, self.d_bc.n), (self.dS, self.dS.n))
        with G.register("r0") as r0:
            G.reg_load(r0, self.c_par[0:1, 0:1])
            G.reg_mul(r0, r0, -1024 * TOK)
            G.reg_add(r0, r0, 1024 * TOK)
            srcap = bass.AP(self.u_d.tensor, r0, [[TOK, 1024], [1, TOK]])
            self.dZg.inc(G.dma_start(out=self.uoth_c, in_=srcap), 16)
        w(G, (self.dZg, self.dZg.n))
        for cin, cout in ((self.uoth_c, self.uoth_g),
                          (self.dt_own, self.dt_gt),
                          (self.bc_own, self.bc_gt)):
            self.cC.inc(G.collective_compute(
                "AllGather", ALU.bypass,
                replica_groups=[[0, 1], [2, 3], [4, 5], [6, 7]],
                ins=[cin.opt()], outs=[cout.opt()]))

    def assemble1(self):
        SP, w = self.SP, self.w
        w(SP, (self.cC, self.cC.n))
        with SP.register("rp") as rp, SP.register("rt") as rt, \
                SP.register("rs") as rs:
            SP.reg_load(rp, self.c_par[0:1, 0:1])
            SP.reg_mul(rt, rp, 1024 * TOK)
            SP.reg_mul(rs, rp, TOK)
            self.dZs.inc(SP.dma_start(
                out=bass.AP(self.u_sc.tensor, rs, [[8192, 1024], [1, TOK]]),
                in_=bass.AP(self.u_d.tensor, rt, [[TOK, 1024], [1, TOK]])), 16)
            SP.reg_mul(rt, rp, -1024 * TOK)
            SP.reg_add(rt, rt, 1024 * TOK)
            SP.reg_mul(rs, rp, -TOK)
            SP.reg_add(rs, rs, TOK)
            self.dZs.inc(SP.dma_start(
                out=bass.AP(self.u_sc.tensor, rs, [[8192, 1024], [1, TOK]]),
                in_=bass.AP(self.uoth_g.tensor, rt, [[TOK, 1024], [1, TOK]])), 16)
            SP.reg_mul(rs, rp, TOK)
            self.dZs.inc(SP.dma_start(
                out=bass.AP(self.dt_sc.tensor, rs, [[8192, DTR], [1, TOK]]),
                in_=self.dt_own), 16)
            SP.reg_mul(rt, rp, -DTR * TOK)
            SP.reg_add(rt, rt, DTR * TOK)
            SP.reg_mul(rs, rp, -TOK)
            SP.reg_add(rs, rs, TOK)
            self.dZs.inc(SP.dma_start(
                out=bass.AP(self.dt_sc.tensor, rs, [[8192, DTR], [1, TOK]]),
                in_=bass.AP(self.dt_gt.tensor, rt, [[TOK, DTR], [1, TOK]])), 16)
            SP.reg_mul(rs, rp, TOK)
            self.dZs.inc(SP.dma_start(
                out=bass.AP(self.bc_sc.tensor, rs, [[8192, 32], [1, TOK]]),
                in_=self.bc_own), 16)
            SP.reg_mul(rt, rp, -32 * TOK)
            SP.reg_add(rt, rt, 32 * TOK)
            SP.reg_mul(rs, rp, -TOK)
            SP.reg_add(rs, rs, TOK)
            self.dZs.inc(SP.dma_start(
                out=bass.AP(self.bc_sc.tensor, rs, [[8192, 32], [1, TOK]]),
                in_=bass.AP(self.bc_gt.tensor, rt, [[TOK, 32], [1, TOK]])), 16)
        self.scan_in_ready = (self.dZs, self.dZs.n)

    def scan_phase(self):
        nc = self.nc
        T, V, A, G, SP = self.T, self.V, self.A, self.G, self.SP
        w = self.w
        pe, ve, ac, po = self.pe, self.ve, self.ac, self.po
        g0, g1 = self.g0, self.g1
        self.enter_phase()
        est = contextlib.ExitStack()
        sbl = lambda n, sh, dt: est.enter_context(nc.sbuf_tensor(n, sh, dt))
        with est:
            wdt = sbl("s_wdt", [DTR, 1024], BF16)
            brep = sbl("s_brep", [128, 16 * 1024], BF16)
            crep = sbl("s_crep", [128, 16 * 1024], BF16)
            sdt = sbl("s_sdt", [DTR, 1024], F32)
            sdtb = sbl("s_sdtb", [DTR, 1024], BF16)
            sdel = sbl("s_sdel", [128, 1024], F32)
            t_a = sbl("s_ta", [128, 1024], F32)
            t_b = sbl("s_tb", [128, 1024], F32)
            sdu = sbl("s_sdu", [128, 1024], BF16)
            sda = [sbl(f"s_sda{i}", [128, 1024], F32) for i in range(2)]
            sdb = [sbl(f"s_sdb{i}", [128, 1024], BF16) for i in range(2)]
            sH = [sbl(f"s_sH{i}", [128, 1024], BF16) for i in range(2)]
            ytm = [sbl(f"s_ytm{i}", [128, 1024], BF16) for i in range(2)]
            ypar = sbl("s_ypar", [128, 12 * 1024], BF16)
            yg = sbl("s_yg", [128, 1024], BF16)
            ug = sbl("s_ug", [128, 1024], BF16)
            Hc = sbl("s_Hc", [128, 128], F32)

            rdy = self.scan_in_ready if hasattr(self, "scan_in_ready") else (self.dZs, 0)
            w(V, rdy)
            ve.inc(V.memset(Hc[:], 0.0))
            w(G, rdy)
            self.d_wt.inc(G.dma_start(out=wdt[:], in_=self.dtproj_t), 16)
            wdt_ready = (self.d_wt, self.d_wt.n)

            bc_free = [(po, 0), (ve, 0)]
            sdt_free = (ve, 0)
            ug_free = (ve, 0)
            sdel_free = [(ac, 0), (po, 0), (ve, 0)]
            sda_free = [(ve, 0), (ve, 0)]
            sdb_free = [(ve, 0), (ve, 0)]
            sH_free = [(ve, 0), (ve, 0)]
            ytm_free = [(po, 0), (po, 0)]
            ypar_free = (ve, 0)
            yg_free = (self.d_y, 0)

            for s in range(8):
                cols = slice(s * 1024, (s + 1) * 1024)
                w(SP, rdy, sdt_free)
                self.d_sl2.inc(SP.dma_start(out=sdt[:], in_=self.dt_sc[:, cols]), 16)
                w(SP, *bc_free)
                for n in range(16):
                    self.d_sl.inc(SP.dma_start(
                        out=brep[:, n * 1024:(n + 1) * 1024],
                        in_=self.bc_sc[n:n + 1, cols]
                            .partition_broadcast(128).squeeze(1)), 16)
                    self.d_sl.inc(SP.dma_start(
                        out=crep[:, n * 1024:(n + 1) * 1024],
                        in_=self.bc_sc[16 + n:17 + n, cols]
                            .partition_broadcast(128).squeeze(1)), 16)
                bc_ready = (self.d_sl, self.d_sl.n)
                w(V, (self.d_sl2, self.d_sl2.n))
                ve.inc(V.tensor_copy(sdtb[:], sdt[:]))
                sdt_free = (ve, ve.n)
                sdtb_ready = (ve, ve.n)

                for g in range(8):
                    self.bank_wait(T, "g0")
                    self.bank_wait(T, "g1")
                    w(T, sdtb_ready, wdt_ready)
                    m1 = T.matmul(g0[:], wdt[:, g * 128:(g + 1) * 128],
                                  sdtb[:, 0:512], start=True, stop=True)
                    m2 = T.matmul(g1[:], wdt[:, g * 128:(g + 1) * 128],
                                  sdtb[:, 512:1024], start=True, stop=True)
                    pe.inc(m2)
                    w(V, (pe, pe.n), *sdel_free)
                    ve.inc(V.tensor_scalar(sdel[:, 0:512], g0[:],
                                           self.c_dtb[:, g:g + 1], None, ALU.add))
                    ve.inc(V.tensor_scalar(sdel[:, 512:1024], g1[:],
                                           self.c_dtb[:, g:g + 1], None, ALU.add))
                    self.bank_used("g0", (ve, ve.n))
                    self.bank_used("g1", (ve, ve.n))
                    w(A, (ve, ve.n), (ac, ac.n))
                    ac.inc(A.activation(t_a[:], sdel[:], AF.Abs,
                                        bias=self.ck(0.0), scale=self.ck(1.0)))
                    w(A, (ac, ac.n))
                    ac.inc(A.activation(t_b[:], t_a[:], AF.Exp,
                                        bias=self.ck(0.0), scale=self.ck(-1.0)))
                    w(A, (ac, ac.n))
                    ac.inc(A.activation(t_b[:], t_b[:], AF.Ln,
                                        bias=self.ck(1.0), scale=self.ck(1.0)))
                    w(V, (ac, ac.n))
                    ve.inc(V.tensor_scalar(t_a[:], sdel[:], 0.0, None, ALU.max))
                    w(V, (ve, ve.n))
                    ve.inc(V.tensor_add(sdel[:], t_a[:], t_b[:]))
                    sdel_ready = (ve, ve.n)
                    w(SP, ug_free)
                    self.d_ug.inc(SP.dma_start(
                        out=ug[:], in_=self.u_sc[g * 128:(g + 1) * 128, cols]), 16)
                    w(V, (self.d_ug, self.d_ug.n), sdel_ready, (po, po.n))
                    ve.inc(V.tensor_mul(sdu[:], sdel[:], ug[:]))
                    ug_free = (ve, ve.n)
                    sdu_ready = (ve, ve.n)

                    for n in range(16):
                        pa = n % 2
                        col = g * 16 + n
                        w(A, sdel_ready, sda_free[pa])
                        ac.inc(A.activation(sda[pa][:], sdel[:], AF.Exp,
                                            bias=self.ck(0.0),
                                            scale=self.c_A[:, col:col + 1]))
                        w(G, sdu_ready, bc_ready, sdb_free[pa])
                        po.inc(G.tensor_mul(sdb[pa][:], sdu[:],
                                            brep[:, n * 1024:(n + 1) * 1024]))
                        w(V, (ac, ac.n), (po, po.n), sH_free[pa])
                        ve.inc(V.tensor_tensor_scan(
                            sH[pa][:], sda[pa][:], sdb[pa][:],
                            Hc[:, col:col + 1], ALU.mult, ALU.add))
                        sda_free[pa] = (ve, ve.n)
                        sdb_free[pa] = (ve, ve.n)
                        w(V, (ve, ve.n))
                        ve.inc(V.tensor_copy(Hc[:, col:col + 1],
                                             sH[pa][:, 1023:1024]))
                        w(V, ytm_free[pa])
                        ve.inc(V.tensor_mul(ytm[pa][:], sH[pa][:],
                                            crep[:, n * 1024:(n + 1) * 1024]))
                        sH_free[pa] = (ve, ve.n)
                        if pa == 1:
                            w(G, (ve, ve.n), ypar_free)
                            po.inc(G.tensor_add(
                                ypar[:, (n // 2) * 1024:(n // 2 + 1) * 1024],
                                ytm[0][:], ytm[1][:]))
                            ytm_free = [(po, po.n), (po, po.n)]
                    w(V, (po, po.n), yg_free)
                    for i2 in range(4):
                        ve.inc(V.tensor_add(
                            ypar[:, (8 + i2) * 1024:(9 + i2) * 1024],
                            ypar[:, 2 * i2 * 1024:(2 * i2 + 1) * 1024],
                            ypar[:, (2 * i2 + 1) * 1024:(2 * i2 + 2) * 1024]))
                    w(V, (ve, ve.n))
                    for i2 in range(2):
                        ve.inc(V.tensor_add(
                            ypar[:, i2 * 1024:(i2 + 1) * 1024],
                            ypar[:, (8 + 2 * i2) * 1024:(9 + 2 * i2) * 1024],
                            ypar[:, (9 + 2 * i2) * 1024:(10 + 2 * i2) * 1024]))
                    w(V, (ve, ve.n))
                    ve.inc(V.tensor_add(yg[:], ypar[:, 0:1024],
                                        ypar[:, 1024:2048]))
                    ypar_free = (ve, ve.n)
                    sdel_free = [(ac, ac.n), (ve, ve.n), (po, po.n)]
                    w(A, (ve, ve.n))
                    self.d_y.inc(A.dma_start(
                        out=self.y_dd[g * 128:(g + 1) * 128, cols], in_=yg[:]), 16)
                    yg_free = (self.d_y, self.d_y.n)
                bc_free = [(po, po.n), (ve, ve.n)]
        self.scan_done = (self.d_y, self.d_y.n)

    def ex2(self):
        G, w = self.G, self.w
        w(G, self.scan_done if hasattr(self, "scan_done") else (self.d_y, 0))
        with G.register("r1") as r1:
            G.reg_load(r1, self.c_par[0:1, 0:1])
            G.reg_mul(r1, r1, -TOK)
            G.reg_add(r1, r1, TOK)
            self.dZg.inc(G.dma_start(
                out=self.y_cc,
                in_=bass.AP(self.y_dd.tensor, r1, [[8192, 1024], [1, TOK]])), 16)
        w(G, (self.dZg, self.dZg.n))
        self.cC.inc(G.collective_compute(
            "AllGather", ALU.bypass,
            replica_groups=[[0, 1], [2, 3], [4, 5], [6, 7]],
            ins=[self.y_cc.opt()], outs=[self.y_gg.opt()]))

    def assemble2(self):
        SP, w = self.SP, self.w
        w(SP, (self.cC, self.cC.n))
        with SP.register("rp2") as rp, SP.register("rt2") as rt, \
                SP.register("rs2") as rs:
            SP.reg_load(rp, self.c_par[0:1, 0:1])
            SP.reg_mul(rt, rp, TOK)
            SP.reg_mul(rs, rp, 1024 * TOK)
            self.dZs.inc(SP.dma_start(
                out=bass.AP(self.y_alld.tensor, rs, [[TOK, 1024], [1, TOK]]),
                in_=bass.AP(self.y_dd.tensor, rt, [[8192, 1024], [1, TOK]])), 16)
            SP.reg_mul(rt, rp, -1024 * TOK)
            SP.reg_add(rt, rt, 1024 * TOK)
            self.dZs.inc(SP.dma_start(
                out=bass.AP(self.y_alld.tensor, rt, [[TOK, 1024], [1, TOK]]),
                in_=bass.AP(self.y_gg.tensor, rt, [[TOK, 1024], [1, TOK]])), 16)
        self.y_ready = (self.dZs, self.dZs.n)

    def tail_phase(self):
        nc = self.nc
        T, V, A, G, SP = self.T, self.V, self.A, self.G, self.SP
        w = self.w
        pe, ve, ac = self.pe, self.ve, self.ac
        g0, g1, g2, g3, g5 = self.g0, self.g1, self.g2, self.g3, self.g5
        self.enter_phase()
        est = contextlib.ExitStack()
        sbl = lambda n, sh, dt: est.enter_context(nc.sbuf_tensor(n, sh, dt))
        with est:
            wop = sbl("t_wop", [128, 16 * 1024], BF16)
            wgt = sbl("t_wgt", [128, 8 * 8], BF16)
            ytb = sbl("t_yt", [128, 16 * 512], BF16)
            utb = sbl("t_ut", [128, 16 * 512], BF16)
            rsb = sbl("t_rs", [128, 16 * 512], BF16)
            xrb = sbl("t_xr", [128, 8 * 512], BF16)
            srs = sbl("t_srs", [128, 16 * 512], BF16)
            ytg = sbl("t_ytg", [128, 16 * 512], BF16)
            mot = sbl("t_mot", [128, 8 * 512], F32)
            sqb = sbl("t_sq", [128, 8 * 512], BF16)
            rr = sbl("t_rr", [1, 512], F32)
            rrb = sbl("t_rrb", [1, 512], BF16)
            xmt = sbl("t_xmt", [128, 8 * 512], BF16)
            lgt = sbl("t_lgt", [128, 128], F32)
            ltm = sbl("t_ltm", [128, 32], F32)
            spt = sbl("t_spt", [128, 128], F32)
            spf = sbl("t_spf", [8, 512], F32)
            r1 = sbl("t_r1", [128, 1], F32)
            r2 = sbl("t_r2", [128, 1], F32)
            r3 = sbl("t_r3", [128, 1], F32)
            egt = sbl("t_egt", [128, 8], F32)
            selt = sbl("t_selt", [128, 8], F32)
            mskt = sbl("t_mskt", [128, 8], F32)
            wtt = sbl("t_wtt", [128, 8], F32)

            yrdy = self.y_ready if hasattr(self, "y_ready") else (self.dZs, 0)
            w(G, yrdy, (self.dS, self.dS.n))
            self.d_wt.inc(G.dma_start(
                out=wop.rearrange("p (a n) -> p a n", a=16),
                in_=self.outp_t.rearrange("(a p) n -> p a n", p=128)), 16)
            self.d_wt.inc(G.dma_start(
                out=wgt.rearrange("p (a n) -> p a n", a=8),
                in_=self.gate_t.rearrange("(a p) n -> p a n", p=128)), 16)
            wrdy = (self.d_wt, self.d_wt.n)
            w(V, (self.dS, self.dS.n))
            ve.inc(V.memset(lgt[:], 0.0))
            ve.inc(V.memset(spt[:], 0.0))

            lds_free = [(pe, 0)]
            mot_free = (ac, 0)
            for c in range(8):
                cols = slice(c * 512, (c + 1) * 512)
                w(SP, yrdy, *lds_free)
                self.d_tl.inc(SP.dma_start(
                    out=ytb.rearrange("p (a n) -> p a n", a=16),
                    in_=self.y_alld[:, cols].rearrange("(a p) n -> p a n", p=128)), 16)
                self.d_tl.inc(SP.dma_start(
                    out=utb.rearrange("p (a n) -> p a n", a=16),
                    in_=self.u_d[:, cols].rearrange("(a p) n -> p a n", p=128)), 16)
                self.d_tl.inc(SP.dma_start(
                    out=rsb.rearrange("p (a n) -> p a n", a=16),
                    in_=self.res_d[:, cols].rearrange("(a p) n -> p a n", p=128)), 16)
                self.d_tl.inc(SP.dma_start(
                    out=xrb.rearrange("p (a n) -> p a n", a=8),
                    in_=self.xr1_d[:, cols].rearrange("(a p) n -> p a n", p=128)), 16)
                ldrdy = (self.d_tl, self.d_tl.n)
                t16 = lambda b, i: b[:, i * 512:(i + 1) * 512]
                w(A, ldrdy, (ve, ve.n))
                for j in range(16):
                    ac.inc(A.activation(t16(srs, j), t16(rsb, j), AF.Silu,
                                        bias=self.ck(0.0), scale=self.ck(1.0)))
                w(V, ldrdy, (pe, pe.n))
                for j in range(16):
                    ve.inc(V.scalar_tensor_tensor(
                        t16(ytg, j), t16(utb, j), self.c_dvec[:, j:j + 1],
                        t16(ytb, j), ALU.mult, ALU.add))
                w(V, (ve, ve.n), (ac, ac.n))
                for j in range(16):
                    ve.inc(V.tensor_mul(t16(ytg, j), t16(ytg, j), t16(srs, j)))
                ytg_rdy = (ve, ve.n)
                banks = [g0, g1, self.g5, self.g6]
                bn = ["g0", "g1", "g5", "g6"]
                w(T, ytg_rdy, wrdy)
                for mi in range(8):
                    bk = banks[mi % 4]
                    self.bank_wait(T, bn[mi % 4])
                    for kk in range(16):
                        mm = T.matmul(bk[:], wop[:, kk * 1024 + mi * 128:
                                                 kk * 1024 + (mi + 1) * 128],
                                      t16(ytg, kk), start=(kk == 0), stop=(kk == 15))
                    pe.inc(mm)
                    w(V, (pe, pe.n), mot_free)
                    ve.inc(V.tensor_add(t16(mot, mi), bk[:],
                                        xrb[:, mi * 512:(mi + 1) * 512]))
                    self.bank_used(bn[mi % 4], (ve, ve.n))
                mot_rdy = (ve, ve.n)
                lds_free = [mot_rdy]
                w(A, mot_rdy, (self.d_xm, self.d_xm.n))
                for i in range(8):
                    ac.inc(A.activation(t16(sqb, i), t16(mot, i), AF.Square,
                                        bias=self.ck(0.0), scale=self.ck(1.0)))
                self.bank_wait(T, "g2")
                w(T, (ac, ac.n))
                for i in range(8):
                    mm = T.matmul(g2[0:1, :], self.c_onec[:], t16(sqb, i),
                                  start=(i == 0), stop=(i == 7))
                pe.inc(mm)
                w(V, (pe, pe.n), (ac, ac.n))
                ve.inc(V.tensor_scalar_max(rr[:], g2[0:1, :], 1e-24))
                self.bank_used("g2", (ve, ve.n))
                w(A, (ve, ve.n))
                ac.inc(A.activation(rr[:], rr[:], AF.Ln,
                                    bias=self.ck(0.0, True), scale=self.ck(1.0, True)))
                w(A, (ac, ac.n))
                ac.inc(A.activation(rr[:], rr[:], AF.Exp,
                                    bias=self.ck(LN32, True), scale=self.ck(-0.5, True)))
                w(V, (ac, ac.n), (pe, pe.n))
                ve.inc(V.tensor_copy(rrb[:], rr[:]))
                self.bank_wait(T, "g3")
                w(T, (ve, ve.n))
                mm = T.matmul(g3[:], self.c_ones[:], rrb[:], start=True, stop=True)
                pe.inc(mm)
                w(V, (pe, pe.n), (self.d_xm, self.d_xm.n))
                for i in range(8):
                    ve.inc(V.tensor_mul(t16(xmt, i), t16(mot, i), g3[:]))
                self.bank_used("g3", (ve, ve.n))
                mot_free = (ve, ve.n)
                xmt_rdy = (ve, ve.n)
                w(A, xmt_rdy)
                self.d_xm.inc(A.dma_start(
                    out=self.xm_d[:, cols].rearrange("(a p) n -> p a n", p=128),
                    in_=xmt.rearrange("p (a n) -> p a n", a=8)), 16)
                self.bank_wait(T, "g2")
                w(T, xmt_rdy, wrdy)
                for kk in range(8):
                    mm = T.matmul(g2[0:8, :], wgt[:, kk * 8:(kk + 1) * 8],
                                  t16(xmt, kk), start=(kk == 0), stop=(kk == 7))
                pe.inc(mm)
                for tt in range(4):
                    w(V, (pe, pe.n), (ve, ve.n))
                    ve.inc(V.tensor_copy(lgt[0:8, 0:128],
                                         g2[0:8, tt * 128:(tt + 1) * 128]))
                    if tt == 3:
                        self.bank_used("g2", (ve, ve.n))
                    self.bank_wait(T, "g5")
                    w(T, (ve, ve.n))
                    mm = T.transpose(g5[:, 0:128], lgt[:], self.c_id32[:])
                    pe.inc(mm)
                    w(V, (pe, pe.n))
                    ve.inc(V.tensor_copy(ltm[:, 0:8], g5[:, 0:8]))
                    self.bank_used("g5", (ve, ve.n))
                    lt = ltm[:, 0:8]
                    w(V, (ve, ve.n))
                    ve.inc(V.reduce_max(r1[:], lt, AX.X, negate=True))
                    w(V, (ve, ve.n))
                    ve.inc(V.tensor_scalar_mul(r2[:], r1[:], -1.0))
                    w(A, (ve, ve.n))
                    ac.inc(A.activation(egt[:], lt, AF.Exp,
                                        bias=r1[:, 0:1], scale=self.ck(1.0)))
                    w(V, (ve, ve.n))
                    ve.inc(V.tensor_scalar(selt[:], lt, r2[:, 0:1], None,
                                           ALU.is_ge))
                    w(V, (ve, ve.n))
                    ve.inc(V.scalar_tensor_tensor(mskt[:], selt[:], -1e30,
                                                  lt, ALU.mult, ALU.add))
                    w(V, (ve, ve.n))
                    ve.inc(V.reduce_max(r3[:], mskt[:], AX.X))
                    w(V, (ve, ve.n))
                    ve.inc(V.tensor_scalar(selt[:], lt, r3[:, 0:1], None,
                                           ALU.is_ge))
                    w(V, (ve, ve.n), (ac, ac.n))
                    ve.inc(V.tensor_mul(wtt[:], egt[:], selt[:]))
                    w(V, (ve, ve.n))
                    ve.inc(V.reduce_sum(r3[:], wtt[:], AX.X))
                    w(V, (ve, ve.n))
                    ve.inc(V.reciprocal(r2[:], r3[:]))
                    w(V, (ve, ve.n))
                    ve.inc(V.tensor_scalar(spt[:, 0:8], wtt[:], r2[:, 0:1],
                                           None, ALU.mult))
                    self.bank_wait(T, "g5")
                    w(T, (ve, ve.n))
                    mm = T.transpose(g5[:, 0:128], spt[:], self.c_id32[:])
                    pe.inc(mm)
                    w(V, (pe, pe.n), (self.d_sp, self.d_sp.n))
                    ve.inc(V.tensor_copy(spf[0:8, tt * 128:(tt + 1) * 128],
                                         g5[0:8, 0:128]))
                    self.bank_used("g5", (ve, ve.n))
                w(A, (ve, ve.n))
                self.d_sp.inc(A.dma_start(out=self.sp_d[:, cols], in_=spf[0:8, :]), 16)
        self.tail_done = ((self.d_xm, self.d_xm.n), (self.d_sp, self.d_sp.n))

    def moe_phase(self):
        nc = self.nc
        T, V, A, G, SP = self.T, self.V, self.A, self.G, self.SP
        w = self.w
        pe, ve, ac = self.pe, self.ve, self.ac
        g0, g1, g2, g3 = self.g0, self.g1, self.g2, self.g3
        self.enter_phase()
        est = contextlib.ExitStack()
        sbl = lambda n, sh, dt: est.enter_context(nc.sbuf_tensor(n, sh, dt))
        with est:
            w1b = [sbl(f"m_w1{i}", [128, 8 * 2048], BF16) for i in range(2)]
            w2b = [sbl(f"m_w2{i}", [128, 16 * 1024], BF16) for i in range(2)]
            xmb = [sbl(f"m_xm{i}", [128, 8 * 512], BF16) for i in range(2)]
            hfb = sbl("m_hf", [128, 16 * 512], BF16)
            sprep = sbl("m_sp", [128, TOK], BF16)
            ysb = [sbl(f"m_ysb{i}", [128, 512], BF16) for i in range(2)]

            tdone = self.tail_done if hasattr(self, "tail_done") else ()
            w(G, *tdone)
            wl = [self.dG1, self.dG2]
            wrdy = [(self.dG1, 0), (self.dG2, 0)]
            wfree = [(pe, 0), (pe, 0)]

            def load_w(e, h, slot):
                self.w(G, wfree[slot], *tdone)
                wl[slot].inc(G.dma_start(
                    out=w1b[slot].rearrange("p (a n) -> p a n", a=8),
                    in_=self.mw1[e, :, h * 2048:(h + 1) * 2048]
                        .rearrange("(a p) n -> p a n", p=128)), 16)
                wl[slot].inc(G.dma_start(
                    out=w2b[slot].rearrange("p (a n) -> p a n", a=16),
                    in_=self.mw2[e, h * 2048:(h + 1) * 2048, :]
                        .rearrange("(a p) n -> p a n", p=128)), 16)
                wrdy[slot] = (wl[slot], wl[slot].n)

            load_w(0, 0, 0)
            sprep_rdy = (self.d_sp2, 0)
            sprep_free = (ve, 0)
            xm_free = [(pe, 0), (pe, 0)]
            hf_free = [(pe, 0)] * 16
            ysb_free = [(self.d_ym, 0), (self.d_ym2, 0)]
            accum_guard = [(self.d_ym, 0), (self.d_ym2, 0)]

            step = 0
            for e in range(8):
                w(G, sprep_free, *tdone)
                self.d_sp2.inc(G.dma_start(
                    out=sprep[:],
                    in_=self.sp_d[e:e + 1, :].partition_broadcast(128)
                        .squeeze(1)), 16)
                sprep_rdy = (self.d_sp2, self.d_sp2.n)
                for h in range(2):
                    slot = step % 2
                    nstep = step + 1
                    if nstep < 16:
                        ne, nh = nstep // 2, nstep % 2
                        load_w(ne, nh, nstep % 2)
                    for nt in range(8):
                        xs = nt % 2
                        dmx = self.d_mx if xs == 0 else self.d_mx2
                        w(SP, xm_free[xs], *tdone)
                        dmx.inc(SP.dma_start(
                            out=xmb[xs].rearrange("p (a n) -> p a n", a=8),
                            in_=self.xm_d[:, nt * 512:(nt + 1) * 512]
                                .rearrange("(a p) n -> p a n", p=128)), 16)
                        xmrdy = (dmx, dmx.n)
                        bk = [g0, g1, self.g5, self.g6]
                        bn = ["g0", "g1", "g5", "g6"]
                        w(T, xmrdy, wrdy[slot])
                        for hc in range(16):
                            b_ = bk[hc % 4]
                            self.bank_wait(T, bn[hc % 4])
                            for dc in range(8):
                                mm = T.matmul(
                                    b_[:], w1b[slot][:, dc * 2048 + hc * 128:
                                                     dc * 2048 + (hc + 1) * 128],
                                    xmb[xs][:, dc * 512:(dc + 1) * 512],
                                    start=(dc == 0), stop=(dc == 7))
                            pe.inc(mm)
                            w(A, (pe, pe.n), hf_free[hc])
                            ac.inc(A.activation(
                                hfb[:, hc * 512:(hc + 1) * 512], b_[:], AF.Gelu,
                                bias=self.ck(0.0), scale=self.ck(1.0)))
                            self.bank_used(bn[hc % 4], (ac, ac.n))
                        hf_rdy = (ac, ac.n)
                        bk2 = [g2, g3]
                        bn2 = ["g2", "g3"]
                        w(T, hf_rdy)
                        for dc in range(8):
                            b_ = bk2[dc % 2]
                            self.bank_wait(T, bn2[dc % 2])
                            for hc in range(16):
                                mm = T.matmul(
                                    b_[:], w2b[slot][:, hc * 1024 + dc * 128:
                                                     hc * 1024 + (dc + 1) * 128],
                                    hfb[:, hc * 512:(hc + 1) * 512],
                                    start=(hc == 0), stop=(hc == 15))
                            pe.inc(mm)
                            hf_free = [(pe, pe.n)] * 16
                            ys = dc % 2
                            w(V, (pe, pe.n), sprep_rdy, ysb_free[ys])
                            ve.inc(V.tensor_mul(
                                ysb[ys][:], b_[:],
                                sprep[:, nt * 512:(nt + 1) * 512]))
                            self.bank_used(bn2[dc % 2], (ve, ve.n))
                            dym = self.d_ym if ys == 0 else self.d_ym2
                            w(G, (ve, ve.n), *accum_guard)
                            op = ALU.bypass if step == 0 else ALU.add
                            dym.inc(G.dma_start(
                                out=self.ymoe_d[dc * 128:(dc + 1) * 128,
                                                nt * 512:(nt + 1) * 512],
                                in_=ysb[ys][:], accum_op=op), 16)
                            ysb_free[ys] = (dym, dym.n)
                        xm_free[xs] = (pe, pe.n)
                    wfree[slot] = (pe, pe.n)
                    accum_guard = [(self.d_ym, self.d_ym.n),
                                   (self.d_ym2, self.d_ym2.n)]
                    step += 1
                sprep_free = (ve, ve.n)
        self.moe_done = (self.d_ym, self.d_ym.n)
        self.moe_done2 = (self.d_ym2, self.d_ym2.n)

    def combine_phase(self):
        nc = self.nc
        V, A, SP, G = self.V, self.A, self.SP, self.G
        w = self.w
        ve = self.ve
        self.enter_phase()
        est = contextlib.ExitStack()
        sbl = lambda n, sh, dt: est.enter_context(nc.sbuf_tensor(n, sh, dt))
        with est:
            ymb = sbl("c_ym", [128, 8 * 512], BF16)
            xnb = sbl("c_xn", [128, 8 * 512], BF16)
            ggb = sbl("c_gg", [128, 8 * 512], BF16)
            xxb = sbl("c_xx", [128, 8 * 512], F32)
            sgb = sbl("c_sg", [128, 8 * 512], F32)
            otb = sbl("c_ot", [128, 8 * 512], F32)
            mdone = self.moe_done if hasattr(self, "moe_done") else (self.d_ym, 0)
            mdone2 = getattr(self, "moe_done2", (self.d_ym2, 0))
            ld_free = (ve, 0)
            ot_free = (self.d_out, 0)
            for c in range(8):
                cols = slice(c * 512, (c + 1) * 512)
                xcols = slice(HALO + c * 512, HALO + (c + 1) * 512)
                w(SP, mdone, mdone2, ld_free,
                  self.ffn_done if hasattr(self, "ffn_done")
                  else (self.d_g, 0))
                self.d_cl.inc(SP.dma_start(
                    out=ymb.rearrange("p (a n) -> p a n", a=8),
                    in_=self.ymoe_d[:, cols].rearrange("(a p) n -> p a n", p=128)), 16)
                self.d_cl.inc(SP.dma_start(
                    out=xnb.rearrange("p (a n) -> p a n", a=8),
                    in_=self.xn_d[:, xcols].rearrange("(a p) n -> p a n", p=128)), 16)
                self.d_cl.inc(SP.dma_start(
                    out=ggb.rearrange("p (a n) -> p a n", a=8),
                    in_=self.g_dd[:, cols].rearrange("(a p) n -> p a n", p=128)), 16)
                self.d_cl.inc(SP.dma_start(
                    out=xxb.rearrange("p (a n) -> p a n", a=8),
                    in_=self.xfm[:, xcols].rearrange("(a p) n -> p a n", p=128)), 16)
                ldrdy = (self.d_cl, self.d_cl.n)
                t8 = lambda b, i: b[:, i * 512:(i + 1) * 512]
                w(V, ldrdy, ot_free)
                for i in range(8):
                    ve.inc(V.tensor_add(t8(sgb, i), t8(ymb, i), t8(xnb, i)))
                w(V, (ve, ve.n))
                for i in range(8):
                    ve.inc(V.tensor_mul(t8(sgb, i), t8(sgb, i), t8(ggb, i)))
                w(V, (ve, ve.n))
                for i in range(8):
                    ve.inc(V.tensor_add(t8(otb, i), t8(sgb, i), t8(xxb, i)))
                ld_free = (ve, ve.n)
                w(A, (ve, ve.n))
                self.d_out.inc(A.dma_start(
                    out=self.out_fm[:, cols].rearrange("(a p) n -> p a n", p=128),
                    in_=otb.rearrange("p (a n) -> p a n", a=8)), 16)
                ot_free = (self.d_out, self.d_out.n)

    def build(self):
        self.declare()
        with self.st:
            self.alloc()
            with self.nc.Block():
                self.p0_consts()
                if "p1" in self.phases:
                    self.p1_front()
                if "ex1" in self.phases:
                    self.ex1(); self.assemble1()
                if "ffn" in self.phases:
                    self.ffn_branch()
                if "scan" in self.phases:
                    self.scan_phase()
                if "ex2" in self.phases:
                    self.ex2(); self.assemble2()
                if "tail" in self.phases:
                    self.tail_phase()
                if "moe" in self.phases:
                    self.moe_phase()
                if "comb" in self.phases:
                    self.combine_phase()
                for s_ in ("d_xn", "d_xr", "d_res", "d_dt", "d_y", "d_xa",
                           "d_x1", "d_g", "d_xm", "d_sp", "d_out"):
                    self.w(self.A, (getattr(self, s_), getattr(self, s_).n))
                for s_ in ("dG1", "dG2", "d_wt", "d_u", "d_bc", "d_ym",
                           "d_ym2", "d_sp2", "dZg"):
                    self.w(self.G, (getattr(self, s_), getattr(self, s_).n))
                for s_ in ("dS", "d_sl", "d_sl2", "d_ug", "d_ffs", "d_ffs2",
                           "d_tl", "d_mx", "d_mx2", "d_cl", "dZs"):
                    self.w(self.SP, (getattr(self, s_), getattr(self, s_).n))
        return self.nc


def _host_prep(inputs):
    x = inputs["L0_attn_x"]
    f32 = np.float32
    cf = np.ascontiguousarray

    def bft(a):
        return cf(a.astype(BFNP))

    def rot_tabs(S):
        t = np.arange(S - 2048, S + 4096, dtype=np.float64)
        hi = np.arange(0, 64, 2, dtype=np.float64)
        inv = 1.0 / 10000.0 ** (hi / 64.0)
        fr = t[:, None] * inv[None, :]
        fr = np.concatenate([fr, fr], -1)
        base = (hi + 0.4 * 64) / (1.4 * 64)
        power = (t - 4096.0) / 256.0
        sc = base[None, :] ** power[:, None]
        sc = np.concatenate([sc, sc], -1)
        cos, sin = np.cos(fr), np.sin(fr)
        r = lambda a: bft(np.tile(a.T.astype(f32), (2, 1)))
        return r(cos * sc), r(sin * sc), r(cos / sc), r(sin / sc)

    ii = np.arange(128)[:, None]
    jj = np.arange(128)[None, :]
    m_triu = np.where(jj >= ii, 0.0, NEG).astype(f32)
    m_tril = np.where(jj <= ii, 0.0, NEG).astype(f32)
    ident = np.eye(128, dtype=BFNP)
    ident32 = np.eye(128, dtype=f32)
    ones_r = np.ones((1, 128), BFNP)
    ind2 = np.zeros((128, 2), BFNP)
    ind2[0:64, 0] = 1
    ind2[64:128, 1] = 1
    rep2 = np.zeros((2, 128), BFNP)
    rep2[0, 0:64] = 1
    rep2[1, 64:128] = 1
    P64 = np.zeros((64, 64), f32)
    P64[0:32, 32:64] = -np.eye(32)
    P64[32:64, 0:32] = np.eye(32)
    rotp = np.zeros((128, 128), f32)
    rotp[0:64, 0:64] = P64
    rotp[64:128, 64:128] = P64
    rot_pt = bft(rotp.T)
    ckarr = np.tile(np.array(CKVALS, f32)[None, :], (128, 1))

    convw = inputs["L0_attn_conv_w"][:, 0, :]
    convw_t = cf(convw.reshape(16, 128, 4).transpose(1, 0, 2)
                 .reshape(128, 64).astype(f32))
    convb_t = cf(inputs["L0_attn_conv_b"].reshape(16, 128).T.astype(f32))
    dvec_t = cf(inputs["L0_attn_D"].reshape(16, 128).T.astype(f32))
    qsc = inputs["L0_ffn_q_scale"]
    qsc_t = cf(np.tile(qsc.T, (2, 1)).astype(f32))
    ksc_t = cf(np.tile(inputs["L0_ffn_k_scale"].T, (2, 1)).astype(f32))
    lng_t = cf(inputs["L0_ffn_ln_g"].reshape(2, 8, 128)
               .transpose(2, 0, 1).reshape(128, 16).astype(f32))
    lnb_t = cf(inputs["L0_ffn_ln_b"].reshape(2, 8, 128)
               .transpose(2, 0, 1).reshape(128, 16).astype(f32))
    fw1 = inputs["L0_ffn_w1"]
    fb1 = inputs["L0_ffn_b1"]
    w1a = np.zeros((2, DIM, HID2), f32)
    w1g = np.zeros((2, DIM, HID2), f32)
    w1a[:, :, 0:2730] = fw1[:, :, 0:2730]
    w1g[:, :, 0:2730] = fw1[:, :, 2730:5460]
    b1a = np.zeros((2, HID2), f32)
    b1g = np.zeros((2, HID2), f32)
    b1a[:, 0:2730] = fb1[:, 0:2730]
    b1g[:, 0:2730] = fb1[:, 2730:5460]
    b1a_t = cf(b1a.reshape(2, 22, 128).transpose(2, 0, 1).reshape(128, 44))
    b1g_t = cf(b1g.reshape(2, 22, 128).transpose(2, 0, 1).reshape(128, 44))
    w2pad = np.zeros((2, HID2, DIM), f32)
    w2pad[:, 0:2730, :] = inputs["L0_ffn_w2"]
    fb2_t = cf(inputs["L0_ffn_b2"].reshape(2, 8, 128)
               .transpose(2, 0, 1).reshape(128, 16).astype(f32))
    inp_t = cf(inputs["L0_attn_in_proj_w"].T.astype(f32))
    xproj_t = np.zeros((DI, 128), f32)
    xproj_t[:, 0:96] = inputs["L0_attn_x_proj_w"].T
    outp_t = cf(inputs["L0_attn_out_proj_w"].T.astype(f32))
    gate_t = cf(inputs["L0_attn_gate_w"].T.astype(f32))
    mw1 = cf(inputs["L0_attn_w1"].transpose(0, 2, 1).astype(f32))
    mw2 = cf(inputs["L0_attn_w2"].transpose(0, 2, 1).astype(f32))
    dtp_t_full = inputs["L0_attn_dt_proj_w"].T.astype(f32)

    shared = dict(
        m_triu=m_triu, m_tril=m_tril, identv=ident, onesrv=ones_r,
        ind2v=ind2, rep2v=rep2, rotptv=rot_pt, ckv=ckarr, convwv=convw_t,
        convbv=convb_t, dvecv=dvec_t, qscv=qsc_t, kscv=ksc_t, lngv=lng_t,
        lnbv=lnb_t, b1av=b1a_t, b1gv=b1g_t, fb2v=fb2_t,
        ident32=ident32,
        wqkv=cf(inputs["L0_ffn_wqkv"].astype(f32)),
        wo_w=cf(inputs["L0_ffn_wo"].astype(f32)),
        w1av=cf(w1a), w1gv=cf(w1g), w2pv=cf(w2pad),
        inp_t=inp_t, xproj_t=xproj_t, outp_t=outp_t, gate_t=gate_t,
        mw1=mw1, mw2=mw2,
    )
    in_maps = []
    for core in range(8):
        b, p = core // 2, core % 2
        Sa = p * 4096
        lo = Sa - HALO
        xs = np.zeros((TOTC, DIM), f32)
        src_lo = max(lo, 0)
        xs[src_lo - lo:] = x[b, src_lo:Sa + 4096]
        tqc, tqs, tkc, tks = rot_tabs(Sa)
        dsh = slice(p * 1024, p * 1024 + 1024)
        m = dict(shared)
        m.update(
            xfm=cf(xs.T),
            tqc=tqc, tqs=tqs, tkc=tkc, tks=tks,
            m_w0=np.full((128, 512), NEGW0 if p == 0 else 0.0, f32),
            x1zv=np.full((128, 1), 0.0 if p == 0 else 1.0, f32),
            parv=np.array([[p]], np.int32),
            dtbv=cf(inputs["L0_attn_dt_proj_b"][dsh].reshape(8, 128).T
                    .astype(f32)),
            alogv=cf(inputs["L0_attn_A_log"][dsh].astype(f32)),
            dtproj_t=cf(dtp_t_full[:, dsh]),
        )
        in_maps.append(m)
    return in_maps


def kernel(**inputs):
    in_maps = _host_prep(inputs)
    nc = K().build()
    res = run_bass_kernel_spmd(nc, in_maps, core_ids=list(range(8)))
    out = np.zeros((4, 8192, DIM), np.float32)
    for core in range(8):
        b, p = core // 2, core % 2
        out[b, p * 4096:(p + 1) * 4096] = res.results[core]["out_fm"].T
    return out


# version 03
# speedup vs baseline: 1.0296x; 1.0296x over previous
import contextlib
import numpy as np
import ml_dtypes
import concourse.bass as bass
import concourse.mybir as mybir
from concourse.bass_utils import run_bass_kernel_spmd

F32 = mybir.dt.float32
BF16 = mybir.dt.bfloat16
I32 = mybir.dt.int32
AF = mybir.ActivationFunctionType
ALU = mybir.AluOpType
AX = mybir.AxisListType
BFNP = ml_dtypes.bfloat16

NEG = -3.0e37
NEGW0 = -1.0e37
DIM = 1024
DI = 2048
NST = 16
DTR = 64
HALO = 2560
TOK = 4096
TOTC = HALO + TOK
NCH = TOTC // 512
L1Q0 = HALO - 2048
HID2 = 2816
MHID = 4096
LN32 = float(np.log(32.0))

CKVALS = [0.0, 1.0, -1.0, 1e-5, -0.5, LN32, 1.0 / 1024.0, 1e-24, 8.0, -1e30]
CK = {v: i for i, v in enumerate(CKVALS)}
NCK = len(CKVALS)


class S:
    def __init__(self, h):
        self.h = h
        self.n = 0

    def inc(self, instr, k=1):
        instr.then_inc(self.h, k)
        self.n += k
        return self.n


class K:
    def __init__(self, dbg=(), phases=("p1", "ex1", "ffn", "scan", "ex2",
                                       "tail", "moe", "comb")):
        self.dbg = set(dbg)
        self.phases = set(phases)
        self.nc = bass.Bass("TRN2", target_bir_lowering=False, debug=False)
        self.st = contextlib.ExitStack()

    def dram(self, name, shape, dt, kind=None):
        kk = {}
        if kind is not None:
            kk["kind"] = kind
        elif name in self.dbg:
            kk["kind"] = "ExternalOutput"
        return self.nc.dram_tensor(name, shape, dt, **kk).ap()

    def sb(self, n, sh, dt):
        return self.st.enter_context(self.nc.sbuf_tensor(n, sh, dt))

    def psumt(self, n, sh, dt):
        return self.st.enter_context(self.nc.psum_tensor(n, sh, dt))

    def sem(self, n):
        return S(self.st.enter_context(self.nc.semaphore(n)))

    def w(self, eng, *deps):
        for s_, c_ in deps:
            if c_ > 0:
                eng.wait_ge(s_.h, c_)

    def ck(self, val, row=False, p=None):
        j = CK[val]
        if p is not None:
            return self.c_ck[0:p, j:j + 1]
        return self.c_ck[0:1, j:j + 1] if row else self.c_ck[:, j:j + 1]

    def bank_wait(self, eng, nm):
        for dep in self.bfree[nm]:
            self.w(eng, dep)
        self.bfree[nm] = []

    def bank_used(self, nm, dep):
        self.bfree[nm].append(dep)

    def enter_phase(self):
        deps = []
        for s_ in ("dS", "dG1", "dG2", "d_wt", "d_u", "d_bc", "d_ym", "d_sp2",
                   "dZg", "d_sl", "d_ug", "d_ffs", "d_tl", "d_mx", "d_cl",
                   "dZs", "d_ffs2", "d_mx2", "d_sl2", "d_ym2",
                   "d_xn", "d_xr", "d_res", "d_dt", "d_y", "d_xa",
                   "d_x1", "d_g", "d_xm", "d_sp", "d_out", "cC",
                   "pe", "ve", "ac", "po"):
            s = getattr(self, s_)
            if s.n > 0:
                deps.append((s, s.n))
        for eng in (self.T, self.V, self.A, self.G, self.SP):
            self.w(eng, *deps)

    def declare(self):
        d = self.dram
        EI = "ExternalInput"
        self.xfm = d("xfm", [DIM, TOTC], F32, EI)
        self.tqc = d("tqc", [128, 6144], BF16, EI)
        self.tqs = d("tqs", [128, 6144], BF16, EI)
        self.tkc = d("tkc", [128, 6144], BF16, EI)
        self.tks = d("tks", [128, 6144], BF16, EI)
        self.m_triu = d("m_triu", [128, 128], F32, EI)
        self.m_tril = d("m_tril", [128, 128], F32, EI)
        self.m_w0 = d("m_w0", [128, 512], F32, EI)
        self.x1zv = d("x1zv", [128, 1], F32, EI)
        self.parv = d("parv", [1, 1], I32, EI)
        self.identv = d("identv", [128, 128], BF16, EI)
        self.ident32v = d("ident32", [128, 128], F32, EI)
        self.onesrv = d("onesrv", [1, 128], BF16, EI)
        self.ind2v = d("ind2v", [128, 2], BF16, EI)
        self.rep2v = d("rep2v", [2, 128], BF16, EI)
        self.rotptv = d("rotptv", [128, 128], BF16, EI)
        self.ckv = d("ckv", [128, NCK], F32, EI)
        self.convwv = d("convwv", [128, 64], F32, EI)
        self.convbv = d("convbv", [128, 16], F32, EI)
        self.dvecv = d("dvecv", [128, 16], F32, EI)
        self.dtbv = d("dtbv", [128, 8], F32, EI)
        self.qscv = d("qscv", [128, 2], F32, EI)
        self.kscv = d("kscv", [128, 2], F32, EI)
        self.lngv = d("lngv", [128, 16], F32, EI)
        self.lnbv = d("lnbv", [128, 16], F32, EI)
        self.b1av = d("b1av", [128, 44], F32, EI)
        self.b1gv = d("b1gv", [128, 44], F32, EI)
        self.fb2v = d("fb2v", [128, 16], F32, EI)
        self.alogv = d("alogv", [1024, 16], F32, EI)
        self.wqkv = d("wqkv", [2, DIM, 1536], F32, EI)
        self.wo_w = d("wo_w", [2, 512, DIM], F32, EI)
        self.w1av = d("w1av", [2, DIM, HID2], F32, EI)
        self.w1gv = d("w1gv", [2, DIM, HID2], F32, EI)
        self.w2pv = d("w2pv", [2, HID2, DIM], F32, EI)
        self.inp_t = d("inp_t", [DIM, 2 * DI], F32, EI)
        self.xproj_t = d("xproj_t", [DI, 128], F32, EI)
        self.dtproj_t = d("dtproj_t", [DTR, 1024], F32, EI)
        self.outp_t = d("outp_t", [DI, DIM], F32, EI)
        self.gate_t = d("gate_t", [DIM, 8], F32, EI)
        self.mw1 = d("mw1", [8, DIM, MHID], F32, EI)
        self.mw2 = d("mw2", [8, MHID, DIM], F32, EI)
        self.xn_d = d("xn_d", [DIM, TOTC], BF16)
        self.u_d = d("u_d", [DI, TOK], BF16)
        self.res_d = d("res_d", [DI, TOK], BF16)
        self.xr1_d = d("xr1_d", [DIM, TOK], BF16)
        self.dt_own = d("dt_own", [DTR, TOK], F32)
        self.bc_own = d("bc_own", [32, TOK], BF16)
        self.uoth_c = d("uoth_c", [1024, TOK], BF16)
        self.uoth_g = d("uoth_g", [2048, TOK], BF16)
        self.dt_gt = d("dt_gt", [2 * DTR, TOK], F32)
        self.bc_gt = d("bc_gt", [64, TOK], BF16)
        self.u_sc = d("u_sc", [1024, 8192], BF16)
        self.dt_sc = d("dt_sc", [DTR, 8192], F32)
        self.bc_sc = d("bc_sc", [32, 8192], BF16)
        self.y_dd = d("y_dd", [1024, 8192], BF16)
        self.y_cc = d("y_cc", [1024, TOK], BF16)
        self.y_gg = d("y_gg", [2048, TOK], BF16)
        self.y_alld = d("y_alld", [DI, TOK], BF16)
        self.xatt_d = d("xatt_d", [DIM, 5632], BF16)
        self.x1_d = d("x1_d", [DIM, 5632], BF16)
        self.g_dd = d("g_dd", [DIM, TOK], BF16)
        self.xm_d = d("xm_d", [DIM, TOK], BF16)
        self.sp_d = d("sp_d", [8, TOK], F32)
        self.ymoe_d = d("ymoe_d", [DIM, TOK], BF16)
        self.out_fm = d("out_fm", [DIM, TOK], F32, "ExternalOutput")

    def alloc(self):
        sb, ps = self.sb, self.psumt
        self.c_id = sb("c_id", [128, 128], BF16)
        self.c_id32 = sb("c_id32", [128, 128], F32)
        self.c_ones = sb("c_ones", [1, 128], BF16)
        self.c_onec = sb("c_onec", [128, 1], BF16)
        self.c_ind2 = sb("c_ind2", [128, 2], BF16)
        self.c_rep2 = sb("c_rep2", [2, 128], BF16)
        self.c_rotp = sb("c_rotp", [128, 128], BF16)
        self.c_triu = sb("c_triu", [128, 128], F32)
        self.c_tril = sb("c_tril", [128, 128], F32)
        self.c_w0 = sb("c_w0", [128, 512], F32)
        self.c_x1z = sb("c_x1z", [128, 1], F32)
        self.c_ck = sb("c_ck", [128, NCK], F32)
        self.c_convw = sb("c_convw", [128, 64], F32)
        self.c_convb = sb("c_convb", [128, 16], F32)
        self.c_dvec = sb("c_dvec", [128, 16], F32)
        self.c_dtb = sb("c_dtb", [128, 8], F32)
        self.c_qsc = sb("c_qsc", [128, 2], F32)
        self.c_ksc = sb("c_ksc", [128, 2], F32)
        self.c_lng = sb("c_lng", [128, 16], F32)
        self.c_lnb = sb("c_lnb", [128, 16], F32)
        self.c_b1a = sb("c_b1a", [128, 44], F32)
        self.c_b1g = sb("c_b1g", [128, 44], F32)
        self.c_fb2 = sb("c_fb2", [128, 16], F32)
        self.c_A = sb("c_A", [128, 128], F32)
        self.c_par = sb("c_par", [1, 1], I32)
        self.g0 = ps("g0", [128, 512], F32)
        self.g1 = ps("g1", [128, 512], F32)
        self.g2 = ps("g2", [128, 512], F32)
        self.g3 = ps("g3", [128, 512], F32)
        self.g4 = ps("g4", [128, 1024], BF16)
        self.g5 = ps("g5", [128, 512], F32)
        self.g6 = ps("g6", [128, 512], F32)
        self.g7 = ps("g7", [128, 512], F32)
        self.bfree = {nm: [] for nm in
                      ("g0", "g1", "g2", "g3", "g4", "g5", "g6", "g7")}
        for nm in ("dS", "dG1", "dG2", "d_wt", "d_u", "d_bc", "d_ym",
                   "d_sp2", "dZg", "d_sl", "d_ug", "d_ffs", "d_tl", "d_mx",
                   "d_cl", "dZs", "d_ffs2", "d_mx2", "d_sl2", "d_ym2",
                   "d_xn", "d_xr", "d_res", "d_dt", "d_y",
                   "d_xa", "d_x1", "d_g", "d_xm", "d_sp", "d_out", "cC",
                   "pe", "ve", "ac", "po"):
            setattr(self, nm, self.sem(nm))
        nc = self.nc
        self.T, self.V, self.A, self.G, self.SP = (
            nc.tensor, nc.vector, nc.scalar, nc.gpsimd, nc.sync)

    def p0_consts(self):
        V, A, SP = self.V, self.A, self.SP
        w, dS, ve, ac = self.w, self.dS, self.ve, self.ac
        pairs = [
            (self.c_id, self.identv), (self.c_id32, self.ident32v),
            (self.c_ones, self.onesrv),
            (self.c_ind2, self.ind2v), (self.c_rep2, self.rep2v),
            (self.c_rotp, self.rotptv), (self.c_triu, self.m_triu),
            (self.c_tril, self.m_tril), (self.c_w0, self.m_w0),
            (self.c_x1z, self.x1zv), (self.c_ck, self.ckv),
            (self.c_convw, self.convwv), (self.c_convb, self.convbv),
            (self.c_dvec, self.dvecv), (self.c_dtb, self.dtbv),
            (self.c_qsc, self.qscv), (self.c_ksc, self.kscv),
            (self.c_lng, self.lngv), (self.c_lnb, self.lnbv),
            (self.c_b1a, self.b1av), (self.c_b1g, self.b1gv),
            (self.c_fb2, self.fb2v), (self.c_par, self.parv),
        ]
        for s_, d_ in pairs:
            dS.inc(SP.dma_start(out=s_[:], in_=d_), 16)
        for gi in range(8):
            dS.inc(SP.dma_start(out=self.c_A[:, gi * 16:(gi + 1) * 16],
                                in_=self.alogv[gi * 128:(gi + 1) * 128, :]), 16)
        ve.inc(self.V.memset(self.c_onec[:], 1.0))
        w(A, (dS, dS.n))
        ac.inc(A.activation(self.c_A[:], self.c_A[:], AF.Exp,
                            bias=self.ck(0.0), scale=self.ck(1.0)))
        w(V, (ac, ac.n))
        ve.inc(V.tensor_scalar_mul(self.c_A[:], self.c_A[:], -1.0))
        self.cr = (ve, ve.n)
        self.crd = (dS, dS.n)

    def p1_front(self):
        nc = self.nc
        T, V, A, G, SP = self.T, self.V, self.A, self.G, self.SP
        w = self.w
        dS = self.dS
        pe, ve, ac = self.pe, self.ve, self.ac
        g0, g1, g2, g3 = self.g0, self.g1, self.g2, self.g3
        self.enter_phase()
        est = contextlib.ExitStack()
        sbl = lambda n, sh, dt: est.enter_context(nc.sbuf_tensor(n, sh, dt))
        with est:
            xb = [sbl(f"p1xb{i}", [128, 8 * 512], BF16) for i in range(2)]
            sq = sbl("p1sq", [128, 8 * 512], BF16)
            rmu = sbl("p1rmu", [1, 512], F32)
            rva = sbl("p1rva", [1, 512], F32)
            rtm = sbl("p1rtm", [1, 512], F32)
            rmub = sbl("p1rmub", [1, 512], BF16)
            rvab = sbl("p1rvab", [1, 512], BF16)
            xnt = sbl("p1xn", [128, 8 * 512], BF16)
            xr1t = sbl("p1xr1", [128, 8 * 512], BF16)
            xmb = sbl("p1xm", [128, 16 * 515], F32)
            resb = [sbl(f"p1res{i}", [128, 512], BF16) for i in range(2)]
            xct = sbl("p1xc", [128, 16 * 512], F32)
            ubt = sbl("p1u", [128, 16 * 512], BF16)
            xdb = sbl("p1xdb", [128, 512], F32)
            wip = sbl("p1wip", [128, 8 * 4096], BF16)
            wxp = sbl("p1wxp", [128, 16 * 128], BF16)

            t8 = lambda buf, i: buf[:, i * 512:(i + 1) * 512]

            self.d_wt.inc(G.dma_start(
                out=wip.rearrange("p (a n) -> p a n", a=8),
                in_=self.inp_t.rearrange("(a p) n -> p a n", p=128)), 16)
            self.d_wt.inc(G.dma_start(
                out=wxp.rearrange("p (a n) -> p a n", a=16),
                in_=self.xproj_t.rearrange("(a p) n -> p a n", p=128)), 16)
            wready = (self.d_wt, self.d_wt.n)

            ldsem = [self.dG1, self.dG2]
            xb_ready = [(dS, 0), (dS, 0)]
            xb_free = [(ve, 0), (ve, 0)]
            sq_free = (pe, 0)
            xnt_free = (ac, 0)
            xct_free = (ac, 0)
            xm_conv_done = (ve, 0)
            res_free_v = [(ve, 0), (ve, 0)]

            for c in range(NCH):
                pb = c % 2
                w(G, xb_free[pb])
                ldsem[pb].inc(G.dma_start(
                    out=xb[pb].rearrange("p (a n) -> p a n", a=8),
                    in_=self.xfm[:, c * 512:(c + 1) * 512]
                        .rearrange("(a p) n -> p a n", p=128)), 16)
                xb_ready[pb] = (ldsem[pb], ldsem[pb].n)

                w(A, xb_ready[pb], sq_free, self.crd)
                for i in range(8):
                    ac.inc(A.activation(t8(sq, i), t8(xb[pb], i), AF.Square,
                                        bias=self.ck(0.0), scale=self.ck(1.0)))
                self.bank_wait(T, "g2")
                self.bank_wait(T, "g3")
                w(T, xb_ready[pb], (ac, ac.n), self.cr)
                for i in range(8):
                    T.matmul(g2[0:1, :], self.c_onec[:], t8(xb[pb], i),
                             start=(i == 0), stop=(i == 7))
                for i in range(8):
                    mm = T.matmul(g3[0:1, :], self.c_onec[:], t8(sq, i),
                                  start=(i == 0), stop=(i == 7))
                pe.inc(mm)
                sq_free = (pe, pe.n)
                w(V, (pe, pe.n), (ac, ac.n))
                ve.inc(V.tensor_scalar_mul(rmu[:], g2[0:1, :], 1.0 / 1024.0))
                ve.inc(V.tensor_scalar_mul(rva[:], g3[0:1, :], 1.0 / 1024.0))
                w(V, (ve, ve.n))
                ve.inc(V.tensor_mul(rtm[:], rmu[:], rmu[:]))
                w(V, (ve, ve.n))
                ve.inc(V.tensor_sub(rva[:], rva[:], rtm[:]))
                self.bank_used("g2", (ve, ve.n))
                self.bank_used("g3", (ve, ve.n))
                w(A, (ve, ve.n))
                ac.inc(A.activation(rva[:], rva[:], AF.Ln,
                                    bias=self.ck(1e-5, True), scale=self.ck(1.0, True)))
                w(A, (ac, ac.n))
                ac.inc(A.activation(rva[:], rva[:], AF.Exp,
                                    bias=self.ck(0.0, True), scale=self.ck(-0.5, True)))
                w(V, (ac, ac.n))
                ve.inc(V.tensor_copy(rmub[:], rmu[:]))
                ve.inc(V.tensor_copy(rvab[:], rva[:]))
                self.bank_wait(T, "g2")
                self.bank_wait(T, "g3")
                w(T, (ve, ve.n))
                m1 = T.matmul(g2[:], self.c_ones[:], rmub[:], start=True, stop=True)
                m2 = T.matmul(g3[:], self.c_ones[:], rvab[:], start=True, stop=True)
                pe.inc(m2)
                w(V, (pe, pe.n), (self.d_xn, self.d_xn.n), xnt_free)
                for i in range(8):
                    ve.inc(V.tensor_sub(t8(xnt, i), t8(xb[pb], i), g2[:]))
                w(V, (ve, ve.n))
                for i in range(8):
                    ve.inc(V.tensor_mul(t8(xnt, i), t8(xnt, i), g3[:]))
                xb_free[pb] = (ve, ve.n)
                self.bank_used("g2", (ve, ve.n))
                self.bank_used("g3", (ve, ve.n))
                xn_done = (ve, ve.n)
                w(A, xn_done)
                self.d_xn.inc(A.dma_start(
                    out=self.xn_d[:, c * 512:(c + 1) * 512]
                        .rearrange("(a p) n -> p a n", p=128),
                    in_=xnt.rearrange("p (a n) -> p a n", a=8)), 16)

                if c < 4:
                    continue

                w(A, xn_done, sq_free)
                for i in range(8):
                    ac.inc(A.activation(t8(sq, i), t8(xnt, i), AF.Square,
                                        bias=self.ck(0.0), scale=self.ck(1.0)))
                xnt_free = (ac, ac.n)
                self.bank_wait(T, "g2")
                w(T, (ac, ac.n))
                for i in range(8):
                    mm = T.matmul(g2[0:1, :], self.c_onec[:], t8(sq, i),
                                  start=(i == 0), stop=(i == 7))
                pe.inc(mm)
                sq_free = (pe, pe.n)
                w(V, (pe, pe.n))
                ve.inc(V.tensor_scalar_max(rtm[:], g2[0:1, :], 1e-24))
                self.bank_used("g2", (ve, ve.n))
                w(A, (ve, ve.n))
                ac.inc(A.activation(rtm[:], rtm[:], AF.Ln,
                                    bias=self.ck(0.0, True), scale=self.ck(1.0, True)))
                w(A, (ac, ac.n))
                ac.inc(A.activation(rtm[:], rtm[:], AF.Exp,
                                    bias=self.ck(LN32, True), scale=self.ck(-0.5, True)))
                w(V, (ac, ac.n), (pe, pe.n))
                ve.inc(V.tensor_copy(rvab[:], rtm[:]))
                self.bank_wait(T, "g2")
                w(T, (ve, ve.n))
                m1 = T.matmul(g2[:], self.c_ones[:], rvab[:], start=True, stop=True)
                pe.inc(m1)
                w(V, (pe, pe.n), (self.d_xr, self.d_xr.n))
                for i in range(8):
                    ve.inc(V.tensor_mul(t8(xr1t, i), t8(xnt, i), g2[:]))
                self.bank_used("g2", (ve, ve.n))
                xr1_done = (ve, ve.n)
                if c >= 5:
                    w(A, xr1_done)
                    self.d_xr.inc(A.dma_start(
                        out=self.xr1_d[:, (c - 5) * 512:(c - 4) * 512]
                            .rearrange("(a p) n -> p a n", p=128),
                        in_=xr1t.rearrange("p (a n) -> p a n", a=8)), 16)

                w(V, xm_conv_done, xct_free)
                if c >= 5:
                    ve.inc(V.tensor_copy(
                        xmb.rearrange("p (a n) -> p a n", a=16)[:, :, 0:3],
                        xmb.rearrange("p (a n) -> p a n", a=16)[:, :, 512:515]))
                carry_done = (ve, ve.n)

                banks = [g0, g1, self.g5, self.g6]
                bnames = ["g0", "g1", "g5", "g6"]
                w(T, xr1_done, wready)
                for mi in range(32):
                    bk = banks[mi % 4]
                    bn = bnames[mi % 4]
                    self.bank_wait(T, bn)
                    for kk in range(8):
                        mm = T.matmul(bk[:], wip[:, kk * 4096 + mi * 128:
                                                 kk * 4096 + (mi + 1) * 128],
                                      t8(xr1t, kk), start=(kk == 0), stop=(kk == 7))
                    pe.inc(mm)
                    mmdep = (pe, pe.n)
                    if mi < 16:
                        w(V, mmdep, carry_done)
                        ve.inc(V.tensor_copy(
                            xmb[:, mi * 515 + 3:mi * 515 + 515], bk[:]))
                        self.bank_used(bn, (ve, ve.n))
                    else:
                        rb = resb[mi % 2]
                        w(V, mmdep, (self.d_res, self.d_res.n),
                          res_free_v[mi % 2])
                        ve.inc(V.tensor_copy(rb[:], bk[:]))
                        res_free_v[mi % 2] = (ve, ve.n)
                        self.bank_used(bn, (ve, ve.n))
                        if c >= 5:
                            w(A, (ve, ve.n))
                            self.d_res.inc(A.dma_start(
                                out=self.res_d[(mi - 16) * 128:(mi - 15) * 128,
                                               (c - 5) * 512:(c - 4) * 512],
                                in_=rb[:]), 16)
                xm_ready = (ve, ve.n)

                if c < 5:
                    xm_conv_done = (ve, ve.n)
                    continue

                w(V, xm_ready, xct_free)
                for j in range(16):
                    base = j * 515
                    ve.inc(V.tensor_scalar(
                        xct[:, j * 512:(j + 1) * 512],
                        xmb[:, base:base + 512],
                        self.c_convw[:, j * 4 + 0:j * 4 + 1], None, ALU.mult))
                w(V, (ve, ve.n))
                for kk in range(1, 4):
                    for j in range(16):
                        base = j * 515
                        ve.inc(V.scalar_tensor_tensor(
                            xct[:, j * 512:(j + 1) * 512],
                            xmb[:, base + kk:base + kk + 512],
                            self.c_convw[:, j * 4 + kk:j * 4 + kk + 1],
                            xct[:, j * 512:(j + 1) * 512], ALU.mult, ALU.add))
                    w(V, (ve, ve.n))
                xm_conv_done = (ve, ve.n)
                w(A, (ve, ve.n), (pe, pe.n), (self.d_u, self.d_u.n))
                for j in range(16):
                    ac.inc(A.activation(ubt[:, j * 512:(j + 1) * 512],
                                        xct[:, j * 512:(j + 1) * 512], AF.Silu,
                                        bias=self.c_convb[:, j:j + 1],
                                        scale=self.ck(1.0)))
                u_done = (ac, ac.n)
                xct_free = (ac, ac.n)
                w(G, u_done)
                self.d_u.inc(G.dma_start(
                    out=self.u_d[:, (c - 5) * 512:(c - 4) * 512]
                        .rearrange("(a p) n -> p a n", p=128),
                    in_=ubt.rearrange("p (a n) -> p a n", a=16)), 16)

                self.bank_wait(T, "g3")
                w(T, u_done, wready)
                for kk in range(16):
                    mm = T.matmul(g3[:], wxp[:, kk * 128:(kk + 1) * 128],
                                  ubt[:, kk * 512:(kk + 1) * 512],
                                  start=(kk == 0), stop=(kk == 15))
                pe.inc(mm)
                w(V, (pe, pe.n), (self.d_dt, self.d_dt.n),
                  (self.d_bc, self.d_bc.n))
                ve.inc(V.tensor_copy(xdb[:], g3[:]))
                self.bank_used("g3", (ve, ve.n))
                w(A, (ve, ve.n))
                self.d_dt.inc(A.dma_start(
                    out=self.dt_own[:, (c - 5) * 512:(c - 4) * 512],
                    in_=xdb[0:64, :]), 16)
                w(G, (ve, ve.n))
                self.d_bc.inc(G.dma_start(
                    out=self.bc_own[:, (c - 5) * 512:(c - 4) * 512],
                    in_=xdb[64:96, :]), 16)
            self.p1_done = ((self.d_xn, self.d_xn.n), (self.d_xr, self.d_xr.n),
                            (self.d_res, self.d_res.n),
                            (self.d_u, self.d_u.n), (self.d_dt, self.d_dt.n),
                            (self.d_bc, self.d_bc.n))

    def ffn_branch(self):
        for lay in (0, 1):
            self.stream_dep = ((self.d_xn, self.d_xn.n) if lay == 0
                               else (self.d_x1, self.d_x1.n))
            self.ffn_passA(lay)
            self.xatt_dep = (self.d_xa, self.d_xa.n)
            self.ffn_passB(lay)
        self.ffn_done = (self.d_g, self.d_g.n)

    def ffn_passA(self, lay):
        nc = self.nc
        T, V, A, G, SP = self.T, self.V, self.A, self.G, self.SP
        w = self.w
        pe, ve, ac = self.pe, self.ve, self.ac
        g0, g1, g2, g3, g4, g5, g6, g7 = (self.g0, self.g1, self.g2, self.g3,
                                          self.g4, self.g5, self.g6, self.g7)
        stream = self.xn_d if lay == 0 else self.x1_d
        base = L1Q0 if lay == 0 else 512
        nqb = 12 if lay == 0 else 10
        tab0 = 0 if lay == 0 else 1024
        w0qb = 4 if lay == 0 else 2
        self.enter_phase()
        est = contextlib.ExitStack()
        sbl = lambda n, sh, dt: est.enter_context(
            nc.sbuf_tensor(f"{n}_{lay}", sh, dt))
        with est:
            wqb = sbl("a_wq", [128, 8 * 1536], BF16)
            wob = sbl("a_wo", [128, 4 * 1024], BF16)
            xin = [sbl(f"a_xi{i}", [128, 8 * 512], BF16) for i in range(2)]
            xsh = [sbl(f"a_xs{i}", [128, 4 * 512], BF16) for i in range(2)]
            tbq = sbl("a_tbq", [128, 2 * 512], BF16)
            tbk = sbl("a_tbk", [128, 2 * 512], BF16)
            qkvb = sbl("a_qkv", [128, 12 * 512], BF16)
            sq2 = sbl("a_sq2", [128, 512], BF16)
            rr2 = sbl("a_rr2", [2, 512], F32)
            rr2b = sbl("a_rr2b", [2, 512], BF16)
            qn = sbl("a_qn", [128, 512], BF16)
            t1b = sbl("a_t1", [128, 512], BF16)
            t2b = sbl("a_t2", [128, 512], BF16)
            qf = [sbl(f"a_qf{i}", [128, 8 * 512], BF16) for i in range(2)]
            vTc = [sbl(f"a_vT{i}", [128, 4 * 512], BF16) for i in range(2)]
            eAp = [sbl(f"a_eA{i}", [128, 512], BF16) for i in range(2)]
            eBp = [sbl(f"a_eB{i}", [128, 128], BF16) for i in range(2)]
            aTp = [sbl(f"a_aT{i}", [128, 640], BF16) for i in range(2)]
            rA = sbl("a_rA", [128, 1], F32)
            rB = sbl("a_rB", [128, 1], F32)
            r1s = sbl("a_r1s", [128, 1], F32)
            sAp = [sbl(f"a_sA{i}", [128, 1], F32) for i in range(2)]
            sBp = [sbl(f"a_sB{i}", [128, 1], F32) for i in range(2)]
            osb = sbl("a_osb", [128, 4 * 512], BF16)
            xatt = sbl("a_xatt", [128, 8 * 512], BF16)

            pdone = getattr(self, "p1_done", ())
            w(G, *pdone)
            self.d_wt.inc(G.dma_start(
                out=wqb.rearrange("p (a n) -> p a n", a=8),
                in_=self.wqkv[lay].rearrange("(a p) n -> p a n", p=128)), 16)
            self.d_wt.inc(G.dma_start(
                out=wob.rearrange("p (a n) -> p a n", a=4),
                in_=self.wo_w[lay].rearrange("(a p) n -> p a n", p=128)), 16)
            wrdy = (self.d_wt, self.d_wt.n)

            xin_free = [(pe, 0), (pe, 0)]
            tb_free = (ve, 0)
            qf_free = [(pe, 0), (pe, 0)]
            vT_free = [(pe, 0), (pe, 0)]
            qkvb_free = (ve, 0)
            osb_free = (pe, 0)
            xatt_free = (self.d_xa, 0)
            eA_free = [(pe, 0), (pe, 0)]
            eB_free = [(pe, 0), (pe, 0)]
            aT_free = [(pe, 0), (pe, 0)]
            t8 = lambda b, i: b[:, i * 512:(i + 1) * 512]

            for bi in range(nqb):
                cur = bi % 2
                prv = 1 - cur
                dffs = self.d_ffs if cur == 0 else self.d_ffs2
                sc0 = base + bi * 512
                w(SP, xin_free[cur], self.stream_dep, *pdone)
                dffs.inc(SP.dma_start(
                    out=xin[cur].rearrange("p (a n) -> p a n", a=8),
                    in_=stream[:, sc0:sc0 + 512]
                        .rearrange("(a p) n -> p a n", p=128)), 16)
                dffs.inc(SP.dma_start(
                    out=xsh[cur].rearrange("p (a n) -> p a n", a=4),
                    in_=stream[512:1024, sc0 - 1:sc0 + 511]
                        .rearrange("(a p) n -> p a n", p=128)), 16)
                w(SP, tb_free)
                dffs.inc(SP.dma_start(
                    out=tbq[:, 0:512], in_=self.tqc[:, tab0 + bi * 512:
                                                    tab0 + (bi + 1) * 512]), 16)
                dffs.inc(SP.dma_start(
                    out=tbq[:, 512:1024], in_=self.tqs[:, tab0 + bi * 512:
                                                       tab0 + (bi + 1) * 512]), 16)
                dffs.inc(SP.dma_start(
                    out=tbk[:, 0:512], in_=self.tkc[:, tab0 + bi * 512:
                                                    tab0 + (bi + 1) * 512]), 16)
                dffs.inc(SP.dma_start(
                    out=tbk[:, 512:1024], in_=self.tks[:, tab0 + bi * 512:
                                                       tab0 + (bi + 1) * 512]), 16)
                ldrdy = (dffs, dffs.n)
                banks = [g0, g1]
                bn = ["g0", "g1"]
                w(T, ldrdy, wrdy, qkvb_free)
                for mi in range(12):
                    bk = banks[mi % 2]
                    self.bank_wait(T, bn[mi % 2])
                    for kk in range(8):
                        rhs = t8(xin[cur], kk) if kk < 4 else t8(xsh[cur], kk - 4)
                        mm = T.matmul(bk[:], wqb[:, kk * 1536 + mi * 128:
                                                  kk * 1536 + (mi + 1) * 128],
                                      rhs, start=(kk == 0), stop=(kk == 7))
                    pe.inc(mm)
                    w(V, (pe, pe.n))
                    ve.inc(V.tensor_copy(t8(qkvb, mi), bk[:]))
                    self.bank_used(bn[mi % 2], (ve, ve.n))
                qkv_rdy = (ve, ve.n)
                for i in range(8):
                    w(A, qkv_rdy, (pe, pe.n))
                    ac.inc(A.activation(sq2[:], t8(qkvb, i), AF.Square,
                                        bias=self.ck(0.0), scale=self.ck(1.0)))
                    self.bank_wait(T, "g2")
                    w(T, (ac, ac.n))
                    mm = T.matmul(g2[0:2, :], self.c_ind2[:], sq2[:],
                                  start=True, stop=True)
                    pe.inc(mm)
                    w(V, (pe, pe.n), (ac, ac.n))
                    ve.inc(V.tensor_scalar_max(rr2[:], g2[0:2, :], 1e-24))
                    self.bank_used("g2", (ve, ve.n))
                    w(A, (ve, ve.n))
                    ac.inc(A.activation(rr2[:], rr2[:], AF.Ln,
                                        bias=self.ck(0.0, p=2),
                                        scale=self.ck(1.0, p=2)))
                    w(A, (ac, ac.n))
                    ac.inc(A.activation(rr2[:], rr2[:], AF.Exp,
                                        bias=self.ck(0.0, p=2),
                                        scale=self.ck(-0.5, p=2)))
                    w(V, (ac, ac.n), (pe, pe.n))
                    ve.inc(V.tensor_copy(rr2b[:], rr2[:]))
                    self.bank_wait(T, "g3")
                    w(T, (ve, ve.n))
                    mm = T.matmul(g3[:], self.c_rep2[:], rr2b[:],
                                  start=True, stop=True)
                    pe.inc(mm)
                    w(V, (pe, pe.n), (ac, ac.n))
                    ve.inc(V.tensor_mul(qn[:], t8(qkvb, i), g3[:]))
                    self.bank_used("g3", (ve, ve.n))
                    sc_col = self.c_qsc if i < 4 else self.c_ksc
                    w(V, (ve, ve.n))
                    ve.inc(V.tensor_scalar(qn[:], qn[:],
                                           sc_col[:, lay:lay + 1], None, ALU.mult))
                    self.bank_wait(T, "g2")
                    w(T, (ve, ve.n))
                    mm = T.matmul(g2[:], self.c_rotp[:], qn[:],
                                  start=True, stop=True)
                    pe.inc(mm)
                    tb = tbq if i < 4 else tbk
                    w(V, ldrdy, (ve, ve.n))
                    ve.inc(V.tensor_mul(t1b[:], qn[:], tb[:, 0:512]))
                    w(V, (pe, pe.n), (ve, ve.n))
                    ve.inc(V.tensor_mul(t2b[:], g2[:], tb[:, 512:1024]))
                    self.bank_used("g2", (ve, ve.n))
                    w(V, (ve, ve.n), qf_free[cur] if i == 0 else (ve, 0))
                    ve.inc(V.tensor_add(t8(qf[cur], i), t1b[:], t2b[:]))
                qf_rdy = (ve, ve.n)
                tb_free = (ve, ve.n)
                self.bank_wait(T, "g4")
                w(T, qkv_rdy, vT_free[cur])
                for vt in range(4):
                    for j in range(4):
                        mm = T.transpose(g4[:, j * 128:(j + 1) * 128],
                                         qkvb[:, (8 + vt) * 512 + j * 128:
                                              (8 + vt) * 512 + (j + 1) * 128],
                                         self.c_id[:])
                    pe.inc(mm)
                    w(V, (pe, pe.n))
                    ve.inc(V.tensor_copy(vTc[cur][:, vt * 512:(vt + 1) * 512],
                                         g4[:, 0:512]))
                    self.bank_used("g4", (ve, ve.n))
                    self.bank_wait(T, "g4")
                vT_rdy = (ve, ve.n)
                qkvb_free = (ve, ve.n)

                if bi == 0:
                    xin_free[cur] = (pe, pe.n)
                    qf_free[cur] = (pe, pe.n)
                    vT_free[cur] = (ve, ve.n)
                    continue

                it = 0
                for hp in range(4):
                    for hh in range(2):
                        h = 2 * hp + hh
                        avbank, avbn = (g7, "g7") if hh == 0 else (g0, "g0")
                        self.bank_wait(T, avbn)
                        for qt in range(4):
                            pa_ = it % 2
                            it += 1
                            sA_, sB_ = (sAp[pa_], sBp[pa_])
                            eA_, eB_ = (eAp[pa_], eBp[pa_])
                            aT_ = aTp[pa_]
                            if pa_ == 0:
                                bA, bB, nA, nB = g5, g6, "g5", "g6"
                            else:
                                bA, bB, nA, nB = g2, g3, "g2", "g3"

                            rowsl = slice(hh * 64, hh * 64 + 64)
                            lhsT = qf[cur][rowsl, (h // 2) * 512 + qt * 128:
                                           (h // 2) * 512 + (qt + 1) * 128]
                            self.bank_wait(T, nA)
                            self.bank_wait(T, nB)
                            w(T, qf_rdy)
                            for jt in range(5):
                                ja = qt + jt
                                slot = prv if ja < 4 else cur
                                rhs = qf[slot][rowsl,
                                               (4 + h // 2) * 512 + (ja % 4) * 128:
                                               (4 + h // 2) * 512 + (ja % 4 + 1) * 128]
                                if jt < 4:
                                    mm = T.matmul(bA[:, jt * 128:(jt + 1) * 128],
                                                  lhsT, rhs, start=True, stop=True)
                                else:
                                    mm = T.matmul(bB[:, 0:128], lhsT, rhs,
                                                  start=True, stop=True)
                            pe.inc(mm)
                            w(V, (pe, pe.n))
                            ve.inc(V.tensor_add(bA[:, 0:128], bA[:, 0:128],
                                                self.c_triu[:]))
                            ve.inc(V.tensor_add(bB[:, 0:128], bB[:, 0:128],
                                                self.c_tril[:]))
                            if bi == w0qb:
                                cw = (4 - qt) * 128
                                w(V, (ve, ve.n))
                                ve.inc(V.tensor_add(bA[:, 0:cw], bA[:, 0:cw],
                                                    self.c_w0[:, 0:cw]))
                            w(V, (ve, ve.n))
                            ve.inc(V.reduce_max(rA[:], bA[:], AX.X, negate=True))
                            ve.inc(V.reduce_max(rB[:], bB[:, 0:128], AX.X,
                                                negate=True))
                            w(V, (ve, ve.n))
                            ve.inc(V.tensor_tensor(r1s[:], rA[:], rB[:], ALU.min))
                            w(V, (ve, ve.n))
                            ve.inc(V.tensor_scalar_mul(r1s[:], r1s[:], 8.0))
                            w(A, (ve, ve.n), eA_free[pa_], eB_free[pa_])
                            ac.inc(A.activation(eA_[:], bA[:], AF.Exp,
                                                bias=r1s[:, 0:1],
                                                scale=self.ck(8.0),
                                                accum_out=sA_[:, 0:1]))
                            ac.inc(A.activation(eB_[:], bB[:, 0:128], AF.Exp,
                                                bias=r1s[:, 0:1],
                                                scale=self.ck(8.0),
                                                accum_out=sB_[:, 0:1]))
                            self.bank_used(nA, (ac, ac.n))
                            self.bank_used(nB, (ac, ac.n))
                            w(V, (ac, ac.n))
                            ve.inc(V.tensor_add(sA_[:], sA_[:], sB_[:]))
                            w(V, (ve, ve.n))
                            ve.inc(V.reciprocal(sB_[:], sA_[:]))
                            w(V, (ve, ve.n))
                            ve.inc(V.tensor_scalar(eA_[:], eA_[:], sB_[:, 0:1],
                                                   None, ALU.mult))
                            ve.inc(V.tensor_scalar(eB_[:], eB_[:], sB_[:, 0:1],
                                                   None, ALU.mult))
                            e_rdy = (ve, ve.n)
                            self.bank_wait(T, "g4")
                            w(T, e_rdy, aT_free[pa_])
                            for jt in range(4):
                                mm = T.transpose(g4[:, jt * 128:(jt + 1) * 128],
                                                 eA_[:, jt * 128:(jt + 1) * 128],
                                                 self.c_id[:])
                            mm = T.transpose(g4[:, 512:640], eB_[:], self.c_id[:])
                            pe.inc(mm)
                            eA_free[pa_] = (pe, pe.n)
                            eB_free[pa_] = (pe, pe.n)
                            w(V, (pe, pe.n))
                            ve.inc(V.tensor_copy(aT_[:, 0:640], g4[:, 0:640]))
                            self.bank_used("g4", (ve, ve.n))
                            w(T, (ve, ve.n), vT_rdy)
                            for jt in range(5):
                                ja = qt + jt
                                slot = prv if ja < 4 else cur
                                lhsT = vTc[slot][:, (h // 2) * 512 +
                                                 (ja % 4) * 128 + hh * 64:
                                                 (h // 2) * 512 +
                                                 (ja % 4) * 128 + hh * 64 + 64]
                                mm = T.matmul(avbank[0:64, qt * 128:(qt + 1) * 128],
                                              lhsT, aT_[:, jt * 128:(jt + 1) * 128],
                                              start=(jt == 0), stop=(jt == 4))
                            pe.inc(mm)
                            aT_free[pa_] = (pe, pe.n)
                    w(V, (pe, pe.n), osb_free)
                    ve.inc(V.tensor_copy(osb[0:64, hp * 512:(hp + 1) * 512],
                                         g7[0:64, :]))
                    ve.inc(V.tensor_copy(osb[64:128, hp * 512:(hp + 1) * 512],
                                         g0[0:64, :]))
                    self.bank_used("g7", (ve, ve.n))
                    self.bank_used("g0", (ve, ve.n))
                o_rdy = (ve, ve.n)
                qf_free[prv] = (pe, pe.n)
                vT_free[prv] = (pe, pe.n)
                qf_free[cur] = (pe, pe.n)
                vT_free[cur] = (pe, pe.n)
                w(T, o_rdy)
                for mi in range(8):
                    bk = banks[mi % 2]
                    self.bank_wait(T, bn[mi % 2])
                    for kk in range(4):
                        mm = T.matmul(bk[:], wob[:, kk * 1024 + mi * 128:
                                                  kk * 1024 + (mi + 1) * 128],
                                      t8(osb, kk), start=(kk == 0), stop=(kk == 3))
                    pe.inc(mm)
                    w(V, (pe, pe.n), xatt_free)
                    ve.inc(V.tensor_add(t8(xatt, mi), bk[:], t8(xin[cur], mi)))
                    self.bank_used(bn[mi % 2], (ve, ve.n))
                osb_free = (pe, pe.n)
                xin_free[cur] = (ve, ve.n)
                xin_free[prv] = (pe, pe.n)
                w(A, (ve, ve.n))
                self.d_xa.inc(A.dma_start(
                    out=self.xatt_d[:, (bi - 1) * 512:bi * 512]
                        .rearrange("(a p) n -> p a n", p=128),
                    in_=xatt.rearrange("p (a n) -> p a n", a=8)), 16)
                xatt_free = (self.d_xa, self.d_xa.n)

    def ffn_passB(self, lay):
        nc = self.nc
        T, V, A, G, SP = self.T, self.V, self.A, self.G, self.SP
        w = self.w
        pe, ve, ac = self.pe, self.ve, self.ac
        g0, g1, g2, g3 = self.g0, self.g1, self.g2, self.g3
        nfb = 11 if lay == 0 else 9
        self.enter_phase()
        est = contextlib.ExitStack()
        sbl = lambda n, sh, dt: est.enter_context(
            nc.sbuf_tensor(f"{n}_{lay}", sh, dt))
        with est:
            w1ab = sbl("b_w1a", [128, 8 * HID2], BF16)
            w1gb = sbl("b_w1g", [128, 8 * HID2], BF16)
            w2pb = sbl("b_w2p", [128, 22 * 1024], BF16)
            xat = sbl("b_xat", [128, 8 * 512], BF16)
            xshb = sbl("b_xsh", [128, 4 * 512], BF16)
            sqf = sbl("b_sqf", [128, 8 * 512], BF16)
            rmu2 = sbl("b_rmu", [1, 512], F32)
            rva2 = sbl("b_rva", [1, 512], F32)
            rtm2 = sbl("b_rtm", [1, 512], F32)
            rmub2 = sbl("b_rmub", [1, 512], BF16)
            rvab2 = sbl("b_rvab", [1, 512], BF16)
            xh = sbl("b_xh", [128, 8 * 512], BF16)
            ab = sbl("b_ab", [128, 22 * 512], BF16)
            ggt = sbl("b_ggt", [128, 512], BF16)
            x2b = sbl("b_x2", [128, 8 * 512], BF16)
            t8 = lambda b, i: b[:, i * 512:(i + 1) * 512]

            w(G, (self.d_xa, self.d_xa.n))
            self.d_wt.inc(G.dma_start(
                out=w1ab.rearrange("p (a n) -> p a n", a=8),
                in_=self.w1av[lay].rearrange("(a p) n -> p a n", p=128)), 16)
            self.d_wt.inc(G.dma_start(
                out=w1gb.rearrange("p (a n) -> p a n", a=8),
                in_=self.w1gv[lay].rearrange("(a p) n -> p a n", p=128)), 16)
            self.d_wt.inc(G.dma_start(
                out=w2pb.rearrange("p (a n) -> p a n", a=22),
                in_=self.w2pv[lay].rearrange("(a p) n -> p a n", p=128)), 16)
            wrdy = (self.d_wt, self.d_wt.n)

            ld_free = (pe, 0)
            ab_free = (pe, 0)
            x2_free = (self.d_x1, 0)
            xh_free = (pe, 0)
            for fb in range(nfb):
                dffs = self.d_ffs if fb % 2 == 0 else self.d_ffs2
                c0 = fb * 512
                csh = max(c0 - 1, 0)
                w(SP, ld_free, self.xatt_dep)
                dffs.inc(SP.dma_start(
                    out=xat.rearrange("p (a n) -> p a n", a=8),
                    in_=self.xatt_d[:, c0:c0 + 512]
                        .rearrange("(a p) n -> p a n", p=128)), 16)
                dffs.inc(SP.dma_start(
                    out=xshb.rearrange("p (a n) -> p a n", a=4),
                    in_=self.xatt_d[512:1024, csh:csh + 512]
                        .rearrange("(a p) n -> p a n", p=128)), 16)
                ldrdy = (dffs, dffs.n)
                hhs = lambda i: t8(xat, i) if i < 4 else t8(xshb, i - 4)
                w(A, ldrdy, (pe, pe.n))
                for i in range(8):
                    ac.inc(A.activation(t8(sqf, i), hhs(i), AF.Square,
                                        bias=self.ck(0.0), scale=self.ck(1.0)))
                self.bank_wait(T, "g2")
                self.bank_wait(T, "g3")
                w(T, ldrdy, (ac, ac.n))
                for i in range(8):
                    T.matmul(g2[0:1, :], self.c_onec[:], hhs(i),
                             start=(i == 0), stop=(i == 7))
                for i in range(8):
                    mm = T.matmul(g3[0:1, :], self.c_onec[:], t8(sqf, i),
                                  start=(i == 0), stop=(i == 7))
                pe.inc(mm)
                w(V, (pe, pe.n), (ac, ac.n))
                ve.inc(V.tensor_scalar_mul(rmu2[:], g2[0:1, :], 1.0 / 1024.0))
                ve.inc(V.tensor_scalar_mul(rva2[:], g3[0:1, :], 1.0 / 1024.0))
                w(V, (ve, ve.n))
                ve.inc(V.tensor_mul(rtm2[:], rmu2[:], rmu2[:]))
                w(V, (ve, ve.n))
                ve.inc(V.tensor_sub(rva2[:], rva2[:], rtm2[:]))
                w(A, (ve, ve.n))
                ac.inc(A.activation(rva2[:], rva2[:], AF.Ln,
                                    bias=self.ck(1e-5, True),
                                    scale=self.ck(1.0, True)))
                w(A, (ac, ac.n))
                ac.inc(A.activation(rva2[:], rva2[:], AF.Exp,
                                    bias=self.ck(0.0, True),
                                    scale=self.ck(-0.5, True)))
                w(V, (ac, ac.n), (pe, pe.n))
                ve.inc(V.tensor_copy(rmub2[:], rmu2[:]))
                ve.inc(V.tensor_copy(rvab2[:], rva2[:]))
                w(T, (ve, ve.n))
                T.matmul(g2[:], self.c_ones[:], rmub2[:], start=True, stop=True)
                mm = T.matmul(g3[:], self.c_ones[:], rvab2[:],
                              start=True, stop=True)
                pe.inc(mm)
                w(V, (pe, pe.n), xh_free)
                for i in range(8):
                    ve.inc(V.tensor_sub(t8(xh, i), hhs(i), g2[:]))
                w(V, (ve, ve.n))
                for i in range(8):
                    ve.inc(V.tensor_mul(t8(xh, i), t8(xh, i), g3[:]))
                w(V, (ve, ve.n))
                for i in range(8):
                    ve.inc(V.tensor_scalar(
                        t8(xh, i), t8(xh, i), self.c_lng[:, lay * 8 + i:
                                                         lay * 8 + i + 1],
                        self.c_lnb[:, lay * 8 + i:lay * 8 + i + 1],
                        ALU.mult, ALU.add))
                self.bank_used("g2", (ve, ve.n))
                self.bank_used("g3", (ve, ve.n))
                xh_rdy = (ve, ve.n)
                banks = [g0, g1, self.g5, self.g6]
                bn = ["g0", "g1", "g5", "g6"]
                w(T, xh_rdy, wrdy, ab_free)
                for hc in range(22):
                    bk = banks[hc % 4]
                    self.bank_wait(T, bn[hc % 4])
                    for kk in range(8):
                        mm = T.matmul(bk[:], w1ab[:, kk * HID2 + hc * 128:
                                                  kk * HID2 + (hc + 1) * 128],
                                      t8(xh, kk), start=(kk == 0), stop=(kk == 7))
                    pe.inc(mm)
                    w(V, (pe, pe.n))
                    ve.inc(V.tensor_scalar(
                        ab[:, hc * 512:(hc + 1) * 512], bk[:],
                        self.c_b1a[:, lay * 22 + hc:lay * 22 + hc + 1],
                        None, ALU.add))
                    self.bank_used(bn[hc % 4], (ve, ve.n))
                for hc in range(22):
                    bk = banks[hc % 4]
                    self.bank_wait(T, bn[hc % 4])
                    for kk in range(8):
                        mm = T.matmul(bk[:], w1gb[:, kk * HID2 + hc * 128:
                                                  kk * HID2 + (hc + 1) * 128],
                                      t8(xh, kk), start=(kk == 0), stop=(kk == 7))
                    pe.inc(mm)
                    w(A, (pe, pe.n), (ve, ve.n))
                    ac.inc(A.activation(
                        ggt[:], bk[:], AF.Gelu,
                        bias=self.c_b1g[:, lay * 22 + hc:lay * 22 + hc + 1],
                        scale=self.ck(1.0)))
                    self.bank_used(bn[hc % 4], (ac, ac.n))
                    w(V, (ac, ac.n))
                    ve.inc(V.tensor_mul(ab[:, hc * 512:(hc + 1) * 512],
                                        ab[:, hc * 512:(hc + 1) * 512], ggt[:]))
                hf_rdy = (ve, ve.n)
                w(T, hf_rdy, x2_free)
                for mi in range(8):
                    bk = banks[mi % 4]
                    self.bank_wait(T, bn[mi % 4])
                    for kk in range(22):
                        mm = T.matmul(bk[:], w2pb[:, kk * 1024 + mi * 128:
                                                  kk * 1024 + (mi + 1) * 128],
                                      ab[:, kk * 512:(kk + 1) * 512],
                                      start=(kk == 0), stop=(kk == 21))
                    pe.inc(mm)
                    w(V, (pe, pe.n))
                    ve.inc(V.scalar_tensor_tensor(
                        t8(x2b, mi), bk[:],
                        self.c_fb2[:, lay * 8 + mi:lay * 8 + mi + 1],
                        t8(xat, mi), ALU.add, ALU.add))
                    self.bank_used(bn[mi % 4], (ve, ve.n))
                ab_free = (pe, pe.n)
                xh_free = (pe, pe.n)
                ld_free = (ve, ve.n)
                if lay == 0:
                    if fb < 3:
                        w(V, (ve, ve.n))
                        for i in range(8):
                            ve.inc(V.tensor_scalar(
                                t8(x2b, i), t8(x2b, i),
                                self.c_x1z[:, 0:1], None, ALU.mult))
                    w(A, (ve, ve.n))
                    self.d_x1.inc(A.dma_start(
                        out=self.x1_d[:, c0:c0 + 512]
                            .rearrange("(a p) n -> p a n", p=128),
                        in_=x2b.rearrange("p (a n) -> p a n", a=8)), 16)
                    x2_free = (self.d_x1, self.d_x1.n)
                else:
                    w(A, (ve, ve.n))
                    for i in range(8):
                        ac.inc(A.activation(t8(x2b, i), t8(x2b, i), AF.Silu,
                                            bias=self.ck(0.0),
                                            scale=self.ck(1.0)))
                    if fb >= 1:
                        w(A, (ac, ac.n))
                        self.d_g.inc(A.dma_start(
                            out=self.g_dd[:, (fb - 1) * 512:fb * 512]
                                .rearrange("(a p) n -> p a n", p=128),
                            in_=x2b.rearrange("p (a n) -> p a n", a=8)), 16)
                        x2_free = (self.d_g, self.d_g.n)
                    else:
                        x2_free = (ac, ac.n)

    def ex1(self):
        G, w = self.G, self.w
        w(G, (self.d_u, self.d_u.n), (self.d_dt, self.d_dt.n),
          (self.d_bc, self.d_bc.n), (self.dS, self.dS.n))
        with G.register("r0") as r0:
            G.reg_load(r0, self.c_par[0:1, 0:1])
            G.reg_mul(r0, r0, -1024 * TOK)
            G.reg_add(r0, r0, 1024 * TOK)
            srcap = bass.AP(self.u_d.tensor, r0, [[TOK, 1024], [1, TOK]])
            self.dZg.inc(G.dma_start(out=self.uoth_c, in_=srcap), 16)
        w(G, (self.dZg, self.dZg.n))
        for cin, cout in ((self.uoth_c, self.uoth_g),
                          (self.dt_own, self.dt_gt),
                          (self.bc_own, self.bc_gt)):
            self.cC.inc(G.collective_compute(
                "AllGather", ALU.bypass,
                replica_groups=[[0, 1], [2, 3], [4, 5], [6, 7]],
                ins=[cin.opt()], outs=[cout.opt()]))

    def assemble1(self):
        SP, w = self.SP, self.w
        w(SP, (self.cC, self.cC.n))
        with SP.register("rp") as rp, SP.register("rt") as rt, \
                SP.register("rs") as rs:
            SP.reg_load(rp, self.c_par[0:1, 0:1])
            SP.reg_mul(rt, rp, 1024 * TOK)
            SP.reg_mul(rs, rp, TOK)
            self.dZs.inc(SP.dma_start(
                out=bass.AP(self.u_sc.tensor, rs, [[8192, 1024], [1, TOK]]),
                in_=bass.AP(self.u_d.tensor, rt, [[TOK, 1024], [1, TOK]])), 16)
            SP.reg_mul(rt, rp, -1024 * TOK)
            SP.reg_add(rt, rt, 1024 * TOK)
            SP.reg_mul(rs, rp, -TOK)
            SP.reg_add(rs, rs, TOK)
            self.dZs.inc(SP.dma_start(
                out=bass.AP(self.u_sc.tensor, rs, [[8192, 1024], [1, TOK]]),
                in_=bass.AP(self.uoth_g.tensor, rt, [[TOK, 1024], [1, TOK]])), 16)
            SP.reg_mul(rs, rp, TOK)
            self.dZs.inc(SP.dma_start(
                out=bass.AP(self.dt_sc.tensor, rs, [[8192, DTR], [1, TOK]]),
                in_=self.dt_own), 16)
            SP.reg_mul(rt, rp, -DTR * TOK)
            SP.reg_add(rt, rt, DTR * TOK)
            SP.reg_mul(rs, rp, -TOK)
            SP.reg_add(rs, rs, TOK)
            self.dZs.inc(SP.dma_start(
                out=bass.AP(self.dt_sc.tensor, rs, [[8192, DTR], [1, TOK]]),
                in_=bass.AP(self.dt_gt.tensor, rt, [[TOK, DTR], [1, TOK]])), 16)
            SP.reg_mul(rs, rp, TOK)
            self.dZs.inc(SP.dma_start(
                out=bass.AP(self.bc_sc.tensor, rs, [[8192, 32], [1, TOK]]),
                in_=self.bc_own), 16)
            SP.reg_mul(rt, rp, -32 * TOK)
            SP.reg_add(rt, rt, 32 * TOK)
            SP.reg_mul(rs, rp, -TOK)
            SP.reg_add(rs, rs, TOK)
            self.dZs.inc(SP.dma_start(
                out=bass.AP(self.bc_sc.tensor, rs, [[8192, 32], [1, TOK]]),
                in_=bass.AP(self.bc_gt.tensor, rt, [[TOK, 32], [1, TOK]])), 16)
        self.scan_in_ready = (self.dZs, self.dZs.n)

    def scan_phase(self):
        nc = self.nc
        T, V, A, G, SP = self.T, self.V, self.A, self.G, self.SP
        w = self.w
        pe, ve, ac, po = self.pe, self.ve, self.ac, self.po
        g0, g1 = self.g0, self.g1
        self.enter_phase()
        est = contextlib.ExitStack()
        sbl = lambda n, sh, dt: est.enter_context(nc.sbuf_tensor(n, sh, dt))
        with est:
            wdt = sbl("s_wdt", [DTR, 1024], BF16)
            brep = sbl("s_brep", [128, 16 * 1024], BF16)
            crep = sbl("s_crep", [128, 16 * 1024], BF16)
            sdt = sbl("s_sdt", [DTR, 1024], F32)
            sdtb = sbl("s_sdtb", [DTR, 1024], BF16)
            sdel = sbl("s_sdel", [128, 1024], F32)
            t_a = sbl("s_ta", [128, 1024], F32)
            t_b = sbl("s_tb", [128, 1024], F32)
            sdu = sbl("s_sdu", [128, 1024], BF16)
            sda = [sbl(f"s_sda{i}", [128, 1024], BF16) for i in range(2)]
            sdb = [sbl(f"s_sdb{i}", [128, 1024], BF16) for i in range(2)]
            sH = [sbl(f"s_sH{i}", [128, 1024], BF16) for i in range(2)]
            ytm = [sbl(f"s_ytm{i}", [128, 1024], BF16) for i in range(2)]
            ypar = sbl("s_ypar", [128, 12 * 1024], BF16)
            yg = sbl("s_yg", [128, 1024], BF16)
            ug = sbl("s_ug", [128, 1024], BF16)
            Hc = sbl("s_Hc", [128, 128], F32)

            rdy = self.scan_in_ready if hasattr(self, "scan_in_ready") else (self.dZs, 0)
            w(V, rdy)
            ve.inc(V.memset(Hc[:], 0.0))
            w(G, rdy)
            self.d_wt.inc(G.dma_start(out=wdt[:], in_=self.dtproj_t), 16)
            wdt_ready = (self.d_wt, self.d_wt.n)

            bc_free = [(po, 0), (ve, 0)]
            sdt_free = (ve, 0)
            ug_free = (ve, 0)
            sdel_free = [(ac, 0), (po, 0), (ve, 0)]
            sda_free = [(ve, 0), (ve, 0)]
            sdb_free = [(ve, 0), (ve, 0)]
            sH_free = [(ve, 0), (ve, 0)]
            ytm_free = [(ve, 0), (ve, 0)]
            ypar_free = (ve, 0)
            yg_free = (self.d_y, 0)

            for s in range(8):
                cols = slice(s * 1024, (s + 1) * 1024)
                w(SP, rdy, sdt_free)
                self.d_sl2.inc(SP.dma_start(out=sdt[:], in_=self.dt_sc[:, cols]), 16)
                w(SP, *bc_free)
                for n in range(16):
                    self.d_sl.inc(SP.dma_start(
                        out=brep[:, n * 1024:(n + 1) * 1024],
                        in_=self.bc_sc[n:n + 1, cols]
                            .partition_broadcast(128).squeeze(1)), 16)
                    self.d_sl.inc(SP.dma_start(
                        out=crep[:, n * 1024:(n + 1) * 1024],
                        in_=self.bc_sc[16 + n:17 + n, cols]
                            .partition_broadcast(128).squeeze(1)), 16)
                bc_ready = (self.d_sl, self.d_sl.n)
                w(V, (self.d_sl2, self.d_sl2.n))
                ve.inc(V.tensor_copy(sdtb[:], sdt[:]))
                sdt_free = (ve, ve.n)
                sdtb_ready = (ve, ve.n)

                for g in range(8):
                    self.bank_wait(T, "g0")
                    self.bank_wait(T, "g1")
                    w(T, sdtb_ready, wdt_ready)
                    m1 = T.matmul(g0[:], wdt[:, g * 128:(g + 1) * 128],
                                  sdtb[:, 0:512], start=True, stop=True)
                    m2 = T.matmul(g1[:], wdt[:, g * 128:(g + 1) * 128],
                                  sdtb[:, 512:1024], start=True, stop=True)
                    pe.inc(m2)
                    w(V, (pe, pe.n), *sdel_free)
                    ve.inc(V.tensor_scalar(sdel[:, 0:512], g0[:],
                                           self.c_dtb[:, g:g + 1], None, ALU.add))
                    ve.inc(V.tensor_scalar(sdel[:, 512:1024], g1[:],
                                           self.c_dtb[:, g:g + 1], None, ALU.add))
                    self.bank_used("g0", (ve, ve.n))
                    self.bank_used("g1", (ve, ve.n))
                    w(A, (ve, ve.n), (ac, ac.n))
                    ac.inc(A.activation(t_a[:], sdel[:], AF.Abs,
                                        bias=self.ck(0.0), scale=self.ck(1.0)))
                    w(A, (ac, ac.n))
                    ac.inc(A.activation(t_b[:], t_a[:], AF.Exp,
                                        bias=self.ck(0.0), scale=self.ck(-1.0)))
                    w(A, (ac, ac.n))
                    ac.inc(A.activation(t_b[:], t_b[:], AF.Ln,
                                        bias=self.ck(1.0), scale=self.ck(1.0)))
                    w(V, (ac, ac.n))
                    ve.inc(V.tensor_scalar(t_a[:], sdel[:], 0.0, None, ALU.max))
                    w(V, (ve, ve.n))
                    ve.inc(V.tensor_add(sdel[:], t_a[:], t_b[:]))
                    sdel_ready = (ve, ve.n)
                    w(SP, ug_free)
                    self.d_ug.inc(SP.dma_start(
                        out=ug[:], in_=self.u_sc[g * 128:(g + 1) * 128, cols]), 16)
                    w(V, (self.d_ug, self.d_ug.n), sdel_ready, (po, po.n))
                    ve.inc(V.tensor_mul(sdu[:], sdel[:], ug[:]))
                    ug_free = (ve, ve.n)
                    sdu_ready = (ve, ve.n)

                    for n in range(16):
                        pa = n % 2
                        col = g * 16 + n
                        w(A, sdel_ready, sda_free[pa])
                        ac.inc(A.activation(sda[pa][:], sdel[:], AF.Exp,
                                            bias=self.ck(0.0),
                                            scale=self.c_A[:, col:col + 1]))
                        w(G, sdu_ready, bc_ready, sdb_free[pa])
                        po.inc(G.tensor_mul(sdb[pa][:], sdu[:],
                                            brep[:, n * 1024:(n + 1) * 1024]))
                        w(V, (ac, ac.n), (po, po.n), sH_free[pa])
                        ve.inc(V.tensor_tensor_scan(
                            sH[pa][:], sda[pa][:], sdb[pa][:],
                            Hc[:, col:col + 1], ALU.mult, ALU.add))
                        sda_free[pa] = (ve, ve.n)
                        sdb_free[pa] = (ve, ve.n)
                        w(V, (ve, ve.n))
                        ve.inc(V.tensor_copy(Hc[:, col:col + 1],
                                             sH[pa][:, 1023:1024]))
                        w(V, ytm_free[pa])
                        ve.inc(V.tensor_mul(ytm[pa][:], sH[pa][:],
                                            crep[:, n * 1024:(n + 1) * 1024]))
                        sH_free[pa] = (ve, ve.n)
                        if pa == 1:
                            w(V, (ve, ve.n), ypar_free)
                            ve.inc(V.tensor_add(
                                ypar[:, (n // 2) * 1024:(n // 2 + 1) * 1024],
                                ytm[0][:], ytm[1][:]))
                            ytm_free = [(ve, ve.n), (ve, ve.n)]
                    w(V, (po, po.n), yg_free)
                    for i2 in range(4):
                        ve.inc(V.tensor_add(
                            ypar[:, (8 + i2) * 1024:(9 + i2) * 1024],
                            ypar[:, 2 * i2 * 1024:(2 * i2 + 1) * 1024],
                            ypar[:, (2 * i2 + 1) * 1024:(2 * i2 + 2) * 1024]))
                    w(V, (ve, ve.n))
                    for i2 in range(2):
                        ve.inc(V.tensor_add(
                            ypar[:, i2 * 1024:(i2 + 1) * 1024],
                            ypar[:, (8 + 2 * i2) * 1024:(9 + 2 * i2) * 1024],
                            ypar[:, (9 + 2 * i2) * 1024:(10 + 2 * i2) * 1024]))
                    w(V, (ve, ve.n))
                    ve.inc(V.tensor_add(yg[:], ypar[:, 0:1024],
                                        ypar[:, 1024:2048]))
                    ypar_free = (ve, ve.n)
                    sdel_free = [(ac, ac.n), (ve, ve.n), (po, po.n)]
                    w(A, (ve, ve.n))
                    self.d_y.inc(A.dma_start(
                        out=self.y_dd[g * 128:(g + 1) * 128, cols], in_=yg[:]), 16)
                    yg_free = (self.d_y, self.d_y.n)
                bc_free = [(po, po.n), (ve, ve.n)]
        self.scan_done = (self.d_y, self.d_y.n)

    def ex2(self):
        G, w = self.G, self.w
        w(G, self.scan_done if hasattr(self, "scan_done") else (self.d_y, 0))
        with G.register("r1") as r1:
            G.reg_load(r1, self.c_par[0:1, 0:1])
            G.reg_mul(r1, r1, -TOK)
            G.reg_add(r1, r1, TOK)
            self.dZg.inc(G.dma_start(
                out=self.y_cc,
                in_=bass.AP(self.y_dd.tensor, r1, [[8192, 1024], [1, TOK]])), 16)
        w(G, (self.dZg, self.dZg.n))
        self.cC.inc(G.collective_compute(
            "AllGather", ALU.bypass,
            replica_groups=[[0, 1], [2, 3], [4, 5], [6, 7]],
            ins=[self.y_cc.opt()], outs=[self.y_gg.opt()]))

    def assemble2(self):
        SP, w = self.SP, self.w
        w(SP, (self.cC, self.cC.n))
        with SP.register("rp2") as rp, SP.register("rt2") as rt, \
                SP.register("rs2") as rs:
            SP.reg_load(rp, self.c_par[0:1, 0:1])
            SP.reg_mul(rt, rp, TOK)
            SP.reg_mul(rs, rp, 1024 * TOK)
            self.dZs.inc(SP.dma_start(
                out=bass.AP(self.y_alld.tensor, rs, [[TOK, 1024], [1, TOK]]),
                in_=bass.AP(self.y_dd.tensor, rt, [[8192, 1024], [1, TOK]])), 16)
            SP.reg_mul(rt, rp, -1024 * TOK)
            SP.reg_add(rt, rt, 1024 * TOK)
            self.dZs.inc(SP.dma_start(
                out=bass.AP(self.y_alld.tensor, rt, [[TOK, 1024], [1, TOK]]),
                in_=bass.AP(self.y_gg.tensor, rt, [[TOK, 1024], [1, TOK]])), 16)
        self.y_ready = (self.dZs, self.dZs.n)

    def tail_phase(self):
        nc = self.nc
        T, V, A, G, SP = self.T, self.V, self.A, self.G, self.SP
        w = self.w
        pe, ve, ac = self.pe, self.ve, self.ac
        g0, g1, g2, g3, g5 = self.g0, self.g1, self.g2, self.g3, self.g5
        self.enter_phase()
        est = contextlib.ExitStack()
        sbl = lambda n, sh, dt: est.enter_context(nc.sbuf_tensor(n, sh, dt))
        with est:
            wop = sbl("t_wop", [128, 16 * 1024], BF16)
            wgt = sbl("t_wgt", [128, 8 * 8], BF16)
            ytb = sbl("t_yt", [128, 16 * 512], BF16)
            utb = sbl("t_ut", [128, 16 * 512], BF16)
            rsb = sbl("t_rs", [128, 16 * 512], BF16)
            xrb = sbl("t_xr", [128, 8 * 512], BF16)
            srs = sbl("t_srs", [128, 16 * 512], BF16)
            ytg = sbl("t_ytg", [128, 16 * 512], BF16)
            mot = sbl("t_mot", [128, 8 * 512], F32)
            sqb = sbl("t_sq", [128, 8 * 512], BF16)
            rr = sbl("t_rr", [1, 512], F32)
            rrb = sbl("t_rrb", [1, 512], BF16)
            xmt = sbl("t_xmt", [128, 8 * 512], BF16)
            lgt = sbl("t_lgt", [128, 128], F32)
            ltm = sbl("t_ltm", [128, 32], F32)
            spt = sbl("t_spt", [128, 128], F32)
            spf = sbl("t_spf", [8, 512], F32)
            r1 = sbl("t_r1", [128, 1], F32)
            r2 = sbl("t_r2", [128, 1], F32)
            r3 = sbl("t_r3", [128, 1], F32)
            egt = sbl("t_egt", [128, 8], F32)
            selt = sbl("t_selt", [128, 8], F32)
            mskt = sbl("t_mskt", [128, 8], F32)
            wtt = sbl("t_wtt", [128, 8], F32)

            yrdy = self.y_ready if hasattr(self, "y_ready") else (self.dZs, 0)
            w(G, yrdy, (self.dS, self.dS.n))
            self.d_wt.inc(G.dma_start(
                out=wop.rearrange("p (a n) -> p a n", a=16),
                in_=self.outp_t.rearrange("(a p) n -> p a n", p=128)), 16)
            self.d_wt.inc(G.dma_start(
                out=wgt.rearrange("p (a n) -> p a n", a=8),
                in_=self.gate_t.rearrange("(a p) n -> p a n", p=128)), 16)
            wrdy = (self.d_wt, self.d_wt.n)
            w(V, (self.dS, self.dS.n))
            ve.inc(V.memset(lgt[:], 0.0))
            ve.inc(V.memset(spt[:], 0.0))

            lds_free = [(pe, 0)]
            mot_free = (ac, 0)
            for c in range(8):
                cols = slice(c * 512, (c + 1) * 512)
                w(SP, yrdy, *lds_free)
                self.d_tl.inc(SP.dma_start(
                    out=ytb.rearrange("p (a n) -> p a n", a=16),
                    in_=self.y_alld[:, cols].rearrange("(a p) n -> p a n", p=128)), 16)
                self.d_tl.inc(SP.dma_start(
                    out=utb.rearrange("p (a n) -> p a n", a=16),
                    in_=self.u_d[:, cols].rearrange("(a p) n -> p a n", p=128)), 16)
                self.d_tl.inc(SP.dma_start(
                    out=rsb.rearrange("p (a n) -> p a n", a=16),
                    in_=self.res_d[:, cols].rearrange("(a p) n -> p a n", p=128)), 16)
                self.d_tl.inc(SP.dma_start(
                    out=xrb.rearrange("p (a n) -> p a n", a=8),
                    in_=self.xr1_d[:, cols].rearrange("(a p) n -> p a n", p=128)), 16)
                ldrdy = (self.d_tl, self.d_tl.n)
                t16 = lambda b, i: b[:, i * 512:(i + 1) * 512]
                w(A, ldrdy, (ve, ve.n))
                for j in range(16):
                    ac.inc(A.activation(t16(srs, j), t16(rsb, j), AF.Silu,
                                        bias=self.ck(0.0), scale=self.ck(1.0)))
                w(V, ldrdy, (pe, pe.n))
                for j in range(16):
                    ve.inc(V.scalar_tensor_tensor(
                        t16(ytg, j), t16(utb, j), self.c_dvec[:, j:j + 1],
                        t16(ytb, j), ALU.mult, ALU.add))
                w(V, (ve, ve.n), (ac, ac.n))
                for j in range(16):
                    ve.inc(V.tensor_mul(t16(ytg, j), t16(ytg, j), t16(srs, j)))
                ytg_rdy = (ve, ve.n)
                banks = [g0, g1, self.g5, self.g6]
                bn = ["g0", "g1", "g5", "g6"]
                w(T, ytg_rdy, wrdy)
                for mi in range(8):
                    bk = banks[mi % 4]
                    self.bank_wait(T, bn[mi % 4])
                    for kk in range(16):
                        mm = T.matmul(bk[:], wop[:, kk * 1024 + mi * 128:
                                                 kk * 1024 + (mi + 1) * 128],
                                      t16(ytg, kk), start=(kk == 0), stop=(kk == 15))
                    pe.inc(mm)
                    w(V, (pe, pe.n), mot_free)
                    ve.inc(V.tensor_add(t16(mot, mi), bk[:],
                                        xrb[:, mi * 512:(mi + 1) * 512]))
                    self.bank_used(bn[mi % 4], (ve, ve.n))
                mot_rdy = (ve, ve.n)
                lds_free = [mot_rdy]
                w(A, mot_rdy, (self.d_xm, self.d_xm.n))
                for i in range(8):
                    ac.inc(A.activation(t16(sqb, i), t16(mot, i), AF.Square,
                                        bias=self.ck(0.0), scale=self.ck(1.0)))
                self.bank_wait(T, "g2")
                w(T, (ac, ac.n))
                for i in range(8):
                    mm = T.matmul(g2[0:1, :], self.c_onec[:], t16(sqb, i),
                                  start=(i == 0), stop=(i == 7))
                pe.inc(mm)
                w(V, (pe, pe.n), (ac, ac.n))
                ve.inc(V.tensor_scalar_max(rr[:], g2[0:1, :], 1e-24))
                self.bank_used("g2", (ve, ve.n))
                w(A, (ve, ve.n))
                ac.inc(A.activation(rr[:], rr[:], AF.Ln,
                                    bias=self.ck(0.0, True), scale=self.ck(1.0, True)))
                w(A, (ac, ac.n))
                ac.inc(A.activation(rr[:], rr[:], AF.Exp,
                                    bias=self.ck(LN32, True), scale=self.ck(-0.5, True)))
                w(V, (ac, ac.n), (pe, pe.n))
                ve.inc(V.tensor_copy(rrb[:], rr[:]))
                self.bank_wait(T, "g3")
                w(T, (ve, ve.n))
                mm = T.matmul(g3[:], self.c_ones[:], rrb[:], start=True, stop=True)
                pe.inc(mm)
                w(V, (pe, pe.n), (self.d_xm, self.d_xm.n))
                for i in range(8):
                    ve.inc(V.tensor_mul(t16(xmt, i), t16(mot, i), g3[:]))
                self.bank_used("g3", (ve, ve.n))
                mot_free = (ve, ve.n)
                xmt_rdy = (ve, ve.n)
                w(A, xmt_rdy)
                self.d_xm.inc(A.dma_start(
                    out=self.xm_d[:, cols].rearrange("(a p) n -> p a n", p=128),
                    in_=xmt.rearrange("p (a n) -> p a n", a=8)), 16)
                self.bank_wait(T, "g2")
                w(T, xmt_rdy, wrdy)
                for kk in range(8):
                    mm = T.matmul(g2[0:8, :], wgt[:, kk * 8:(kk + 1) * 8],
                                  t16(xmt, kk), start=(kk == 0), stop=(kk == 7))
                pe.inc(mm)
                for tt in range(4):
                    w(V, (pe, pe.n), (ve, ve.n))
                    ve.inc(V.tensor_copy(lgt[0:8, 0:128],
                                         g2[0:8, tt * 128:(tt + 1) * 128]))
                    if tt == 3:
                        self.bank_used("g2", (ve, ve.n))
                    self.bank_wait(T, "g5")
                    w(T, (ve, ve.n))
                    mm = T.transpose(g5[:, 0:128], lgt[:], self.c_id32[:])
                    pe.inc(mm)
                    w(V, (pe, pe.n))
                    ve.inc(V.tensor_copy(ltm[:, 0:8], g5[:, 0:8]))
                    self.bank_used("g5", (ve, ve.n))
                    lt = ltm[:, 0:8]
                    w(V, (ve, ve.n))
                    ve.inc(V.reduce_max(r1[:], lt, AX.X, negate=True))
                    w(V, (ve, ve.n))
                    ve.inc(V.tensor_scalar_mul(r2[:], r1[:], -1.0))
                    w(A, (ve, ve.n))
                    ac.inc(A.activation(egt[:], lt, AF.Exp,
                                        bias=r1[:, 0:1], scale=self.ck(1.0)))
                    w(V, (ve, ve.n))
                    ve.inc(V.tensor_scalar(selt[:], lt, r2[:, 0:1], None,
                                           ALU.is_ge))
                    w(V, (ve, ve.n))
                    ve.inc(V.scalar_tensor_tensor(mskt[:], selt[:], -1e30,
                                                  lt, ALU.mult, ALU.add))
                    w(V, (ve, ve.n))
                    ve.inc(V.reduce_max(r3[:], mskt[:], AX.X))
                    w(V, (ve, ve.n))
                    ve.inc(V.tensor_scalar(selt[:], lt, r3[:, 0:1], None,
                                           ALU.is_ge))
                    w(V, (ve, ve.n), (ac, ac.n))
                    ve.inc(V.tensor_mul(wtt[:], egt[:], selt[:]))
                    w(V, (ve, ve.n))
                    ve.inc(V.reduce_sum(r3[:], wtt[:], AX.X))
                    w(V, (ve, ve.n))
                    ve.inc(V.reciprocal(r2[:], r3[:]))
                    w(V, (ve, ve.n))
                    ve.inc(V.tensor_scalar(spt[:, 0:8], wtt[:], r2[:, 0:1],
                                           None, ALU.mult))
                    self.bank_wait(T, "g5")
                    w(T, (ve, ve.n))
                    mm = T.transpose(g5[:, 0:128], spt[:], self.c_id32[:])
                    pe.inc(mm)
                    w(V, (pe, pe.n), (self.d_sp, self.d_sp.n))
                    ve.inc(V.tensor_copy(spf[0:8, tt * 128:(tt + 1) * 128],
                                         g5[0:8, 0:128]))
                    self.bank_used("g5", (ve, ve.n))
                w(A, (ve, ve.n))
                self.d_sp.inc(A.dma_start(out=self.sp_d[:, cols], in_=spf[0:8, :]), 16)
        self.tail_done = ((self.d_xm, self.d_xm.n), (self.d_sp, self.d_sp.n))

    def moe_phase(self):
        nc = self.nc
        T, V, A, G, SP = self.T, self.V, self.A, self.G, self.SP
        w = self.w
        pe, ve, ac = self.pe, self.ve, self.ac
        g0, g1, g2, g3 = self.g0, self.g1, self.g2, self.g3
        self.enter_phase()
        est = contextlib.ExitStack()
        sbl = lambda n, sh, dt: est.enter_context(nc.sbuf_tensor(n, sh, dt))
        with est:
            w1b = [sbl(f"m_w1{i}", [128, 8 * 2048], BF16) for i in range(2)]
            w2b = [sbl(f"m_w2{i}", [128, 16 * 1024], BF16) for i in range(2)]
            xmb = [sbl(f"m_xm{i}", [128, 8 * 512], BF16) for i in range(2)]
            hfb = sbl("m_hf", [128, 16 * 512], BF16)
            sprep = sbl("m_sp", [128, TOK], BF16)
            ysb = [sbl(f"m_ysb{i}", [128, 512], BF16) for i in range(2)]

            tdone = self.tail_done if hasattr(self, "tail_done") else ()
            w(G, *tdone)
            wl = [self.dG1, self.dG2]
            wrdy = [(self.dG1, 0), (self.dG2, 0)]
            wfree = [(pe, 0), (pe, 0)]

            def load_w(e, h, slot):
                self.w(G, wfree[slot], *tdone)
                wl[slot].inc(G.dma_start(
                    out=w1b[slot].rearrange("p (a n) -> p a n", a=8),
                    in_=self.mw1[e, :, h * 2048:(h + 1) * 2048]
                        .rearrange("(a p) n -> p a n", p=128)), 16)
                wl[slot].inc(G.dma_start(
                    out=w2b[slot].rearrange("p (a n) -> p a n", a=16),
                    in_=self.mw2[e, h * 2048:(h + 1) * 2048, :]
                        .rearrange("(a p) n -> p a n", p=128)), 16)
                wrdy[slot] = (wl[slot], wl[slot].n)

            load_w(0, 0, 0)
            sprep_rdy = (self.d_sp2, 0)
            sprep_free = (ve, 0)
            xm_free = [(pe, 0), (pe, 0)]
            hf_free = [(pe, 0)] * 16
            ysb_free = [(self.d_ym, 0), (self.d_ym2, 0)]
            accum_guard = [(self.d_ym, 0), (self.d_ym2, 0)]

            step = 0
            for e in range(8):
                w(G, sprep_free, *tdone)
                self.d_sp2.inc(G.dma_start(
                    out=sprep[:],
                    in_=self.sp_d[e:e + 1, :].partition_broadcast(128)
                        .squeeze(1)), 16)
                sprep_rdy = (self.d_sp2, self.d_sp2.n)
                for h in range(2):
                    slot = step % 2
                    nstep = step + 1
                    if nstep < 16:
                        ne, nh = nstep // 2, nstep % 2
                        load_w(ne, nh, nstep % 2)
                    for nt in range(8):
                        xs = nt % 2
                        dmx = self.d_mx if xs == 0 else self.d_mx2
                        w(SP, xm_free[xs], *tdone)
                        dmx.inc(SP.dma_start(
                            out=xmb[xs].rearrange("p (a n) -> p a n", a=8),
                            in_=self.xm_d[:, nt * 512:(nt + 1) * 512]
                                .rearrange("(a p) n -> p a n", p=128)), 16)
                        xmrdy = (dmx, dmx.n)
                        bk = [g0, g1, self.g5, self.g6]
                        bn = ["g0", "g1", "g5", "g6"]
                        w(T, xmrdy, wrdy[slot])
                        for hc in range(16):
                            b_ = bk[hc % 4]
                            self.bank_wait(T, bn[hc % 4])
                            for dc in range(8):
                                mm = T.matmul(
                                    b_[:], w1b[slot][:, dc * 2048 + hc * 128:
                                                     dc * 2048 + (hc + 1) * 128],
                                    xmb[xs][:, dc * 512:(dc + 1) * 512],
                                    start=(dc == 0), stop=(dc == 7))
                            pe.inc(mm)
                            w(A, (pe, pe.n), hf_free[hc])
                            ac.inc(A.activation(
                                hfb[:, hc * 512:(hc + 1) * 512], b_[:], AF.Gelu,
                                bias=self.ck(0.0), scale=self.ck(1.0)))
                            self.bank_used(bn[hc % 4], (ac, ac.n))
                        hf_rdy = (ac, ac.n)
                        bk2 = [g2, g3]
                        bn2 = ["g2", "g3"]
                        w(T, hf_rdy)
                        for dc in range(8):
                            b_ = bk2[dc % 2]
                            self.bank_wait(T, bn2[dc % 2])
                            for hc in range(16):
                                mm = T.matmul(
                                    b_[:], w2b[slot][:, hc * 1024 + dc * 128:
                                                     hc * 1024 + (dc + 1) * 128],
                                    hfb[:, hc * 512:(hc + 1) * 512],
                                    start=(hc == 0), stop=(hc == 15))
                            pe.inc(mm)
                            hf_free = [(pe, pe.n)] * 16
                            ys = dc % 2
                            w(V, (pe, pe.n), sprep_rdy, ysb_free[ys])
                            ve.inc(V.tensor_mul(
                                ysb[ys][:], b_[:],
                                sprep[:, nt * 512:(nt + 1) * 512]))
                            self.bank_used(bn2[dc % 2], (ve, ve.n))
                            dym = self.d_ym if ys == 0 else self.d_ym2
                            w(G, (ve, ve.n), *accum_guard)
                            op = ALU.bypass if step == 0 else ALU.add
                            dym.inc(G.dma_start(
                                out=self.ymoe_d[dc * 128:(dc + 1) * 128,
                                                nt * 512:(nt + 1) * 512],
                                in_=ysb[ys][:], accum_op=op), 16)
                            ysb_free[ys] = (dym, dym.n)
                        xm_free[xs] = (pe, pe.n)
                    wfree[slot] = (pe, pe.n)
                    accum_guard = [(self.d_ym, self.d_ym.n),
                                   (self.d_ym2, self.d_ym2.n)]
                    step += 1
                sprep_free = (ve, ve.n)
        self.moe_done = (self.d_ym, self.d_ym.n)
        self.moe_done2 = (self.d_ym2, self.d_ym2.n)

    def combine_phase(self):
        nc = self.nc
        V, A, SP, G = self.V, self.A, self.SP, self.G
        w = self.w
        ve = self.ve
        self.enter_phase()
        est = contextlib.ExitStack()
        sbl = lambda n, sh, dt: est.enter_context(nc.sbuf_tensor(n, sh, dt))
        with est:
            ymb = sbl("c_ym", [128, 8 * 512], BF16)
            xnb = sbl("c_xn", [128, 8 * 512], BF16)
            ggb = sbl("c_gg", [128, 8 * 512], BF16)
            xxb = sbl("c_xx", [128, 8 * 512], F32)
            sgb = sbl("c_sg", [128, 8 * 512], F32)
            otb = sbl("c_ot", [128, 8 * 512], F32)
            mdone = self.moe_done if hasattr(self, "moe_done") else (self.d_ym, 0)
            mdone2 = getattr(self, "moe_done2", (self.d_ym2, 0))
            ld_free = (ve, 0)
            ot_free = (self.d_out, 0)
            for c in range(8):
                cols = slice(c * 512, (c + 1) * 512)
                xcols = slice(HALO + c * 512, HALO + (c + 1) * 512)
                w(SP, mdone, mdone2, ld_free,
                  self.ffn_done if hasattr(self, "ffn_done")
                  else (self.d_g, 0))
                self.d_cl.inc(SP.dma_start(
                    out=ymb.rearrange("p (a n) -> p a n", a=8),
                    in_=self.ymoe_d[:, cols].rearrange("(a p) n -> p a n", p=128)), 16)
                self.d_cl.inc(SP.dma_start(
                    out=xnb.rearrange("p (a n) -> p a n", a=8),
                    in_=self.xn_d[:, xcols].rearrange("(a p) n -> p a n", p=128)), 16)
                self.d_cl.inc(SP.dma_start(
                    out=ggb.rearrange("p (a n) -> p a n", a=8),
                    in_=self.g_dd[:, cols].rearrange("(a p) n -> p a n", p=128)), 16)
                self.d_cl.inc(SP.dma_start(
                    out=xxb.rearrange("p (a n) -> p a n", a=8),
                    in_=self.xfm[:, xcols].rearrange("(a p) n -> p a n", p=128)), 16)
                ldrdy = (self.d_cl, self.d_cl.n)
                t8 = lambda b, i: b[:, i * 512:(i + 1) * 512]
                w(V, ldrdy, ot_free)
                for i in range(8):
                    ve.inc(V.tensor_add(t8(sgb, i), t8(ymb, i), t8(xnb, i)))
                w(V, (ve, ve.n))
                for i in range(8):
                    ve.inc(V.tensor_mul(t8(sgb, i), t8(sgb, i), t8(ggb, i)))
                w(V, (ve, ve.n))
                for i in range(8):
                    ve.inc(V.tensor_add(t8(otb, i), t8(sgb, i), t8(xxb, i)))
                ld_free = (ve, ve.n)
                w(A, (ve, ve.n))
                self.d_out.inc(A.dma_start(
                    out=self.out_fm[:, cols].rearrange("(a p) n -> p a n", p=128),
                    in_=otb.rearrange("p (a n) -> p a n", a=8)), 16)
                ot_free = (self.d_out, self.d_out.n)

    def build(self):
        self.declare()
        with self.st:
            self.alloc()
            with self.nc.Block():
                self.p0_consts()
                if "p1" in self.phases:
                    self.p1_front()
                if "ex1" in self.phases:
                    self.ex1(); self.assemble1()
                if "ffn" in self.phases:
                    self.ffn_branch()
                if "scan" in self.phases:
                    self.scan_phase()
                if "ex2" in self.phases:
                    self.ex2(); self.assemble2()
                if "tail" in self.phases:
                    self.tail_phase()
                if "moe" in self.phases:
                    self.moe_phase()
                if "comb" in self.phases:
                    self.combine_phase()
                for s_ in ("d_xn", "d_xr", "d_res", "d_dt", "d_y", "d_xa",
                           "d_x1", "d_g", "d_xm", "d_sp", "d_out"):
                    self.w(self.A, (getattr(self, s_), getattr(self, s_).n))
                for s_ in ("dG1", "dG2", "d_wt", "d_u", "d_bc", "d_ym",
                           "d_ym2", "d_sp2", "dZg"):
                    self.w(self.G, (getattr(self, s_), getattr(self, s_).n))
                for s_ in ("dS", "d_sl", "d_sl2", "d_ug", "d_ffs", "d_ffs2",
                           "d_tl", "d_mx", "d_mx2", "d_cl", "dZs"):
                    self.w(self.SP, (getattr(self, s_), getattr(self, s_).n))
        return self.nc


def _host_prep(inputs):
    x = inputs["L0_attn_x"]
    f32 = np.float32
    cf = np.ascontiguousarray

    def bft(a):
        return cf(a.astype(BFNP))

    def rot_tabs(S):
        t = np.arange(S - 2048, S + 4096, dtype=np.float64)
        hi = np.arange(0, 64, 2, dtype=np.float64)
        inv = 1.0 / 10000.0 ** (hi / 64.0)
        fr = t[:, None] * inv[None, :]
        fr = np.concatenate([fr, fr], -1)
        base = (hi + 0.4 * 64) / (1.4 * 64)
        power = (t - 4096.0) / 256.0
        sc = base[None, :] ** power[:, None]
        sc = np.concatenate([sc, sc], -1)
        cos, sin = np.cos(fr), np.sin(fr)
        r = lambda a: bft(np.tile(a.T.astype(f32), (2, 1)))
        return r(cos * sc), r(sin * sc), r(cos / sc), r(sin / sc)

    ii = np.arange(128)[:, None]
    jj = np.arange(128)[None, :]
    m_triu = np.where(jj >= ii, 0.0, NEG).astype(f32)
    m_tril = np.where(jj <= ii, 0.0, NEG).astype(f32)
    ident = np.eye(128, dtype=BFNP)
    ident32 = np.eye(128, dtype=f32)
    ones_r = np.ones((1, 128), BFNP)
    ind2 = np.zeros((128, 2), BFNP)
    ind2[0:64, 0] = 1
    ind2[64:128, 1] = 1
    rep2 = np.zeros((2, 128), BFNP)
    rep2[0, 0:64] = 1
    rep2[1, 64:128] = 1
    P64 = np.zeros((64, 64), f32)
    P64[0:32, 32:64] = -np.eye(32)
    P64[32:64, 0:32] = np.eye(32)
    rotp = np.zeros((128, 128), f32)
    rotp[0:64, 0:64] = P64
    rotp[64:128, 64:128] = P64
    rot_pt = bft(rotp.T)
    ckarr = np.tile(np.array(CKVALS, f32)[None, :], (128, 1))

    convw = inputs["L0_attn_conv_w"][:, 0, :]
    convw_t = cf(convw.reshape(16, 128, 4).transpose(1, 0, 2)
                 .reshape(128, 64).astype(f32))
    convb_t = cf(inputs["L0_attn_conv_b"].reshape(16, 128).T.astype(f32))
    dvec_t = cf(inputs["L0_attn_D"].reshape(16, 128).T.astype(f32))
    qsc = inputs["L0_ffn_q_scale"]
    qsc_t = cf(np.tile(qsc.T, (2, 1)).astype(f32))
    ksc_t = cf(np.tile(inputs["L0_ffn_k_scale"].T, (2, 1)).astype(f32))
    lng_t = cf(inputs["L0_ffn_ln_g"].reshape(2, 8, 128)
               .transpose(2, 0, 1).reshape(128, 16).astype(f32))
    lnb_t = cf(inputs["L0_ffn_ln_b"].reshape(2, 8, 128)
               .transpose(2, 0, 1).reshape(128, 16).astype(f32))
    fw1 = inputs["L0_ffn_w1"]
    fb1 = inputs["L0_ffn_b1"]
    w1a = np.zeros((2, DIM, HID2), f32)
    w1g = np.zeros((2, DIM, HID2), f32)
    w1a[:, :, 0:2730] = fw1[:, :, 0:2730]
    w1g[:, :, 0:2730] = fw1[:, :, 2730:5460]
    b1a = np.zeros((2, HID2), f32)
    b1g = np.zeros((2, HID2), f32)
    b1a[:, 0:2730] = fb1[:, 0:2730]
    b1g[:, 0:2730] = fb1[:, 2730:5460]
    b1a_t = cf(b1a.reshape(2, 22, 128).transpose(2, 0, 1).reshape(128, 44))
    b1g_t = cf(b1g.reshape(2, 22, 128).transpose(2, 0, 1).reshape(128, 44))
    w2pad = np.zeros((2, HID2, DIM), f32)
    w2pad[:, 0:2730, :] = inputs["L0_ffn_w2"]
    fb2_t = cf(inputs["L0_ffn_b2"].reshape(2, 8, 128)
               .transpose(2, 0, 1).reshape(128, 16).astype(f32))
    inp_t = cf(inputs["L0_attn_in_proj_w"].T.astype(f32))
    xproj_t = np.zeros((DI, 128), f32)
    xproj_t[:, 0:96] = inputs["L0_attn_x_proj_w"].T
    outp_t = cf(inputs["L0_attn_out_proj_w"].T.astype(f32))
    gate_t = cf(inputs["L0_attn_gate_w"].T.astype(f32))
    mw1 = cf(inputs["L0_attn_w1"].transpose(0, 2, 1).astype(f32))
    mw2 = cf(inputs["L0_attn_w2"].transpose(0, 2, 1).astype(f32))
    dtp_t_full = inputs["L0_attn_dt_proj_w"].T.astype(f32)

    shared = dict(
        m_triu=m_triu, m_tril=m_tril, identv=ident, onesrv=ones_r,
        ind2v=ind2, rep2v=rep2, rotptv=rot_pt, ckv=ckarr, convwv=convw_t,
        convbv=convb_t, dvecv=dvec_t, qscv=qsc_t, kscv=ksc_t, lngv=lng_t,
        lnbv=lnb_t, b1av=b1a_t, b1gv=b1g_t, fb2v=fb2_t,
        ident32=ident32,
        wqkv=cf(inputs["L0_ffn_wqkv"].astype(f32)),
        wo_w=cf(inputs["L0_ffn_wo"].astype(f32)),
        w1av=cf(w1a), w1gv=cf(w1g), w2pv=cf(w2pad),
        inp_t=inp_t, xproj_t=xproj_t, outp_t=outp_t, gate_t=gate_t,
        mw1=mw1, mw2=mw2,
    )
    in_maps = []
    for core in range(8):
        b, p = core // 2, core % 2
        Sa = p * 4096
        lo = Sa - HALO
        xs = np.zeros((TOTC, DIM), f32)
        src_lo = max(lo, 0)
        xs[src_lo - lo:] = x[b, src_lo:Sa + 4096]
        tqc, tqs, tkc, tks = rot_tabs(Sa)
        dsh = slice(p * 1024, p * 1024 + 1024)
        m = dict(shared)
        m.update(
            xfm=cf(xs.T),
            tqc=tqc, tqs=tqs, tkc=tkc, tks=tks,
            m_w0=np.full((128, 512), NEGW0 if p == 0 else 0.0, f32),
            x1zv=np.full((128, 1), 0.0 if p == 0 else 1.0, f32),
            parv=np.array([[p]], np.int32),
            dtbv=cf(inputs["L0_attn_dt_proj_b"][dsh].reshape(8, 128).T
                    .astype(f32)),
            alogv=cf(inputs["L0_attn_A_log"][dsh].astype(f32)),
            dtproj_t=cf(dtp_t_full[:, dsh]),
        )
        in_maps.append(m)
    return in_maps


def kernel(**inputs):
    in_maps = _host_prep(inputs)
    nc = K().build()
    res = run_bass_kernel_spmd(nc, in_maps, core_ids=list(range(8)))
    out = np.zeros((4, 8192, DIM), np.float32)
    for core in range(8):
        b, p = core // 2, core % 2
        out[b, p * 4096:(p + 1) * 4096] = res.results[core]["out_fm"].T
    return out


# version 04
# speedup vs baseline: 1.1133x; 1.0813x over previous
import contextlib
import numpy as np
import ml_dtypes
import concourse.bass as bass
import concourse.mybir as mybir
from concourse.bass_utils import run_bass_kernel_spmd

F32 = mybir.dt.float32
BF16 = mybir.dt.bfloat16
I32 = mybir.dt.int32
AF = mybir.ActivationFunctionType
ALU = mybir.AluOpType
AX = mybir.AxisListType
BFNP = ml_dtypes.bfloat16

NEG = -3.0e37
NEGW0 = -1.0e37
DIM = 1024
DI = 2048
NST = 16
DTR = 64
HALO = 2560
TOK = 4096
TOTC = HALO + TOK
NCH = TOTC // 512
L1Q0 = HALO - 2048
HID2 = 2816
MHID = 4096
LN32 = float(np.log(32.0))

CKVALS = [0.0, 1.0, -1.0, 1e-5, -0.5, LN32, 1.0 / 1024.0, 1e-24, 8.0, -1e30]
CK = {v: i for i, v in enumerate(CKVALS)}
NCK = len(CKVALS)


class S:
    def __init__(self, h):
        self.h = h
        self.n = 0

    def inc(self, instr, k=1):
        instr.then_inc(self.h, k)
        self.n += k
        return self.n


class K:
    def __init__(self, dbg=(), phases=("p1", "ex1", "ffn", "scan", "ex2",
                                       "tail", "moe", "comb")):
        self.dbg = set(dbg)
        self.phases = set(phases)
        self.nc = bass.Bass("TRN2", target_bir_lowering=False, debug=False)
        self.st = contextlib.ExitStack()

    def dram(self, name, shape, dt, kind=None):
        kk = {}
        if kind is not None:
            kk["kind"] = kind
        elif name in self.dbg:
            kk["kind"] = "ExternalOutput"
        return self.nc.dram_tensor(name, shape, dt, **kk).ap()

    def sb(self, n, sh, dt):
        return self.st.enter_context(self.nc.sbuf_tensor(n, sh, dt))

    def psumt(self, n, sh, dt):
        return self.st.enter_context(self.nc.psum_tensor(n, sh, dt))

    def sem(self, n):
        return S(self.st.enter_context(self.nc.semaphore(n)))

    def w(self, eng, *deps):
        for s_, c_ in deps:
            if c_ > 0:
                eng.wait_ge(s_.h, c_)

    def ck(self, val, row=False, p=None):
        j = CK[val]
        if p is not None:
            return self.c_ck[0:p, j:j + 1]
        return self.c_ck[0:1, j:j + 1] if row else self.c_ck[:, j:j + 1]

    def bank_wait(self, eng, nm):
        for dep in self.bfree[nm]:
            self.w(eng, dep)
        self.bfree[nm] = []

    def bank_used(self, nm, dep):
        self.bfree[nm].append(dep)

    def enter_phase(self):
        deps = []
        for s_ in ("dS", "dG1", "dG2", "d_wt", "d_u", "d_bc", "d_ym", "d_sp2",
                   "dZg", "d_sl", "d_ug", "d_ffs", "d_tl", "d_mx", "d_cl",
                   "dZs", "d_ffs2", "d_mx2", "d_sl2", "d_ym2",
                   "d_xn", "d_xr", "d_res", "d_dt", "d_y", "d_xa",
                   "d_x1", "d_g", "d_xm", "d_sp", "d_out", "cC",
                   "pe", "ve", "ac", "po"):
            s = getattr(self, s_)
            if s.n > 0:
                deps.append((s, s.n))
        for eng in (self.T, self.V, self.A, self.G, self.SP):
            self.w(eng, *deps)

    def declare(self):
        d = self.dram
        EI = "ExternalInput"
        self.xfm = d("xfm", [DIM, TOTC], F32, EI)
        self.tqc = d("tqc", [128, 6144], BF16, EI)
        self.tqs = d("tqs", [128, 6144], BF16, EI)
        self.tkc = d("tkc", [128, 6144], BF16, EI)
        self.tks = d("tks", [128, 6144], BF16, EI)
        self.m_triu = d("m_triu", [128, 128], F32, EI)
        self.m_tril = d("m_tril", [128, 128], F32, EI)
        self.m_w0 = d("m_w0", [128, 512], F32, EI)
        self.x1zv = d("x1zv", [128, 1], F32, EI)
        self.parv = d("parv", [1, 1], I32, EI)
        self.identv = d("identv", [128, 128], BF16, EI)
        self.ident32v = d("ident32", [128, 128], F32, EI)
        self.onesrv = d("onesrv", [1, 128], BF16, EI)
        self.ind2v = d("ind2v", [128, 2], BF16, EI)
        self.rep2v = d("rep2v", [2, 128], BF16, EI)
        self.rotptv = d("rotptv", [128, 128], BF16, EI)
        self.ckv = d("ckv", [128, NCK], F32, EI)
        self.convwv = d("convwv", [128, 64], F32, EI)
        self.convbv = d("convbv", [128, 16], F32, EI)
        self.dvecv = d("dvecv", [128, 16], F32, EI)
        self.dtbv = d("dtbv", [128, 8], F32, EI)
        self.qscv = d("qscv", [128, 2], F32, EI)
        self.kscv = d("kscv", [128, 2], F32, EI)
        self.lngv = d("lngv", [128, 16], F32, EI)
        self.lnbv = d("lnbv", [128, 16], F32, EI)
        self.b1av = d("b1av", [128, 44], F32, EI)
        self.b1gv = d("b1gv", [128, 44], F32, EI)
        self.fb2v = d("fb2v", [128, 16], F32, EI)
        self.alogv = d("alogv", [1024, 16], F32, EI)
        self.wqkv = d("wqkv", [2, DIM, 1536], F32, EI)
        self.wo_w = d("wo_w", [2, 512, DIM], F32, EI)
        self.w1av = d("w1av", [2, DIM, HID2], F32, EI)
        self.w1gv = d("w1gv", [2, DIM, HID2], F32, EI)
        self.w2pv = d("w2pv", [2, HID2, DIM], F32, EI)
        self.inp_t = d("inp_t", [DIM, 2 * DI], F32, EI)
        self.xproj_t = d("xproj_t", [DI, 128], F32, EI)
        self.dtproj_t = d("dtproj_t", [DTR, 1024], F32, EI)
        self.outp_t = d("outp_t", [DI, DIM], F32, EI)
        self.gate_t = d("gate_t", [DIM, 8], F32, EI)
        self.mw1 = d("mw1", [8, DIM, MHID], F32, EI)
        self.mw2 = d("mw2", [8, MHID, DIM], F32, EI)
        self.xn_d = d("xn_d", [DIM, TOTC], BF16)
        self.u_d = d("u_d", [DI, TOK], BF16)
        self.res_d = d("res_d", [DI, TOK], BF16)
        self.xr1_d = d("xr1_d", [DIM, TOK], BF16)
        self.dt_own = d("dt_own", [DTR, TOK], F32)
        self.bc_own = d("bc_own", [32, TOK], BF16)
        self.uoth_c = d("uoth_c", [1024, TOK], BF16)
        self.uoth_g = d("uoth_g", [2048, TOK], BF16)
        self.dt_gt = d("dt_gt", [2 * DTR, TOK], F32)
        self.bc_gt = d("bc_gt", [64, TOK], BF16)
        self.u_sc = d("u_sc", [1024, 8192], BF16)
        self.dt_sc = d("dt_sc", [DTR, 8192], F32)
        self.bc_sc = d("bc_sc", [32, 8192], BF16)
        self.y_dd = d("y_dd", [1024, 8192], BF16)
        self.y_cc = d("y_cc", [1024, TOK], BF16)
        self.y_gg = d("y_gg", [2048, TOK], BF16)
        self.y_alld = d("y_alld", [DI, TOK], BF16)
        self.xatt_d = d("xatt_d", [DIM, 5632], BF16)
        self.x1_d = d("x1_d", [DIM, 5632], BF16)
        self.g_dd = d("g_dd", [DIM, TOK], BF16)
        self.xm_d = d("xm_d", [DIM, TOK], BF16)
        self.sp_d = d("sp_d", [8, TOK], F32)
        self.ymoe_d = d("ymoe_d", [DIM, TOK], BF16)
        self.out_fm = d("out_fm", [DIM, TOK], F32, "ExternalOutput")

    def alloc(self):
        sb, ps = self.sb, self.psumt
        self.c_id = sb("c_id", [128, 128], BF16)
        self.c_id32 = sb("c_id32", [128, 128], F32)
        self.c_ones = sb("c_ones", [1, 128], BF16)
        self.c_onec = sb("c_onec", [128, 1], BF16)
        self.c_ind2 = sb("c_ind2", [128, 2], BF16)
        self.c_rep2 = sb("c_rep2", [2, 128], BF16)
        self.c_rotp = sb("c_rotp", [128, 128], BF16)
        self.c_triu = sb("c_triu", [128, 128], F32)
        self.c_tril = sb("c_tril", [128, 128], F32)
        self.c_w0 = sb("c_w0", [128, 512], F32)
        self.c_x1z = sb("c_x1z", [128, 1], F32)
        self.c_ck = sb("c_ck", [128, NCK], F32)
        self.c_convw = sb("c_convw", [128, 64], F32)
        self.c_convb = sb("c_convb", [128, 16], F32)
        self.c_dvec = sb("c_dvec", [128, 16], F32)
        self.c_dtb = sb("c_dtb", [128, 8], F32)
        self.c_qsc = sb("c_qsc", [128, 2], F32)
        self.c_ksc = sb("c_ksc", [128, 2], F32)
        self.c_lng = sb("c_lng", [128, 16], F32)
        self.c_lnb = sb("c_lnb", [128, 16], F32)
        self.c_b1a = sb("c_b1a", [128, 44], F32)
        self.c_b1g = sb("c_b1g", [128, 44], F32)
        self.c_fb2 = sb("c_fb2", [128, 16], F32)
        self.c_A = sb("c_A", [128, 128], F32)
        self.c_par = sb("c_par", [1, 1], I32)
        self.g0 = ps("g0", [128, 512], F32)
        self.g1 = ps("g1", [128, 512], F32)
        self.g2 = ps("g2", [128, 512], F32)
        self.g3 = ps("g3", [128, 512], F32)
        self.g4 = ps("g4", [128, 1024], BF16)
        self.g5 = ps("g5", [128, 512], F32)
        self.g6 = ps("g6", [128, 512], F32)
        self.g7 = ps("g7", [128, 512], F32)
        self.bfree = {nm: [] for nm in
                      ("g0", "g1", "g2", "g3", "g4", "g5", "g6", "g7")}
        for nm in ("dS", "dG1", "dG2", "d_wt", "d_u", "d_bc", "d_ym",
                   "d_sp2", "dZg", "d_sl", "d_ug", "d_ffs", "d_tl", "d_mx",
                   "d_cl", "dZs", "d_ffs2", "d_mx2", "d_sl2", "d_ym2",
                   "d_xn", "d_xr", "d_res", "d_dt", "d_y",
                   "d_xa", "d_x1", "d_g", "d_xm", "d_sp", "d_out", "cC",
                   "pe", "ve", "ac", "po"):
            setattr(self, nm, self.sem(nm))
        nc = self.nc
        self.T, self.V, self.A, self.G, self.SP = (
            nc.tensor, nc.vector, nc.scalar, nc.gpsimd, nc.sync)

    def p0_consts(self):
        V, A, SP = self.V, self.A, self.SP
        w, dS, ve, ac = self.w, self.dS, self.ve, self.ac
        pairs = [
            (self.c_id, self.identv), (self.c_id32, self.ident32v),
            (self.c_ones, self.onesrv),
            (self.c_ind2, self.ind2v), (self.c_rep2, self.rep2v),
            (self.c_rotp, self.rotptv), (self.c_triu, self.m_triu),
            (self.c_tril, self.m_tril), (self.c_w0, self.m_w0),
            (self.c_x1z, self.x1zv), (self.c_ck, self.ckv),
            (self.c_convw, self.convwv), (self.c_convb, self.convbv),
            (self.c_dvec, self.dvecv), (self.c_dtb, self.dtbv),
            (self.c_qsc, self.qscv), (self.c_ksc, self.kscv),
            (self.c_lng, self.lngv), (self.c_lnb, self.lnbv),
            (self.c_b1a, self.b1av), (self.c_b1g, self.b1gv),
            (self.c_fb2, self.fb2v), (self.c_par, self.parv),
        ]
        for s_, d_ in pairs:
            dS.inc(SP.dma_start(out=s_[:], in_=d_), 16)
        for gi in range(8):
            dS.inc(SP.dma_start(out=self.c_A[:, gi * 16:(gi + 1) * 16],
                                in_=self.alogv[gi * 128:(gi + 1) * 128, :]), 16)
        ve.inc(self.V.memset(self.c_onec[:], 1.0))
        w(A, (dS, dS.n))
        ac.inc(A.activation(self.c_A[:], self.c_A[:], AF.Exp,
                            bias=self.ck(0.0), scale=self.ck(1.0)))
        w(V, (ac, ac.n))
        ve.inc(V.tensor_scalar_mul(self.c_A[:], self.c_A[:], -1.0))
        self.cr = (ve, ve.n)
        self.crd = (dS, dS.n)

    def p1_front(self):
        nc = self.nc
        T, V, A, G, SP = self.T, self.V, self.A, self.G, self.SP
        w = self.w
        dS = self.dS
        pe, ve, ac = self.pe, self.ve, self.ac
        g0, g1, g2, g3 = self.g0, self.g1, self.g2, self.g3
        self.enter_phase()
        est = contextlib.ExitStack()
        sbl = lambda n, sh, dt: est.enter_context(nc.sbuf_tensor(n, sh, dt))
        with est:
            xb = [sbl(f"p1xb{i}", [128, 8 * 512], BF16) for i in range(2)]
            sq = sbl("p1sq", [128, 8 * 512], BF16)
            rmu = sbl("p1rmu", [1, 512], F32)
            rva = sbl("p1rva", [1, 512], F32)
            rtm = sbl("p1rtm", [1, 512], F32)
            rmub = sbl("p1rmub", [1, 512], BF16)
            rvab = sbl("p1rvab", [1, 512], BF16)
            xnt = sbl("p1xn", [128, 8 * 512], BF16)
            xr1t = sbl("p1xr1", [128, 8 * 512], BF16)
            xmb = sbl("p1xm", [128, 16 * 515], F32)
            resb = [sbl(f"p1res{i}", [128, 512], BF16) for i in range(2)]
            xct = sbl("p1xc", [128, 16 * 512], F32)
            ubt = sbl("p1u", [128, 16 * 512], BF16)
            xdb = sbl("p1xdb", [128, 512], F32)
            wip = sbl("p1wip", [128, 8 * 4096], BF16)
            wxp = sbl("p1wxp", [128, 16 * 128], BF16)

            t8 = lambda buf, i: buf[:, i * 512:(i + 1) * 512]

            self.d_wt.inc(G.dma_start(
                out=wip.rearrange("p (a n) -> p a n", a=8),
                in_=self.inp_t.rearrange("(a p) n -> p a n", p=128)), 16)
            self.d_wt.inc(G.dma_start(
                out=wxp.rearrange("p (a n) -> p a n", a=16),
                in_=self.xproj_t.rearrange("(a p) n -> p a n", p=128)), 16)
            wready = (self.d_wt, self.d_wt.n)

            ldsem = [self.dG1, self.dG2]
            xb_ready = [(dS, 0), (dS, 0)]
            xb_free = [(ve, 0), (ve, 0)]
            sq_free = (pe, 0)
            xnt_free = (ac, 0)
            xct_free = (ac, 0)
            xm_conv_done = (ve, 0)

            for c in range(NCH):
                pb = c % 2
                w(G, xb_free[pb])
                ldsem[pb].inc(G.dma_start(
                    out=xb[pb].rearrange("p (a n) -> p a n", a=8),
                    in_=self.xfm[:, c * 512:(c + 1) * 512]
                        .rearrange("(a p) n -> p a n", p=128)), 16)
                xb_ready[pb] = (ldsem[pb], ldsem[pb].n)

                w(A, xb_ready[pb], sq_free, self.crd)
                for i in range(8):
                    ac.inc(A.activation(t8(sq, i), t8(xb[pb], i), AF.Square,
                                        bias=self.ck(0.0), scale=self.ck(1.0)))
                self.bank_wait(T, "g2")
                self.bank_wait(T, "g3")
                w(T, xb_ready[pb], (ac, ac.n), self.cr)
                for i in range(8):
                    T.matmul(g2[0:1, :], self.c_onec[:], t8(xb[pb], i),
                             start=(i == 0), stop=(i == 7))
                for i in range(8):
                    mm = T.matmul(g3[0:1, :], self.c_onec[:], t8(sq, i),
                                  start=(i == 0), stop=(i == 7))
                pe.inc(mm)
                sq_free = (pe, pe.n)
                w(V, (pe, pe.n), (ac, ac.n))
                ve.inc(V.tensor_scalar_mul(rmu[:], g2[0:1, :], 1.0 / 1024.0))
                ve.inc(V.tensor_scalar_mul(rva[:], g3[0:1, :], 1.0 / 1024.0))
                w(V, (ve, ve.n))
                ve.inc(V.tensor_mul(rtm[:], rmu[:], rmu[:]))
                w(V, (ve, ve.n))
                ve.inc(V.tensor_sub(rva[:], rva[:], rtm[:]))
                self.bank_used("g2", (ve, ve.n))
                self.bank_used("g3", (ve, ve.n))
                w(A, (ve, ve.n))
                ac.inc(A.activation(rva[:], rva[:], AF.Ln,
                                    bias=self.ck(1e-5, True), scale=self.ck(1.0, True)))
                w(A, (ac, ac.n))
                ac.inc(A.activation(rva[:], rva[:], AF.Exp,
                                    bias=self.ck(0.0, True), scale=self.ck(-0.5, True)))
                w(V, (ac, ac.n))
                ve.inc(V.tensor_copy(rmub[:], rmu[:]))
                ve.inc(V.tensor_copy(rvab[:], rva[:]))
                self.bank_wait(T, "g2")
                self.bank_wait(T, "g3")
                w(T, (ve, ve.n))
                m1 = T.matmul(g2[:], self.c_ones[:], rmub[:], start=True, stop=True)
                m2 = T.matmul(g3[:], self.c_ones[:], rvab[:], start=True, stop=True)
                pe.inc(m2)
                w(V, (pe, pe.n), (self.d_xn, self.d_xn.n), xnt_free)
                for i in range(8):
                    ve.inc(V.tensor_sub(t8(xnt, i), t8(xb[pb], i), g2[:]))
                w(V, (ve, ve.n))
                for i in range(8):
                    ve.inc(V.tensor_mul(t8(xnt, i), t8(xnt, i), g3[:]))
                xb_free[pb] = (ve, ve.n)
                self.bank_used("g2", (ve, ve.n))
                self.bank_used("g3", (ve, ve.n))
                xn_done = (ve, ve.n)
                w(A, xn_done)
                self.d_xn.inc(A.dma_start(
                    out=self.xn_d[:, c * 512:(c + 1) * 512]
                        .rearrange("(a p) n -> p a n", p=128),
                    in_=xnt.rearrange("p (a n) -> p a n", a=8)), 16)

                if c < 4:
                    continue

                w(A, xn_done, sq_free)
                for i in range(8):
                    ac.inc(A.activation(t8(sq, i), t8(xnt, i), AF.Square,
                                        bias=self.ck(0.0), scale=self.ck(1.0)))
                xnt_free = (ac, ac.n)
                self.bank_wait(T, "g2")
                w(T, (ac, ac.n))
                for i in range(8):
                    mm = T.matmul(g2[0:1, :], self.c_onec[:], t8(sq, i),
                                  start=(i == 0), stop=(i == 7))
                pe.inc(mm)
                sq_free = (pe, pe.n)
                w(V, (pe, pe.n))
                ve.inc(V.tensor_scalar_max(rtm[:], g2[0:1, :], 1e-24))
                self.bank_used("g2", (ve, ve.n))
                w(A, (ve, ve.n))
                ac.inc(A.activation(rtm[:], rtm[:], AF.Ln,
                                    bias=self.ck(0.0, True), scale=self.ck(1.0, True)))
                w(A, (ac, ac.n))
                ac.inc(A.activation(rtm[:], rtm[:], AF.Exp,
                                    bias=self.ck(LN32, True), scale=self.ck(-0.5, True)))
                w(V, (ac, ac.n), (pe, pe.n))
                ve.inc(V.tensor_copy(rvab[:], rtm[:]))
                self.bank_wait(T, "g2")
                w(T, (ve, ve.n))
                m1 = T.matmul(g2[:], self.c_ones[:], rvab[:], start=True, stop=True)
                pe.inc(m1)
                w(V, (pe, pe.n), (self.d_xr, self.d_xr.n))
                for i in range(8):
                    ve.inc(V.tensor_mul(t8(xr1t, i), t8(xnt, i), g2[:]))
                self.bank_used("g2", (ve, ve.n))
                xr1_done = (ve, ve.n)
                if c >= 5:
                    w(A, xr1_done)
                    self.d_xr.inc(A.dma_start(
                        out=self.xr1_d[:, (c - 5) * 512:(c - 4) * 512]
                            .rearrange("(a p) n -> p a n", p=128),
                        in_=xr1t.rearrange("p (a n) -> p a n", a=8)), 16)

                w(V, xm_conv_done, xct_free)
                if c >= 5:
                    ve.inc(V.tensor_copy(
                        xmb.rearrange("p (a n) -> p a n", a=16)[:, :, 0:3],
                        xmb.rearrange("p (a n) -> p a n", a=16)[:, :, 512:515]))
                carry_done = (ve, ve.n)

                banks = [g0, g1]
                bnames = ["g0", "g1"]
                w(T, xr1_done, wready)
                for mi in range(32):
                    bk = banks[mi % 2]
                    bn = bnames[mi % 2]
                    self.bank_wait(T, bn)
                    for kk in range(8):
                        mm = T.matmul(bk[:], wip[:, kk * 4096 + mi * 128:
                                                 kk * 4096 + (mi + 1) * 128],
                                      t8(xr1t, kk), start=(kk == 0), stop=(kk == 7))
                    pe.inc(mm)
                    mmdep = (pe, pe.n)
                    if mi < 16:
                        w(V, mmdep, carry_done)
                        ve.inc(V.tensor_copy(
                            xmb[:, mi * 515 + 3:mi * 515 + 515], bk[:]))
                        self.bank_used(bn, (ve, ve.n))
                    else:
                        rb = resb[mi % 2]
                        w(V, mmdep, (self.d_res, self.d_res.n))
                        ve.inc(V.tensor_copy(rb[:], bk[:]))
                        self.bank_used(bn, (ve, ve.n))
                        if c >= 5:
                            w(A, (ve, ve.n))
                            self.d_res.inc(A.dma_start(
                                out=self.res_d[(mi - 16) * 128:(mi - 15) * 128,
                                               (c - 5) * 512:(c - 4) * 512],
                                in_=rb[:]), 16)
                xm_ready = (ve, ve.n)

                if c < 5:
                    xm_conv_done = (ve, ve.n)
                    continue

                w(V, xm_ready, xct_free)
                for j in range(16):
                    base = j * 515
                    ve.inc(V.tensor_scalar(
                        xct[:, j * 512:(j + 1) * 512],
                        xmb[:, base:base + 512],
                        self.c_convw[:, j * 4 + 0:j * 4 + 1], None, ALU.mult))
                w(V, (ve, ve.n))
                for kk in range(1, 4):
                    for j in range(16):
                        base = j * 515
                        ve.inc(V.scalar_tensor_tensor(
                            xct[:, j * 512:(j + 1) * 512],
                            xmb[:, base + kk:base + kk + 512],
                            self.c_convw[:, j * 4 + kk:j * 4 + kk + 1],
                            xct[:, j * 512:(j + 1) * 512], ALU.mult, ALU.add))
                    w(V, (ve, ve.n))
                xm_conv_done = (ve, ve.n)
                w(A, (ve, ve.n), (pe, pe.n), (self.d_u, self.d_u.n))
                for j in range(16):
                    ac.inc(A.activation(ubt[:, j * 512:(j + 1) * 512],
                                        xct[:, j * 512:(j + 1) * 512], AF.Silu,
                                        bias=self.c_convb[:, j:j + 1],
                                        scale=self.ck(1.0)))
                u_done = (ac, ac.n)
                xct_free = (ac, ac.n)
                w(G, u_done)
                self.d_u.inc(G.dma_start(
                    out=self.u_d[:, (c - 5) * 512:(c - 4) * 512]
                        .rearrange("(a p) n -> p a n", p=128),
                    in_=ubt.rearrange("p (a n) -> p a n", a=16)), 16)

                self.bank_wait(T, "g3")
                w(T, u_done, wready)
                for kk in range(16):
                    mm = T.matmul(g3[:], wxp[:, kk * 128:(kk + 1) * 128],
                                  ubt[:, kk * 512:(kk + 1) * 512],
                                  start=(kk == 0), stop=(kk == 15))
                pe.inc(mm)
                w(V, (pe, pe.n), (self.d_dt, self.d_dt.n),
                  (self.d_bc, self.d_bc.n))
                ve.inc(V.tensor_copy(xdb[:], g3[:]))
                self.bank_used("g3", (ve, ve.n))
                w(A, (ve, ve.n))
                self.d_dt.inc(A.dma_start(
                    out=self.dt_own[:, (c - 5) * 512:(c - 4) * 512],
                    in_=xdb[0:64, :]), 16)
                w(G, (ve, ve.n))
                self.d_bc.inc(G.dma_start(
                    out=self.bc_own[:, (c - 5) * 512:(c - 4) * 512],
                    in_=xdb[64:96, :]), 16)
            self.p1_done = ((self.d_xn, self.d_xn.n), (self.d_xr, self.d_xr.n),
                            (self.d_res, self.d_res.n),
                            (self.d_u, self.d_u.n), (self.d_dt, self.d_dt.n),
                            (self.d_bc, self.d_bc.n))

    def ffn_branch(self):
        for lay in (0, 1):
            self.stream_dep = ((self.d_xn, self.d_xn.n) if lay == 0
                               else (self.d_x1, self.d_x1.n))
            self.ffn_passA(lay)
            self.xatt_dep = (self.d_xa, self.d_xa.n)
            self.ffn_passB(lay)
        self.ffn_done = (self.d_g, self.d_g.n)

    def ffn_passA(self, lay):
        nc = self.nc
        T, V, A, G, SP = self.T, self.V, self.A, self.G, self.SP
        w = self.w
        pe, ve, ac = self.pe, self.ve, self.ac
        g0, g1, g2, g3, g4, g5, g6, g7 = (self.g0, self.g1, self.g2, self.g3,
                                          self.g4, self.g5, self.g6, self.g7)
        stream = self.xn_d if lay == 0 else self.x1_d
        base = L1Q0 if lay == 0 else 512
        nqb = 12 if lay == 0 else 10
        tab0 = 0 if lay == 0 else 1024
        w0qb = 4 if lay == 0 else 2
        self.enter_phase()
        est = contextlib.ExitStack()
        sbl = lambda n, sh, dt: est.enter_context(
            nc.sbuf_tensor(f"{n}_{lay}", sh, dt))
        with est:
            wqb = sbl("a_wq", [128, 8 * 1536], BF16)
            wob = sbl("a_wo", [128, 4 * 1024], BF16)
            xin = [sbl(f"a_xi{i}", [128, 8 * 512], BF16) for i in range(2)]
            xsh = [sbl(f"a_xs{i}", [128, 4 * 512], BF16) for i in range(2)]
            tbq = sbl("a_tbq", [128, 2 * 512], BF16)
            tbk = sbl("a_tbk", [128, 2 * 512], BF16)
            qkvb = sbl("a_qkv", [128, 12 * 512], BF16)
            sq2 = sbl("a_sq2", [128, 512], BF16)
            rr2 = sbl("a_rr2", [2, 512], F32)
            rr2b = sbl("a_rr2b", [2, 512], BF16)
            qn = sbl("a_qn", [128, 512], BF16)
            t1b = sbl("a_t1", [128, 512], BF16)
            t2b = sbl("a_t2", [128, 512], BF16)
            qf = [sbl(f"a_qf{i}", [128, 8 * 512], BF16) for i in range(2)]
            vTc = [sbl(f"a_vT{i}", [128, 4 * 512], BF16) for i in range(2)]
            eAb = sbl("a_eA", [128, 512], BF16)
            eBb = sbl("a_eB", [128, 128], BF16)
            aT = sbl("a_aT", [128, 640], BF16)
            rA = sbl("a_rA", [128, 1], F32)
            rB = sbl("a_rB", [128, 1], F32)
            r1s = sbl("a_r1s", [128, 1], F32)
            sA = sbl("a_sA", [128, 1], F32)
            sB = sbl("a_sB", [128, 1], F32)
            osb = sbl("a_osb", [128, 4 * 512], BF16)
            xatt = sbl("a_xatt", [128, 8 * 512], BF16)

            pdone = getattr(self, "p1_done", ())
            w(G, *pdone)
            self.d_wt.inc(G.dma_start(
                out=wqb.rearrange("p (a n) -> p a n", a=8),
                in_=self.wqkv[lay].rearrange("(a p) n -> p a n", p=128)), 16)
            self.d_wt.inc(G.dma_start(
                out=wob.rearrange("p (a n) -> p a n", a=4),
                in_=self.wo_w[lay].rearrange("(a p) n -> p a n", p=128)), 16)
            wrdy = (self.d_wt, self.d_wt.n)

            xin_free = [(pe, 0), (pe, 0)]
            tb_free = (ve, 0)
            qf_free = [(pe, 0), (pe, 0)]
            vT_free = [(pe, 0), (pe, 0)]
            qkvb_free = (ve, 0)
            osb_free = (pe, 0)
            xatt_free = (self.d_xa, 0)
            eA_free = (pe, 0)
            eB_free = (pe, 0)
            aT_free = (pe, 0)
            t8 = lambda b, i: b[:, i * 512:(i + 1) * 512]

            for bi in range(nqb):
                cur = bi % 2
                prv = 1 - cur
                dffs = self.d_ffs if cur == 0 else self.d_ffs2
                sc0 = base + bi * 512
                w(SP, xin_free[cur], self.stream_dep, *pdone)
                dffs.inc(SP.dma_start(
                    out=xin[cur].rearrange("p (a n) -> p a n", a=8),
                    in_=stream[:, sc0:sc0 + 512]
                        .rearrange("(a p) n -> p a n", p=128)), 16)
                dffs.inc(SP.dma_start(
                    out=xsh[cur].rearrange("p (a n) -> p a n", a=4),
                    in_=stream[512:1024, sc0 - 1:sc0 + 511]
                        .rearrange("(a p) n -> p a n", p=128)), 16)
                w(SP, tb_free)
                dffs.inc(SP.dma_start(
                    out=tbq[:, 0:512], in_=self.tqc[:, tab0 + bi * 512:
                                                    tab0 + (bi + 1) * 512]), 16)
                dffs.inc(SP.dma_start(
                    out=tbq[:, 512:1024], in_=self.tqs[:, tab0 + bi * 512:
                                                       tab0 + (bi + 1) * 512]), 16)
                dffs.inc(SP.dma_start(
                    out=tbk[:, 0:512], in_=self.tkc[:, tab0 + bi * 512:
                                                    tab0 + (bi + 1) * 512]), 16)
                dffs.inc(SP.dma_start(
                    out=tbk[:, 512:1024], in_=self.tks[:, tab0 + bi * 512:
                                                       tab0 + (bi + 1) * 512]), 16)
                ldrdy = (dffs, dffs.n)
                banks = [g0, g1]
                bn = ["g0", "g1"]
                w(T, ldrdy, wrdy, qkvb_free)
                for mi in range(12):
                    bk = banks[mi % 2]
                    self.bank_wait(T, bn[mi % 2])
                    for kk in range(8):
                        rhs = t8(xin[cur], kk) if kk < 4 else t8(xsh[cur], kk - 4)
                        mm = T.matmul(bk[:], wqb[:, kk * 1536 + mi * 128:
                                                  kk * 1536 + (mi + 1) * 128],
                                      rhs, start=(kk == 0), stop=(kk == 7))
                    pe.inc(mm)
                    w(V, (pe, pe.n))
                    ve.inc(V.tensor_copy(t8(qkvb, mi), bk[:]))
                    self.bank_used(bn[mi % 2], (ve, ve.n))
                qkv_rdy = (ve, ve.n)
                for i in range(8):
                    w(A, qkv_rdy, (pe, pe.n))
                    ac.inc(A.activation(sq2[:], t8(qkvb, i), AF.Square,
                                        bias=self.ck(0.0), scale=self.ck(1.0)))
                    self.bank_wait(T, "g2")
                    w(T, (ac, ac.n))
                    mm = T.matmul(g2[0:2, :], self.c_ind2[:], sq2[:],
                                  start=True, stop=True)
                    pe.inc(mm)
                    w(V, (pe, pe.n), (ac, ac.n))
                    ve.inc(V.tensor_scalar_max(rr2[:], g2[0:2, :], 1e-24))
                    self.bank_used("g2", (ve, ve.n))
                    w(A, (ve, ve.n))
                    ac.inc(A.activation(rr2[:], rr2[:], AF.Ln,
                                        bias=self.ck(0.0, p=2),
                                        scale=self.ck(1.0, p=2)))
                    w(A, (ac, ac.n))
                    ac.inc(A.activation(rr2[:], rr2[:], AF.Exp,
                                        bias=self.ck(0.0, p=2),
                                        scale=self.ck(-0.5, p=2)))
                    w(V, (ac, ac.n), (pe, pe.n))
                    ve.inc(V.tensor_copy(rr2b[:], rr2[:]))
                    self.bank_wait(T, "g3")
                    w(T, (ve, ve.n))
                    mm = T.matmul(g3[:], self.c_rep2[:], rr2b[:],
                                  start=True, stop=True)
                    pe.inc(mm)
                    w(V, (pe, pe.n), (ac, ac.n))
                    ve.inc(V.tensor_mul(qn[:], t8(qkvb, i), g3[:]))
                    self.bank_used("g3", (ve, ve.n))
                    sc_col = self.c_qsc if i < 4 else self.c_ksc
                    w(V, (ve, ve.n))
                    ve.inc(V.tensor_scalar(qn[:], qn[:],
                                           sc_col[:, lay:lay + 1], None, ALU.mult))
                    self.bank_wait(T, "g2")
                    w(T, (ve, ve.n))
                    mm = T.matmul(g2[:], self.c_rotp[:], qn[:],
                                  start=True, stop=True)
                    pe.inc(mm)
                    tb = tbq if i < 4 else tbk
                    w(V, ldrdy, (ve, ve.n))
                    ve.inc(V.tensor_mul(t1b[:], qn[:], tb[:, 0:512]))
                    w(V, (pe, pe.n), (ve, ve.n))
                    ve.inc(V.tensor_mul(t2b[:], g2[:], tb[:, 512:1024]))
                    self.bank_used("g2", (ve, ve.n))
                    w(V, (ve, ve.n), qf_free[cur] if i == 0 else (ve, 0))
                    ve.inc(V.tensor_add(t8(qf[cur], i), t1b[:], t2b[:]))
                qf_rdy = (ve, ve.n)
                tb_free = (ve, ve.n)
                self.bank_wait(T, "g4")
                w(T, qkv_rdy, vT_free[cur])
                for vt in range(4):
                    for j in range(4):
                        mm = T.transpose(g4[:, j * 128:(j + 1) * 128],
                                         qkvb[:, (8 + vt) * 512 + j * 128:
                                              (8 + vt) * 512 + (j + 1) * 128],
                                         self.c_id[:])
                    pe.inc(mm)
                    w(V, (pe, pe.n))
                    ve.inc(V.tensor_copy(vTc[cur][:, vt * 512:(vt + 1) * 512],
                                         g4[:, 0:512]))
                    self.bank_used("g4", (ve, ve.n))
                    self.bank_wait(T, "g4")
                vT_rdy = (ve, ve.n)
                qkvb_free = (ve, ve.n)

                if bi == 0:
                    xin_free[cur] = (pe, pe.n)
                    qf_free[cur] = (pe, pe.n)
                    vT_free[cur] = (ve, ve.n)
                    continue

                for hp in range(4):
                    for hh in range(2):
                        h = 2 * hp + hh
                        avbank, avbn = (g7, "g7") if hh == 0 else (g3, "g3")
                        self.bank_wait(T, avbn)
                        for qt in range(4):
                            rowsl = slice(hh * 64, hh * 64 + 64)
                            lhsT = qf[cur][rowsl, (h // 2) * 512 + qt * 128:
                                           (h // 2) * 512 + (qt + 1) * 128]
                            self.bank_wait(T, "g5")
                            self.bank_wait(T, "g6")
                            w(T, qf_rdy)
                            for jt in range(5):
                                ja = qt + jt
                                slot = prv if ja < 4 else cur
                                rhs = qf[slot][rowsl,
                                               (4 + h // 2) * 512 + (ja % 4) * 128:
                                               (4 + h // 2) * 512 + (ja % 4 + 1) * 128]
                                if jt < 4:
                                    mm = T.matmul(g5[:, jt * 128:(jt + 1) * 128],
                                                  lhsT, rhs, start=True, stop=True)
                                else:
                                    mm = T.matmul(g6[:, 0:128], lhsT, rhs,
                                                  start=True, stop=True)
                            pe.inc(mm)
                            w(V, (pe, pe.n), (ac, ac.n))
                            ve.inc(V.tensor_add(g5[:, 0:128], g5[:, 0:128],
                                                self.c_triu[:]))
                            ve.inc(V.tensor_add(g6[:, 0:128], g6[:, 0:128],
                                                self.c_tril[:]))
                            if bi == w0qb:
                                cw = (4 - qt) * 128
                                w(V, (ve, ve.n))
                                ve.inc(V.tensor_add(g5[:, 0:cw], g5[:, 0:cw],
                                                    self.c_w0[:, 0:cw]))
                            w(V, (ve, ve.n))
                            ve.inc(V.reduce_max(rA[:], g5[:], AX.X, negate=True))
                            ve.inc(V.reduce_max(rB[:], g6[:, 0:128], AX.X,
                                                negate=True))
                            w(V, (ve, ve.n))
                            ve.inc(V.tensor_tensor(r1s[:], rA[:], rB[:], ALU.min))
                            w(V, (ve, ve.n))
                            ve.inc(V.tensor_scalar_mul(r1s[:], r1s[:], 8.0))
                            w(A, (ve, ve.n), eA_free, eB_free)
                            ac.inc(A.activation(eAb[:], g5[:], AF.Exp,
                                                bias=r1s[:, 0:1],
                                                scale=self.ck(8.0),
                                                accum_out=sA[:, 0:1]))
                            ac.inc(A.activation(eBb[:], g6[:, 0:128], AF.Exp,
                                                bias=r1s[:, 0:1],
                                                scale=self.ck(8.0),
                                                accum_out=sB[:, 0:1]))
                            self.bank_used("g5", (ac, ac.n))
                            self.bank_used("g6", (ac, ac.n))
                            w(V, (ac, ac.n))
                            ve.inc(V.tensor_add(sA[:], sA[:], sB[:]))
                            w(V, (ve, ve.n))
                            ve.inc(V.reciprocal(sB[:], sA[:]))
                            w(V, (ve, ve.n))
                            ve.inc(V.tensor_scalar(eAb[:], eAb[:], sB[:, 0:1],
                                                   None, ALU.mult))
                            ve.inc(V.tensor_scalar(eBb[:], eBb[:], sB[:, 0:1],
                                                   None, ALU.mult))
                            e_rdy = (ve, ve.n)
                            self.bank_wait(T, "g4")
                            w(T, e_rdy, aT_free)
                            for jt in range(4):
                                mm = T.transpose(g4[:, jt * 128:(jt + 1) * 128],
                                                 eAb[:, jt * 128:(jt + 1) * 128],
                                                 self.c_id[:])
                            mm = T.transpose(g4[:, 512:640], eBb[:], self.c_id[:])
                            pe.inc(mm)
                            eA_free = (pe, pe.n)
                            eB_free = (pe, pe.n)
                            w(V, (pe, pe.n))
                            ve.inc(V.tensor_copy(aT[:, 0:640], g4[:, 0:640]))
                            self.bank_used("g4", (ve, ve.n))
                            w(T, (ve, ve.n), vT_rdy)
                            for jt in range(5):
                                ja = qt + jt
                                slot = prv if ja < 4 else cur
                                lhsT = vTc[slot][:, (h // 2) * 512 +
                                                 (ja % 4) * 128 + hh * 64:
                                                 (h // 2) * 512 +
                                                 (ja % 4) * 128 + hh * 64 + 64]
                                mm = T.matmul(avbank[0:64, qt * 128:(qt + 1) * 128],
                                              lhsT, aT[:, jt * 128:(jt + 1) * 128],
                                              start=(jt == 0), stop=(jt == 4))
                            pe.inc(mm)
                            aT_free = (pe, pe.n)
                    w(V, (pe, pe.n), osb_free)
                    ve.inc(V.tensor_copy(osb[0:64, hp * 512:(hp + 1) * 512],
                                         g7[0:64, :]))
                    ve.inc(V.tensor_copy(osb[64:128, hp * 512:(hp + 1) * 512],
                                         g3[0:64, :]))
                    self.bank_used("g7", (ve, ve.n))
                    self.bank_used("g3", (ve, ve.n))
                o_rdy = (ve, ve.n)
                qf_free[prv] = (pe, pe.n)
                vT_free[prv] = (pe, pe.n)
                qf_free[cur] = (pe, pe.n)
                vT_free[cur] = (pe, pe.n)
                w(T, o_rdy)
                for mi in range(8):
                    bk = banks[mi % 2]
                    self.bank_wait(T, bn[mi % 2])
                    for kk in range(4):
                        mm = T.matmul(bk[:], wob[:, kk * 1024 + mi * 128:
                                                  kk * 1024 + (mi + 1) * 128],
                                      t8(osb, kk), start=(kk == 0), stop=(kk == 3))
                    pe.inc(mm)
                    w(V, (pe, pe.n), xatt_free)
                    ve.inc(V.tensor_add(t8(xatt, mi), bk[:], t8(xin[cur], mi)))
                    self.bank_used(bn[mi % 2], (ve, ve.n))
                osb_free = (pe, pe.n)
                xin_free[cur] = (ve, ve.n)
                xin_free[prv] = (pe, pe.n)
                w(A, (ve, ve.n))
                self.d_xa.inc(A.dma_start(
                    out=self.xatt_d[:, (bi - 1) * 512:bi * 512]
                        .rearrange("(a p) n -> p a n", p=128),
                    in_=xatt.rearrange("p (a n) -> p a n", a=8)), 16)
                xatt_free = (self.d_xa, self.d_xa.n)

    def ffn_passB(self, lay):
        nc = self.nc
        T, V, A, G, SP = self.T, self.V, self.A, self.G, self.SP
        w = self.w
        pe, ve, ac = self.pe, self.ve, self.ac
        g0, g1, g2, g3 = self.g0, self.g1, self.g2, self.g3
        nfb = 11 if lay == 0 else 9
        self.enter_phase()
        est = contextlib.ExitStack()
        sbl = lambda n, sh, dt: est.enter_context(
            nc.sbuf_tensor(f"{n}_{lay}", sh, dt))
        with est:
            w1ab = sbl("b_w1a", [128, 8 * HID2], BF16)
            w1gb = sbl("b_w1g", [128, 8 * HID2], BF16)
            w2pb = sbl("b_w2p", [128, 22 * 1024], BF16)
            xat = sbl("b_xat", [128, 8 * 512], BF16)
            xshb = sbl("b_xsh", [128, 4 * 512], BF16)
            sqf = sbl("b_sqf", [128, 8 * 512], BF16)
            rmu2 = sbl("b_rmu", [1, 512], F32)
            rva2 = sbl("b_rva", [1, 512], F32)
            rtm2 = sbl("b_rtm", [1, 512], F32)
            rmub2 = sbl("b_rmub", [1, 512], BF16)
            rvab2 = sbl("b_rvab", [1, 512], BF16)
            xh = sbl("b_xh", [128, 8 * 512], BF16)
            ab = sbl("b_ab", [128, 22 * 512], BF16)
            ggt = sbl("b_ggt", [128, 512], BF16)
            x2b = sbl("b_x2", [128, 8 * 512], BF16)
            t8 = lambda b, i: b[:, i * 512:(i + 1) * 512]

            w(G, (self.d_xa, self.d_xa.n))
            self.d_wt.inc(G.dma_start(
                out=w1ab.rearrange("p (a n) -> p a n", a=8),
                in_=self.w1av[lay].rearrange("(a p) n -> p a n", p=128)), 16)
            self.d_wt.inc(G.dma_start(
                out=w1gb.rearrange("p (a n) -> p a n", a=8),
                in_=self.w1gv[lay].rearrange("(a p) n -> p a n", p=128)), 16)
            self.d_wt.inc(G.dma_start(
                out=w2pb.rearrange("p (a n) -> p a n", a=22),
                in_=self.w2pv[lay].rearrange("(a p) n -> p a n", p=128)), 16)
            wrdy = (self.d_wt, self.d_wt.n)

            ld_free = (pe, 0)
            ab_free = (pe, 0)
            x2_free = (self.d_x1, 0)
            xh_free = (pe, 0)
            for fb in range(nfb):
                dffs = self.d_ffs if fb % 2 == 0 else self.d_ffs2
                c0 = fb * 512
                csh = max(c0 - 1, 0)
                w(SP, ld_free, self.xatt_dep)
                dffs.inc(SP.dma_start(
                    out=xat.rearrange("p (a n) -> p a n", a=8),
                    in_=self.xatt_d[:, c0:c0 + 512]
                        .rearrange("(a p) n -> p a n", p=128)), 16)
                dffs.inc(SP.dma_start(
                    out=xshb.rearrange("p (a n) -> p a n", a=4),
                    in_=self.xatt_d[512:1024, csh:csh + 512]
                        .rearrange("(a p) n -> p a n", p=128)), 16)
                ldrdy = (dffs, dffs.n)
                hhs = lambda i: t8(xat, i) if i < 4 else t8(xshb, i - 4)
                w(A, ldrdy, (pe, pe.n))
                for i in range(8):
                    ac.inc(A.activation(t8(sqf, i), hhs(i), AF.Square,
                                        bias=self.ck(0.0), scale=self.ck(1.0)))
                self.bank_wait(T, "g2")
                self.bank_wait(T, "g3")
                w(T, ldrdy, (ac, ac.n))
                for i in range(8):
                    T.matmul(g2[0:1, :], self.c_onec[:], hhs(i),
                             start=(i == 0), stop=(i == 7))
                for i in range(8):
                    mm = T.matmul(g3[0:1, :], self.c_onec[:], t8(sqf, i),
                                  start=(i == 0), stop=(i == 7))
                pe.inc(mm)
                w(V, (pe, pe.n), (ac, ac.n))
                ve.inc(V.tensor_scalar_mul(rmu2[:], g2[0:1, :], 1.0 / 1024.0))
                ve.inc(V.tensor_scalar_mul(rva2[:], g3[0:1, :], 1.0 / 1024.0))
                w(V, (ve, ve.n))
                ve.inc(V.tensor_mul(rtm2[:], rmu2[:], rmu2[:]))
                w(V, (ve, ve.n))
                ve.inc(V.tensor_sub(rva2[:], rva2[:], rtm2[:]))
                w(A, (ve, ve.n))
                ac.inc(A.activation(rva2[:], rva2[:], AF.Ln,
                                    bias=self.ck(1e-5, True),
                                    scale=self.ck(1.0, True)))
                w(A, (ac, ac.n))
                ac.inc(A.activation(rva2[:], rva2[:], AF.Exp,
                                    bias=self.ck(0.0, True),
                                    scale=self.ck(-0.5, True)))
                w(V, (ac, ac.n), (pe, pe.n))
                ve.inc(V.tensor_copy(rmub2[:], rmu2[:]))
                ve.inc(V.tensor_copy(rvab2[:], rva2[:]))
                w(T, (ve, ve.n))
                T.matmul(g2[:], self.c_ones[:], rmub2[:], start=True, stop=True)
                mm = T.matmul(g3[:], self.c_ones[:], rvab2[:],
                              start=True, stop=True)
                pe.inc(mm)
                w(V, (pe, pe.n), xh_free)
                for i in range(8):
                    ve.inc(V.tensor_sub(t8(xh, i), hhs(i), g2[:]))
                w(V, (ve, ve.n))
                for i in range(8):
                    ve.inc(V.tensor_mul(t8(xh, i), t8(xh, i), g3[:]))
                w(V, (ve, ve.n))
                for i in range(8):
                    ve.inc(V.tensor_scalar(
                        t8(xh, i), t8(xh, i), self.c_lng[:, lay * 8 + i:
                                                         lay * 8 + i + 1],
                        self.c_lnb[:, lay * 8 + i:lay * 8 + i + 1],
                        ALU.mult, ALU.add))
                self.bank_used("g2", (ve, ve.n))
                self.bank_used("g3", (ve, ve.n))
                xh_rdy = (ve, ve.n)
                banks = [g0, g1]
                bn = ["g0", "g1"]
                w(T, xh_rdy, wrdy, ab_free)
                for hc in range(22):
                    bk = banks[hc % 2]
                    self.bank_wait(T, bn[hc % 2])
                    for kk in range(8):
                        mm = T.matmul(bk[:], w1ab[:, kk * HID2 + hc * 128:
                                                  kk * HID2 + (hc + 1) * 128],
                                      t8(xh, kk), start=(kk == 0), stop=(kk == 7))
                    pe.inc(mm)
                    w(V, (pe, pe.n))
                    ve.inc(V.tensor_scalar(
                        ab[:, hc * 512:(hc + 1) * 512], bk[:],
                        self.c_b1a[:, lay * 22 + hc:lay * 22 + hc + 1],
                        None, ALU.add))
                    self.bank_used(bn[hc % 2], (ve, ve.n))
                for hc in range(22):
                    bk = banks[hc % 2]
                    self.bank_wait(T, bn[hc % 2])
                    for kk in range(8):
                        mm = T.matmul(bk[:], w1gb[:, kk * HID2 + hc * 128:
                                                  kk * HID2 + (hc + 1) * 128],
                                      t8(xh, kk), start=(kk == 0), stop=(kk == 7))
                    pe.inc(mm)
                    w(A, (pe, pe.n), (ve, ve.n))
                    ac.inc(A.activation(
                        ggt[:], bk[:], AF.Gelu,
                        bias=self.c_b1g[:, lay * 22 + hc:lay * 22 + hc + 1],
                        scale=self.ck(1.0)))
                    self.bank_used(bn[hc % 2], (ac, ac.n))
                    w(V, (ac, ac.n))
                    ve.inc(V.tensor_mul(ab[:, hc * 512:(hc + 1) * 512],
                                        ab[:, hc * 512:(hc + 1) * 512], ggt[:]))
                hf_rdy = (ve, ve.n)
                w(T, hf_rdy, x2_free)
                for mi in range(8):
                    bk = banks[mi % 2]
                    self.bank_wait(T, bn[mi % 2])
                    for kk in range(22):
                        mm = T.matmul(bk[:], w2pb[:, kk * 1024 + mi * 128:
                                                  kk * 1024 + (mi + 1) * 128],
                                      ab[:, kk * 512:(kk + 1) * 512],
                                      start=(kk == 0), stop=(kk == 21))
                    pe.inc(mm)
                    w(V, (pe, pe.n))
                    ve.inc(V.scalar_tensor_tensor(
                        t8(x2b, mi), bk[:],
                        self.c_fb2[:, lay * 8 + mi:lay * 8 + mi + 1],
                        t8(xat, mi), ALU.add, ALU.add))
                    self.bank_used(bn[mi % 2], (ve, ve.n))
                ab_free = (pe, pe.n)
                xh_free = (pe, pe.n)
                ld_free = (ve, ve.n)
                if lay == 0:
                    if fb < 3:
                        w(V, (ve, ve.n))
                        for i in range(8):
                            ve.inc(V.tensor_scalar(
                                t8(x2b, i), t8(x2b, i),
                                self.c_x1z[:, 0:1], None, ALU.mult))
                    w(A, (ve, ve.n))
                    self.d_x1.inc(A.dma_start(
                        out=self.x1_d[:, c0:c0 + 512]
                            .rearrange("(a p) n -> p a n", p=128),
                        in_=x2b.rearrange("p (a n) -> p a n", a=8)), 16)
                    x2_free = (self.d_x1, self.d_x1.n)
                else:
                    w(A, (ve, ve.n))
                    for i in range(8):
                        ac.inc(A.activation(t8(x2b, i), t8(x2b, i), AF.Silu,
                                            bias=self.ck(0.0),
                                            scale=self.ck(1.0)))
                    if fb >= 1:
                        w(A, (ac, ac.n))
                        self.d_g.inc(A.dma_start(
                            out=self.g_dd[:, (fb - 1) * 512:fb * 512]
                                .rearrange("(a p) n -> p a n", p=128),
                            in_=x2b.rearrange("p (a n) -> p a n", a=8)), 16)
                        x2_free = (self.d_g, self.d_g.n)
                    else:
                        x2_free = (ac, ac.n)

    def ex1(self):
        G, w = self.G, self.w
        w(G, (self.d_u, self.d_u.n), (self.d_dt, self.d_dt.n),
          (self.d_bc, self.d_bc.n), (self.dS, self.dS.n))
        with G.register("r0") as r0:
            G.reg_load(r0, self.c_par[0:1, 0:1])
            G.reg_mul(r0, r0, -1024 * TOK)
            G.reg_add(r0, r0, 1024 * TOK)
            srcap = bass.AP(self.u_d.tensor, r0, [[TOK, 1024], [1, TOK]])
            self.dZg.inc(G.dma_start(out=self.uoth_c, in_=srcap), 16)
        w(G, (self.dZg, self.dZg.n))
        for cin, cout in ((self.uoth_c, self.uoth_g),
                          (self.dt_own, self.dt_gt),
                          (self.bc_own, self.bc_gt)):
            self.cC.inc(G.collective_compute(
                "AllGather", ALU.bypass,
                replica_groups=[[0, 1], [2, 3], [4, 5], [6, 7]],
                ins=[cin.opt()], outs=[cout.opt()]))

    def assemble1(self):
        SP, w = self.SP, self.w
        w(SP, (self.cC, self.cC.n))
        with SP.register("rp") as rp, SP.register("rt") as rt, \
                SP.register("rs") as rs:
            SP.reg_load(rp, self.c_par[0:1, 0:1])
            SP.reg_mul(rt, rp, 1024 * TOK)
            SP.reg_mul(rs, rp, TOK)
            self.dZs.inc(SP.dma_start(
                out=bass.AP(self.u_sc.tensor, rs, [[8192, 1024], [1, TOK]]),
                in_=bass.AP(self.u_d.tensor, rt, [[TOK, 1024], [1, TOK]])), 16)
            SP.reg_mul(rt, rp, -1024 * TOK)
            SP.reg_add(rt, rt, 1024 * TOK)
            SP.reg_mul(rs, rp, -TOK)
            SP.reg_add(rs, rs, TOK)
            self.dZs.inc(SP.dma_start(
                out=bass.AP(self.u_sc.tensor, rs, [[8192, 1024], [1, TOK]]),
                in_=bass.AP(self.uoth_g.tensor, rt, [[TOK, 1024], [1, TOK]])), 16)
            SP.reg_mul(rs, rp, TOK)
            self.dZs.inc(SP.dma_start(
                out=bass.AP(self.dt_sc.tensor, rs, [[8192, DTR], [1, TOK]]),
                in_=self.dt_own), 16)
            SP.reg_mul(rt, rp, -DTR * TOK)
            SP.reg_add(rt, rt, DTR * TOK)
            SP.reg_mul(rs, rp, -TOK)
            SP.reg_add(rs, rs, TOK)
            self.dZs.inc(SP.dma_start(
                out=bass.AP(self.dt_sc.tensor, rs, [[8192, DTR], [1, TOK]]),
                in_=bass.AP(self.dt_gt.tensor, rt, [[TOK, DTR], [1, TOK]])), 16)
            SP.reg_mul(rs, rp, TOK)
            self.dZs.inc(SP.dma_start(
                out=bass.AP(self.bc_sc.tensor, rs, [[8192, 32], [1, TOK]]),
                in_=self.bc_own), 16)
            SP.reg_mul(rt, rp, -32 * TOK)
            SP.reg_add(rt, rt, 32 * TOK)
            SP.reg_mul(rs, rp, -TOK)
            SP.reg_add(rs, rs, TOK)
            self.dZs.inc(SP.dma_start(
                out=bass.AP(self.bc_sc.tensor, rs, [[8192, 32], [1, TOK]]),
                in_=bass.AP(self.bc_gt.tensor, rt, [[TOK, 32], [1, TOK]])), 16)
        self.scan_in_ready = (self.dZs, self.dZs.n)

    def scan_phase(self):
        nc = self.nc
        T, V, A, G, SP = self.T, self.V, self.A, self.G, self.SP
        w = self.w
        pe, ve, ac, po = self.pe, self.ve, self.ac, self.po
        g0, g1 = self.g0, self.g1
        self.enter_phase()
        est = contextlib.ExitStack()
        sbl = lambda n, sh, dt: est.enter_context(nc.sbuf_tensor(n, sh, dt))
        with est:
            wdt = sbl("s_wdt", [DTR, 1024], BF16)
            brep = [sbl(f"s_brep{i}", [128, 16 * 1024], BF16) for i in range(2)]
            crep = [sbl(f"s_crep{i}", [128, 16 * 1024], BF16) for i in range(2)]
            sdt = [sbl(f"s_sdt{i}", [DTR, 1024], F32) for i in range(2)]
            sdtb = [sbl(f"s_sdtb{i}", [DTR, 1024], BF16) for i in range(2)]
            sdel = sbl("s_sdel", [128, 1024], F32)
            t_a = sbl("s_ta", [128, 1024], F32)
            t_b = sbl("s_tb", [128, 1024], F32)
            sdu = sbl("s_sdu", [128, 1024], BF16)
            sda = [sbl(f"s_sda{i}", [128, 1024], F32) for i in range(2)]
            sdb = [sbl(f"s_sdb{i}", [128, 1024], BF16) for i in range(2)]
            sH = [sbl(f"s_sH{i}", [128, 1024], BF16) for i in range(2)]
            ytm = [sbl(f"s_ytm{i}", [128, 1024], BF16) for i in range(2)]
            ypar = sbl("s_ypar", [128, 10 * 1024], BF16)
            ug = [sbl(f"s_ug{i}", [128, 1024], BF16) for i in range(2)]
            Hc = sbl("s_Hc", [128, 128], F32)

            rdy = self.scan_in_ready if hasattr(self, "scan_in_ready") else (self.dZs, 0)
            w(V, rdy)
            ve.inc(V.memset(Hc[:], 0.0))
            w(G, rdy)
            self.d_wt.inc(G.dma_start(out=wdt[:], in_=self.dtproj_t), 16)
            wdt_ready = (self.d_wt, self.d_wt.n)

            bc_free = [[(po, 0), (ve, 0)], [(po, 0), (ve, 0)]]
            sdt_free = [(ve, 0), (ve, 0)]
            sdtb_free = [(pe, 0), (pe, 0)]
            ug_free = [(ve, 0), (ve, 0)]
            bc_ready_p = [(self.d_sl, 0), (self.d_tl, 0)]
            sdt_ready_p = [(self.d_sl2, 0), (self.d_mx, 0)]
            ug_ready_p = [(self.d_ug, 0), (self.d_cl, 0)]
            bsem = [self.d_sl, self.d_tl]
            tsem = [self.d_sl2, self.d_mx]
            usem = [self.d_ug, self.d_cl]

            def issue_loads(si):
                sp_ = si % 2
                scols = slice(si * 1024, (si + 1) * 1024)
                w(SP, rdy, sdt_free[sp_], sdtb_free[sp_])
                tsem[sp_].inc(SP.dma_start(out=sdt[sp_][:],
                                           in_=self.dt_sc[:, scols]), 16)
                sdt_ready_p[sp_] = (tsem[sp_], tsem[sp_].n)
                w(SP, *bc_free[sp_])
                for n_ in range(16):
                    bsem[sp_].inc(SP.dma_start(
                        out=brep[sp_][:, n_ * 1024:(n_ + 1) * 1024],
                        in_=self.bc_sc[n_:n_ + 1, scols]
                            .partition_broadcast(128).squeeze(1)), 16)
                    bsem[sp_].inc(SP.dma_start(
                        out=crep[sp_][:, n_ * 1024:(n_ + 1) * 1024],
                        in_=self.bc_sc[16 + n_:17 + n_, scols]
                            .partition_broadcast(128).squeeze(1)), 16)
                bc_ready_p[sp_] = (bsem[sp_], bsem[sp_].n)
                w(SP, ug_free[0])
                usem[0].inc(SP.dma_start(
                    out=ug[0][:], in_=self.u_sc[0:128, scols]), 16)
                ug_ready_p[0] = (usem[0], usem[0].n)

            issue_loads(0)
            sdel_free = [(ac, 0), (po, 0), (ve, 0)]
            sda_free = [(ve, 0), (ve, 0)]
            sdb_free = [(ve, 0), (ve, 0)]
            sH_free = [(ve, 0), (ve, 0)]
            ytm_free = [(po, 0), (po, 0)]
            ypar_free = (ve, 0)
            yg_free = (self.d_y, 0)

            for s in range(8):
                sp = s % 2
                cols = slice(s * 1024, (s + 1) * 1024)
                bre, cre = brep[sp], crep[sp]
                bc_ready = bc_ready_p[sp]
                w(V, sdt_ready_p[sp], sdtb_free[sp])
                ve.inc(V.tensor_copy(sdtb[sp][:], sdt[sp][:]))
                sdt_free[sp] = (ve, ve.n)
                sdtb_ready = (ve, ve.n)
                if s + 1 < 8:
                    issue_loads(s + 1)

                for g in range(8):
                    self.bank_wait(T, "g0")
                    self.bank_wait(T, "g1")
                    w(T, sdtb_ready, wdt_ready)
                    m1 = T.matmul(g0[:], wdt[:, g * 128:(g + 1) * 128],
                                  sdtb[sp][:, 0:512], start=True, stop=True)
                    m2 = T.matmul(g1[:], wdt[:, g * 128:(g + 1) * 128],
                                  sdtb[sp][:, 512:1024], start=True, stop=True)
                    pe.inc(m2)
                    w(V, (pe, pe.n), *sdel_free)
                    ve.inc(V.tensor_scalar(sdel[:, 0:512], g0[:],
                                           self.c_dtb[:, g:g + 1], None, ALU.add))
                    ve.inc(V.tensor_scalar(sdel[:, 512:1024], g1[:],
                                           self.c_dtb[:, g:g + 1], None, ALU.add))
                    self.bank_used("g0", (ve, ve.n))
                    self.bank_used("g1", (ve, ve.n))
                    w(A, (ve, ve.n), (ac, ac.n))
                    ac.inc(A.activation(t_a[:], sdel[:], AF.Abs,
                                        bias=self.ck(0.0), scale=self.ck(1.0)))
                    w(A, (ac, ac.n))
                    ac.inc(A.activation(t_b[:], t_a[:], AF.Exp,
                                        bias=self.ck(0.0), scale=self.ck(-1.0)))
                    w(A, (ac, ac.n))
                    ac.inc(A.activation(t_b[:], t_b[:], AF.Ln,
                                        bias=self.ck(1.0), scale=self.ck(1.0)))
                    w(V, (ac, ac.n))
                    ve.inc(V.tensor_scalar(t_a[:], sdel[:], 0.0, None, ALU.max))
                    w(V, (ve, ve.n))
                    ve.inc(V.tensor_add(sdel[:], t_a[:], t_b[:]))
                    sdel_ready = (ve, ve.n)
                    gp = g % 2
                    if g + 1 < 8:
                        gq = (g + 1) % 2
                        w(SP, ug_free[gq])
                        usem[gq].inc(SP.dma_start(
                            out=ug[gq][:],
                            in_=self.u_sc[(g + 1) * 128:(g + 2) * 128, cols]), 16)
                        ug_ready_p[gq] = (usem[gq], usem[gq].n)
                    w(V, ug_ready_p[gp], sdel_ready, (po, po.n))
                    ve.inc(V.tensor_mul(sdu[:], sdel[:], ug[gp][:]))
                    ug_free[gp] = (ve, ve.n)
                    sdu_ready = (ve, ve.n)

                    for n in range(16):
                        pa = n % 2
                        col = g * 16 + n
                        w(A, sdel_ready, sda_free[pa])
                        ac.inc(A.activation(sda[pa][:], sdel[:], AF.Exp,
                                            bias=self.ck(0.0),
                                            scale=self.c_A[:, col:col + 1]))
                        w(G, sdu_ready, bc_ready, sdb_free[pa])
                        po.inc(G.tensor_mul(sdb[pa][:], sdu[:],
                                            bre[:, n * 1024:(n + 1) * 1024]))
                        w(V, (ac, ac.n), (po, po.n), sH_free[pa])
                        ve.inc(V.tensor_tensor_scan(
                            sH[pa][:], sda[pa][:], sdb[pa][:],
                            Hc[:, col:col + 1], ALU.mult, ALU.add))
                        sda_free[pa] = (ve, ve.n)
                        sdb_free[pa] = (ve, ve.n)
                        w(V, (ve, ve.n))
                        ve.inc(V.tensor_copy(Hc[:, col:col + 1],
                                             sH[pa][:, 1023:1024]))
                        w(V, ytm_free[pa])
                        ve.inc(V.tensor_mul(ytm[pa][:], sH[pa][:],
                                            cre[:, n * 1024:(n + 1) * 1024]))
                        sH_free[pa] = (ve, ve.n)
                        if pa == 1:
                            w(G, (ve, ve.n), ypar_free)
                            po.inc(G.tensor_add(
                                ypar[:, (n // 2) * 1024:(n // 2 + 1) * 1024],
                                ytm[0][:], ytm[1][:]))
                            ytm_free = [(po, po.n), (po, po.n)]
                    w(V, (po, po.n), yg_free, *ytm_free)
                    lvl1 = [ypar[:, 8 * 1024:9 * 1024], ypar[:, 9 * 1024:10 * 1024],
                            ytm[0][:], ytm[1][:]]
                    for i2 in range(4):
                        ve.inc(V.tensor_add(
                            lvl1[i2],
                            ypar[:, 2 * i2 * 1024:(2 * i2 + 1) * 1024],
                            ypar[:, (2 * i2 + 1) * 1024:(2 * i2 + 2) * 1024]))
                    w(V, (ve, ve.n))
                    ve.inc(V.tensor_add(ypar[:, 0:1024], lvl1[0], lvl1[1]))
                    ve.inc(V.tensor_add(ypar[:, 1024:2048], lvl1[2], lvl1[3]))
                    w(V, (ve, ve.n))
                    ve.inc(V.tensor_add(ypar[:, 8 * 1024:9 * 1024],
                                        ypar[:, 0:1024], ypar[:, 1024:2048]))
                    ytm_free = [(ve, ve.n), (ve, ve.n)]
                    ypar_free = (ve, ve.n)
                    sdel_free = [(ac, ac.n), (ve, ve.n), (po, po.n)]
                    w(A, (ve, ve.n))
                    self.d_y.inc(A.dma_start(
                        out=self.y_dd[g * 128:(g + 1) * 128, cols],
                        in_=ypar[:, 8 * 1024:9 * 1024]), 16)
                    yg_free = (self.d_y, self.d_y.n)
                bc_free[sp] = [(po, po.n), (ve, ve.n)]
                sdtb_free[sp] = (pe, pe.n)
        self.scan_done = (self.d_y, self.d_y.n)

    def ex2(self):
        G, w = self.G, self.w
        w(G, self.scan_done if hasattr(self, "scan_done") else (self.d_y, 0))
        with G.register("r1") as r1:
            G.reg_load(r1, self.c_par[0:1, 0:1])
            G.reg_mul(r1, r1, -TOK)
            G.reg_add(r1, r1, TOK)
            self.dZg.inc(G.dma_start(
                out=self.y_cc,
                in_=bass.AP(self.y_dd.tensor, r1, [[8192, 1024], [1, TOK]])), 16)
        w(G, (self.dZg, self.dZg.n))
        self.cC.inc(G.collective_compute(
            "AllGather", ALU.bypass,
            replica_groups=[[0, 1], [2, 3], [4, 5], [6, 7]],
            ins=[self.y_cc.opt()], outs=[self.y_gg.opt()]))

    def assemble2(self):
        SP, w = self.SP, self.w
        w(SP, (self.cC, self.cC.n))
        with SP.register("rp2") as rp, SP.register("rt2") as rt, \
                SP.register("rs2") as rs:
            SP.reg_load(rp, self.c_par[0:1, 0:1])
            SP.reg_mul(rt, rp, TOK)
            SP.reg_mul(rs, rp, 1024 * TOK)
            self.dZs.inc(SP.dma_start(
                out=bass.AP(self.y_alld.tensor, rs, [[TOK, 1024], [1, TOK]]),
                in_=bass.AP(self.y_dd.tensor, rt, [[8192, 1024], [1, TOK]])), 16)
            SP.reg_mul(rt, rp, -1024 * TOK)
            SP.reg_add(rt, rt, 1024 * TOK)
            self.dZs.inc(SP.dma_start(
                out=bass.AP(self.y_alld.tensor, rt, [[TOK, 1024], [1, TOK]]),
                in_=bass.AP(self.y_gg.tensor, rt, [[TOK, 1024], [1, TOK]])), 16)
        self.y_ready = (self.dZs, self.dZs.n)

    def tail_phase(self):
        nc = self.nc
        T, V, A, G, SP = self.T, self.V, self.A, self.G, self.SP
        w = self.w
        pe, ve, ac = self.pe, self.ve, self.ac
        g0, g1, g2, g3, g5 = self.g0, self.g1, self.g2, self.g3, self.g5
        self.enter_phase()
        est = contextlib.ExitStack()
        sbl = lambda n, sh, dt: est.enter_context(nc.sbuf_tensor(n, sh, dt))
        with est:
            wop = sbl("t_wop", [128, 16 * 1024], BF16)
            wgt = sbl("t_wgt", [128, 8 * 8], BF16)
            ytb = sbl("t_yt", [128, 16 * 512], BF16)
            utb = sbl("t_ut", [128, 16 * 512], BF16)
            rsb = sbl("t_rs", [128, 16 * 512], BF16)
            xrb = sbl("t_xr", [128, 8 * 512], BF16)
            srs = sbl("t_srs", [128, 16 * 512], BF16)
            ytg = sbl("t_ytg", [128, 16 * 512], BF16)
            mot = sbl("t_mot", [128, 8 * 512], F32)
            sqb = sbl("t_sq", [128, 8 * 512], BF16)
            rr = sbl("t_rr", [1, 512], F32)
            rrb = sbl("t_rrb", [1, 512], BF16)
            xmt = sbl("t_xmt", [128, 8 * 512], BF16)
            lgt = sbl("t_lgt", [128, 128], F32)
            ltm = sbl("t_ltm", [128, 32], F32)
            spt = sbl("t_spt", [128, 128], F32)
            spf = sbl("t_spf", [8, 512], F32)
            r1 = sbl("t_r1", [128, 1], F32)
            r2 = sbl("t_r2", [128, 1], F32)
            r3 = sbl("t_r3", [128, 1], F32)
            egt = sbl("t_egt", [128, 8], F32)
            selt = sbl("t_selt", [128, 8], F32)
            mskt = sbl("t_mskt", [128, 8], F32)
            wtt = sbl("t_wtt", [128, 8], F32)

            yrdy = self.y_ready if hasattr(self, "y_ready") else (self.dZs, 0)
            w(G, yrdy, (self.dS, self.dS.n))
            self.d_wt.inc(G.dma_start(
                out=wop.rearrange("p (a n) -> p a n", a=16),
                in_=self.outp_t.rearrange("(a p) n -> p a n", p=128)), 16)
            self.d_wt.inc(G.dma_start(
                out=wgt.rearrange("p (a n) -> p a n", a=8),
                in_=self.gate_t.rearrange("(a p) n -> p a n", p=128)), 16)
            wrdy = (self.d_wt, self.d_wt.n)
            w(V, (self.dS, self.dS.n))
            ve.inc(V.memset(lgt[:], 0.0))
            ve.inc(V.memset(spt[:], 0.0))

            lds_free = [(pe, 0)]
            mot_free = (ac, 0)
            for c in range(8):
                cols = slice(c * 512, (c + 1) * 512)
                w(SP, yrdy, *lds_free)
                self.d_tl.inc(SP.dma_start(
                    out=ytb.rearrange("p (a n) -> p a n", a=16),
                    in_=self.y_alld[:, cols].rearrange("(a p) n -> p a n", p=128)), 16)
                self.d_tl.inc(SP.dma_start(
                    out=utb.rearrange("p (a n) -> p a n", a=16),
                    in_=self.u_d[:, cols].rearrange("(a p) n -> p a n", p=128)), 16)
                self.d_tl.inc(SP.dma_start(
                    out=rsb.rearrange("p (a n) -> p a n", a=16),
                    in_=self.res_d[:, cols].rearrange("(a p) n -> p a n", p=128)), 16)
                self.d_tl.inc(SP.dma_start(
                    out=xrb.rearrange("p (a n) -> p a n", a=8),
                    in_=self.xr1_d[:, cols].rearrange("(a p) n -> p a n", p=128)), 16)
                ldrdy = (self.d_tl, self.d_tl.n)
                t16 = lambda b, i: b[:, i * 512:(i + 1) * 512]
                w(A, ldrdy, (ve, ve.n))
                for j in range(16):
                    ac.inc(A.activation(t16(srs, j), t16(rsb, j), AF.Silu,
                                        bias=self.ck(0.0), scale=self.ck(1.0)))
                w(V, ldrdy, (pe, pe.n))
                for j in range(16):
                    ve.inc(V.scalar_tensor_tensor(
                        t16(ytg, j), t16(utb, j), self.c_dvec[:, j:j + 1],
                        t16(ytb, j), ALU.mult, ALU.add))
                w(V, (ve, ve.n), (ac, ac.n))
                for j in range(16):
                    ve.inc(V.tensor_mul(t16(ytg, j), t16(ytg, j), t16(srs, j)))
                ytg_rdy = (ve, ve.n)
                banks = [g0, g1]
                bn = ["g0", "g1"]
                w(T, ytg_rdy, wrdy)
                for mi in range(8):
                    bk = banks[mi % 2]
                    self.bank_wait(T, bn[mi % 2])
                    for kk in range(16):
                        mm = T.matmul(bk[:], wop[:, kk * 1024 + mi * 128:
                                                 kk * 1024 + (mi + 1) * 128],
                                      t16(ytg, kk), start=(kk == 0), stop=(kk == 15))
                    pe.inc(mm)
                    w(V, (pe, pe.n), mot_free)
                    ve.inc(V.tensor_add(t16(mot, mi), bk[:],
                                        xrb[:, mi * 512:(mi + 1) * 512]))
                    self.bank_used(bn[mi % 2], (ve, ve.n))
                mot_rdy = (ve, ve.n)
                lds_free = [mot_rdy]
                w(A, mot_rdy, (self.d_xm, self.d_xm.n))
                for i in range(8):
                    ac.inc(A.activation(t16(sqb, i), t16(mot, i), AF.Square,
                                        bias=self.ck(0.0), scale=self.ck(1.0)))
                self.bank_wait(T, "g2")
                w(T, (ac, ac.n))
                for i in range(8):
                    mm = T.matmul(g2[0:1, :], self.c_onec[:], t16(sqb, i),
                                  start=(i == 0), stop=(i == 7))
                pe.inc(mm)
                w(V, (pe, pe.n), (ac, ac.n))
                ve.inc(V.tensor_scalar_max(rr[:], g2[0:1, :], 1e-24))
                self.bank_used("g2", (ve, ve.n))
                w(A, (ve, ve.n))
                ac.inc(A.activation(rr[:], rr[:], AF.Ln,
                                    bias=self.ck(0.0, True), scale=self.ck(1.0, True)))
                w(A, (ac, ac.n))
                ac.inc(A.activation(rr[:], rr[:], AF.Exp,
                                    bias=self.ck(LN32, True), scale=self.ck(-0.5, True)))
                w(V, (ac, ac.n), (pe, pe.n))
                ve.inc(V.tensor_copy(rrb[:], rr[:]))
                self.bank_wait(T, "g3")
                w(T, (ve, ve.n))
                mm = T.matmul(g3[:], self.c_ones[:], rrb[:], start=True, stop=True)
                pe.inc(mm)
                w(V, (pe, pe.n), (self.d_xm, self.d_xm.n))
                for i in range(8):
                    ve.inc(V.tensor_mul(t16(xmt, i), t16(mot, i), g3[:]))
                self.bank_used("g3", (ve, ve.n))
                mot_free = (ve, ve.n)
                xmt_rdy = (ve, ve.n)
                w(A, xmt_rdy)
                self.d_xm.inc(A.dma_start(
                    out=self.xm_d[:, cols].rearrange("(a p) n -> p a n", p=128),
                    in_=xmt.rearrange("p (a n) -> p a n", a=8)), 16)
                self.bank_wait(T, "g2")
                w(T, xmt_rdy, wrdy)
                for kk in range(8):
                    mm = T.matmul(g2[0:8, :], wgt[:, kk * 8:(kk + 1) * 8],
                                  t16(xmt, kk), start=(kk == 0), stop=(kk == 7))
                pe.inc(mm)
                for tt in range(4):
                    w(V, (pe, pe.n), (ve, ve.n))
                    ve.inc(V.tensor_copy(lgt[0:8, 0:128],
                                         g2[0:8, tt * 128:(tt + 1) * 128]))
                    if tt == 3:
                        self.bank_used("g2", (ve, ve.n))
                    self.bank_wait(T, "g5")
                    w(T, (ve, ve.n))
                    mm = T.transpose(g5[:, 0:128], lgt[:], self.c_id32[:])
                    pe.inc(mm)
                    w(V, (pe, pe.n))
                    ve.inc(V.tensor_copy(ltm[:, 0:8], g5[:, 0:8]))
                    self.bank_used("g5", (ve, ve.n))
                    lt = ltm[:, 0:8]
                    w(V, (ve, ve.n))
                    ve.inc(V.reduce_max(r1[:], lt, AX.X, negate=True))
                    w(V, (ve, ve.n))
                    ve.inc(V.tensor_scalar_mul(r2[:], r1[:], -1.0))
                    w(A, (ve, ve.n))
                    ac.inc(A.activation(egt[:], lt, AF.Exp,
                                        bias=r1[:, 0:1], scale=self.ck(1.0)))
                    w(V, (ve, ve.n))
                    ve.inc(V.tensor_scalar(selt[:], lt, r2[:, 0:1], None,
                                           ALU.is_ge))
                    w(V, (ve, ve.n))
                    ve.inc(V.scalar_tensor_tensor(mskt[:], selt[:], -1e30,
                                                  lt, ALU.mult, ALU.add))
                    w(V, (ve, ve.n))
                    ve.inc(V.reduce_max(r3[:], mskt[:], AX.X))
                    w(V, (ve, ve.n))
                    ve.inc(V.tensor_scalar(selt[:], lt, r3[:, 0:1], None,
                                           ALU.is_ge))
                    w(V, (ve, ve.n), (ac, ac.n))
                    ve.inc(V.tensor_mul(wtt[:], egt[:], selt[:]))
                    w(V, (ve, ve.n))
                    ve.inc(V.reduce_sum(r3[:], wtt[:], AX.X))
                    w(V, (ve, ve.n))
                    ve.inc(V.reciprocal(r2[:], r3[:]))
                    w(V, (ve, ve.n))
                    ve.inc(V.tensor_scalar(spt[:, 0:8], wtt[:], r2[:, 0:1],
                                           None, ALU.mult))
                    self.bank_wait(T, "g5")
                    w(T, (ve, ve.n))
                    mm = T.transpose(g5[:, 0:128], spt[:], self.c_id32[:])
                    pe.inc(mm)
                    w(V, (pe, pe.n), (self.d_sp, self.d_sp.n))
                    ve.inc(V.tensor_copy(spf[0:8, tt * 128:(tt + 1) * 128],
                                         g5[0:8, 0:128]))
                    self.bank_used("g5", (ve, ve.n))
                w(A, (ve, ve.n))
                self.d_sp.inc(A.dma_start(out=self.sp_d[:, cols], in_=spf[0:8, :]), 16)
        self.tail_done = ((self.d_xm, self.d_xm.n), (self.d_sp, self.d_sp.n))

    def moe_phase(self):
        nc = self.nc
        T, V, A, G, SP = self.T, self.V, self.A, self.G, self.SP
        w = self.w
        pe, ve, ac = self.pe, self.ve, self.ac
        g0, g1, g2, g3 = self.g0, self.g1, self.g2, self.g3
        self.enter_phase()
        est = contextlib.ExitStack()
        sbl = lambda n, sh, dt: est.enter_context(nc.sbuf_tensor(n, sh, dt))
        with est:
            w1b = [sbl(f"m_w1{i}", [128, 8 * 2048], BF16) for i in range(2)]
            w2b = [sbl(f"m_w2{i}", [128, 16 * 1024], BF16) for i in range(2)]
            xmb = [sbl(f"m_xm{i}", [128, 8 * 512], BF16) for i in range(2)]
            hfb = sbl("m_hf", [128, 16 * 512], BF16)
            sprep = sbl("m_sp", [128, TOK], BF16)
            ysb = [sbl(f"m_ysb{i}", [128, 512], BF16) for i in range(2)]

            tdone = self.tail_done if hasattr(self, "tail_done") else ()
            w(G, *tdone)
            wl = [self.dG1, self.dG2]
            wrdy = [(self.dG1, 0), (self.dG2, 0)]
            wfree = [(pe, 0), (pe, 0)]

            def load_w(e, h, slot):
                self.w(G, wfree[slot], *tdone)
                wl[slot].inc(G.dma_start(
                    out=w1b[slot].rearrange("p (a n) -> p a n", a=8),
                    in_=self.mw1[e, :, h * 2048:(h + 1) * 2048]
                        .rearrange("(a p) n -> p a n", p=128)), 16)
                wl[slot].inc(G.dma_start(
                    out=w2b[slot].rearrange("p (a n) -> p a n", a=16),
                    in_=self.mw2[e, h * 2048:(h + 1) * 2048, :]
                        .rearrange("(a p) n -> p a n", p=128)), 16)
                wrdy[slot] = (wl[slot], wl[slot].n)

            load_w(0, 0, 0)
            sprep_rdy = (self.d_sp2, 0)
            sprep_free = (ve, 0)
            xm_free = [(pe, 0), (pe, 0)]
            hf_free = [(pe, 0)] * 16
            ysb_free = [(self.d_ym, 0), (self.d_ym2, 0)]
            accum_guard = [(self.d_ym, 0), (self.d_ym2, 0)]

            step = 0
            for e in range(8):
                w(G, sprep_free, *tdone)
                self.d_sp2.inc(G.dma_start(
                    out=sprep[:],
                    in_=self.sp_d[e:e + 1, :].partition_broadcast(128)
                        .squeeze(1)), 16)
                sprep_rdy = (self.d_sp2, self.d_sp2.n)
                for h in range(2):
                    slot = step % 2
                    nstep = step + 1
                    if nstep < 16:
                        ne, nh = nstep // 2, nstep % 2
                        load_w(ne, nh, nstep % 2)
                    for nt in range(8):
                        xs = nt % 2
                        dmx = self.d_mx if xs == 0 else self.d_mx2
                        w(SP, xm_free[xs], *tdone)
                        dmx.inc(SP.dma_start(
                            out=xmb[xs].rearrange("p (a n) -> p a n", a=8),
                            in_=self.xm_d[:, nt * 512:(nt + 1) * 512]
                                .rearrange("(a p) n -> p a n", p=128)), 16)
                        xmrdy = (dmx, dmx.n)
                        bk = [g0, g1]
                        bn = ["g0", "g1"]
                        w(T, xmrdy, wrdy[slot])
                        for hc in range(16):
                            b_ = bk[hc % 2]
                            self.bank_wait(T, bn[hc % 2])
                            for dc in range(8):
                                mm = T.matmul(
                                    b_[:], w1b[slot][:, dc * 2048 + hc * 128:
                                                     dc * 2048 + (hc + 1) * 128],
                                    xmb[xs][:, dc * 512:(dc + 1) * 512],
                                    start=(dc == 0), stop=(dc == 7))
                            pe.inc(mm)
                            w(A, (pe, pe.n), hf_free[hc])
                            ac.inc(A.activation(
                                hfb[:, hc * 512:(hc + 1) * 512], b_[:], AF.Gelu,
                                bias=self.ck(0.0), scale=self.ck(1.0)))
                            self.bank_used(bn[hc % 2], (ac, ac.n))
                        hf_rdy = (ac, ac.n)
                        bk2 = [g2, g3]
                        bn2 = ["g2", "g3"]
                        w(T, hf_rdy)
                        for dc in range(8):
                            b_ = bk2[dc % 2]
                            self.bank_wait(T, bn2[dc % 2])
                            for hc in range(16):
                                mm = T.matmul(
                                    b_[:], w2b[slot][:, hc * 1024 + dc * 128:
                                                     hc * 1024 + (dc + 1) * 128],
                                    hfb[:, hc * 512:(hc + 1) * 512],
                                    start=(hc == 0), stop=(hc == 15))
                            pe.inc(mm)
                            hf_free = [(pe, pe.n)] * 16
                            ys = dc % 2
                            w(V, (pe, pe.n), sprep_rdy, ysb_free[ys])
                            ve.inc(V.tensor_mul(
                                ysb[ys][:], b_[:],
                                sprep[:, nt * 512:(nt + 1) * 512]))
                            self.bank_used(bn2[dc % 2], (ve, ve.n))
                            dym = self.d_ym if ys == 0 else self.d_ym2
                            w(G, (ve, ve.n), *accum_guard)
                            op = ALU.bypass if step == 0 else ALU.add
                            dym.inc(G.dma_start(
                                out=self.ymoe_d[dc * 128:(dc + 1) * 128,
                                                nt * 512:(nt + 1) * 512],
                                in_=ysb[ys][:], accum_op=op), 16)
                            ysb_free[ys] = (dym, dym.n)
                        xm_free[xs] = (pe, pe.n)
                    wfree[slot] = (pe, pe.n)
                    accum_guard = [(self.d_ym, self.d_ym.n),
                                   (self.d_ym2, self.d_ym2.n)]
                    step += 1
                sprep_free = (ve, ve.n)
        self.moe_done = (self.d_ym, self.d_ym.n)
        self.moe_done2 = (self.d_ym2, self.d_ym2.n)

    def combine_phase(self):
        nc = self.nc
        V, A, SP, G = self.V, self.A, self.SP, self.G
        w = self.w
        ve = self.ve
        self.enter_phase()
        est = contextlib.ExitStack()
        sbl = lambda n, sh, dt: est.enter_context(nc.sbuf_tensor(n, sh, dt))
        with est:
            ymb = sbl("c_ym", [128, 8 * 512], BF16)
            xnb = sbl("c_xn", [128, 8 * 512], BF16)
            ggb = sbl("c_gg", [128, 8 * 512], BF16)
            xxb = sbl("c_xx", [128, 8 * 512], F32)
            sgb = sbl("c_sg", [128, 8 * 512], F32)
            otb = sbl("c_ot", [128, 8 * 512], F32)
            mdone = self.moe_done if hasattr(self, "moe_done") else (self.d_ym, 0)
            mdone2 = getattr(self, "moe_done2", (self.d_ym2, 0))
            ld_free = (ve, 0)
            ot_free = (self.d_out, 0)
            for c in range(8):
                cols = slice(c * 512, (c + 1) * 512)
                xcols = slice(HALO + c * 512, HALO + (c + 1) * 512)
                w(SP, mdone, mdone2, ld_free,
                  self.ffn_done if hasattr(self, "ffn_done")
                  else (self.d_g, 0))
                self.d_cl.inc(SP.dma_start(
                    out=ymb.rearrange("p (a n) -> p a n", a=8),
                    in_=self.ymoe_d[:, cols].rearrange("(a p) n -> p a n", p=128)), 16)
                self.d_cl.inc(SP.dma_start(
                    out=xnb.rearrange("p (a n) -> p a n", a=8),
                    in_=self.xn_d[:, xcols].rearrange("(a p) n -> p a n", p=128)), 16)
                self.d_cl.inc(SP.dma_start(
                    out=ggb.rearrange("p (a n) -> p a n", a=8),
                    in_=self.g_dd[:, cols].rearrange("(a p) n -> p a n", p=128)), 16)
                self.d_cl.inc(SP.dma_start(
                    out=xxb.rearrange("p (a n) -> p a n", a=8),
                    in_=self.xfm[:, xcols].rearrange("(a p) n -> p a n", p=128)), 16)
                ldrdy = (self.d_cl, self.d_cl.n)
                t8 = lambda b, i: b[:, i * 512:(i + 1) * 512]
                w(V, ldrdy, ot_free)
                for i in range(8):
                    ve.inc(V.tensor_add(t8(sgb, i), t8(ymb, i), t8(xnb, i)))
                w(V, (ve, ve.n))
                for i in range(8):
                    ve.inc(V.tensor_mul(t8(sgb, i), t8(sgb, i), t8(ggb, i)))
                w(V, (ve, ve.n))
                for i in range(8):
                    ve.inc(V.tensor_add(t8(otb, i), t8(sgb, i), t8(xxb, i)))
                ld_free = (ve, ve.n)
                w(A, (ve, ve.n))
                self.d_out.inc(A.dma_start(
                    out=self.out_fm[:, cols].rearrange("(a p) n -> p a n", p=128),
                    in_=otb.rearrange("p (a n) -> p a n", a=8)), 16)
                ot_free = (self.d_out, self.d_out.n)

    def build(self):
        self.declare()
        with self.st:
            self.alloc()
            with self.nc.Block():
                self.p0_consts()
                if "p1" in self.phases:
                    self.p1_front()
                if "ex1" in self.phases:
                    self.ex1(); self.assemble1()
                if "ffn" in self.phases:
                    self.ffn_branch()
                if "scan" in self.phases:
                    self.scan_phase()
                if "ex2" in self.phases:
                    self.ex2(); self.assemble2()
                if "tail" in self.phases:
                    self.tail_phase()
                if "moe" in self.phases:
                    self.moe_phase()
                if "comb" in self.phases:
                    self.combine_phase()
                for s_ in ("d_xn", "d_xr", "d_res", "d_dt", "d_y", "d_xa",
                           "d_x1", "d_g", "d_xm", "d_sp", "d_out"):
                    self.w(self.A, (getattr(self, s_), getattr(self, s_).n))
                for s_ in ("dG1", "dG2", "d_wt", "d_u", "d_bc", "d_ym",
                           "d_ym2", "d_sp2", "dZg"):
                    self.w(self.G, (getattr(self, s_), getattr(self, s_).n))
                for s_ in ("dS", "d_sl", "d_sl2", "d_ug", "d_ffs", "d_ffs2",
                           "d_tl", "d_mx", "d_mx2", "d_cl", "dZs"):
                    self.w(self.SP, (getattr(self, s_), getattr(self, s_).n))
        return self.nc


def _host_prep(inputs):
    x = inputs["L0_attn_x"]
    f32 = np.float32
    cf = np.ascontiguousarray

    def bft(a):
        return cf(a.astype(BFNP))

    def rot_tabs(S):
        t = np.arange(S - 2048, S + 4096, dtype=np.float64)
        hi = np.arange(0, 64, 2, dtype=np.float64)
        inv = 1.0 / 10000.0 ** (hi / 64.0)
        fr = t[:, None] * inv[None, :]
        fr = np.concatenate([fr, fr], -1)
        base = (hi + 0.4 * 64) / (1.4 * 64)
        power = (t - 4096.0) / 256.0
        sc = base[None, :] ** power[:, None]
        sc = np.concatenate([sc, sc], -1)
        cos, sin = np.cos(fr), np.sin(fr)
        r = lambda a: bft(np.tile(a.T.astype(f32), (2, 1)))
        return r(cos * sc), r(sin * sc), r(cos / sc), r(sin / sc)

    ii = np.arange(128)[:, None]
    jj = np.arange(128)[None, :]
    m_triu = np.where(jj >= ii, 0.0, NEG).astype(f32)
    m_tril = np.where(jj <= ii, 0.0, NEG).astype(f32)
    ident = np.eye(128, dtype=BFNP)
    ident32 = np.eye(128, dtype=f32)
    ones_r = np.ones((1, 128), BFNP)
    ind2 = np.zeros((128, 2), BFNP)
    ind2[0:64, 0] = 1
    ind2[64:128, 1] = 1
    rep2 = np.zeros((2, 128), BFNP)
    rep2[0, 0:64] = 1
    rep2[1, 64:128] = 1
    P64 = np.zeros((64, 64), f32)
    P64[0:32, 32:64] = -np.eye(32)
    P64[32:64, 0:32] = np.eye(32)
    rotp = np.zeros((128, 128), f32)
    rotp[0:64, 0:64] = P64
    rotp[64:128, 64:128] = P64
    rot_pt = bft(rotp.T)
    ckarr = np.tile(np.array(CKVALS, f32)[None, :], (128, 1))

    convw = inputs["L0_attn_conv_w"][:, 0, :]
    convw_t = cf(convw.reshape(16, 128, 4).transpose(1, 0, 2)
                 .reshape(128, 64).astype(f32))
    convb_t = cf(inputs["L0_attn_conv_b"].reshape(16, 128).T.astype(f32))
    dvec_t = cf(inputs["L0_attn_D"].reshape(16, 128).T.astype(f32))
    qsc = inputs["L0_ffn_q_scale"]
    qsc_t = cf(np.tile(qsc.T, (2, 1)).astype(f32))
    ksc_t = cf(np.tile(inputs["L0_ffn_k_scale"].T, (2, 1)).astype(f32))
    lng_t = cf(inputs["L0_ffn_ln_g"].reshape(2, 8, 128)
               .transpose(2, 0, 1).reshape(128, 16).astype(f32))
    lnb_t = cf(inputs["L0_ffn_ln_b"].reshape(2, 8, 128)
               .transpose(2, 0, 1).reshape(128, 16).astype(f32))
    fw1 = inputs["L0_ffn_w1"]
    fb1 = inputs["L0_ffn_b1"]
    w1a = np.zeros((2, DIM, HID2), f32)
    w1g = np.zeros((2, DIM, HID2), f32)
    w1a[:, :, 0:2730] = fw1[:, :, 0:2730]
    w1g[:, :, 0:2730] = fw1[:, :, 2730:5460]
    b1a = np.zeros((2, HID2), f32)
    b1g = np.zeros((2, HID2), f32)
    b1a[:, 0:2730] = fb1[:, 0:2730]
    b1g[:, 0:2730] = fb1[:, 2730:5460]
    b1a_t = cf(b1a.reshape(2, 22, 128).transpose(2, 0, 1).reshape(128, 44))
    b1g_t = cf(b1g.reshape(2, 22, 128).transpose(2, 0, 1).reshape(128, 44))
    w2pad = np.zeros((2, HID2, DIM), f32)
    w2pad[:, 0:2730, :] = inputs["L0_ffn_w2"]
    fb2_t = cf(inputs["L0_ffn_b2"].reshape(2, 8, 128)
               .transpose(2, 0, 1).reshape(128, 16).astype(f32))
    inp_t = cf(inputs["L0_attn_in_proj_w"].T.astype(f32))
    xproj_t = np.zeros((DI, 128), f32)
    xproj_t[:, 0:96] = inputs["L0_attn_x_proj_w"].T
    outp_t = cf(inputs["L0_attn_out_proj_w"].T.astype(f32))
    gate_t = cf(inputs["L0_attn_gate_w"].T.astype(f32))
    mw1 = cf(inputs["L0_attn_w1"].transpose(0, 2, 1).astype(f32))
    mw2 = cf(inputs["L0_attn_w2"].transpose(0, 2, 1).astype(f32))
    dtp_t_full = inputs["L0_attn_dt_proj_w"].T.astype(f32)

    shared = dict(
        m_triu=m_triu, m_tril=m_tril, identv=ident, onesrv=ones_r,
        ind2v=ind2, rep2v=rep2, rotptv=rot_pt, ckv=ckarr, convwv=convw_t,
        convbv=convb_t, dvecv=dvec_t, qscv=qsc_t, kscv=ksc_t, lngv=lng_t,
        lnbv=lnb_t, b1av=b1a_t, b1gv=b1g_t, fb2v=fb2_t,
        ident32=ident32,
        wqkv=cf(inputs["L0_ffn_wqkv"].astype(f32)),
        wo_w=cf(inputs["L0_ffn_wo"].astype(f32)),
        w1av=cf(w1a), w1gv=cf(w1g), w2pv=cf(w2pad),
        inp_t=inp_t, xproj_t=xproj_t, outp_t=outp_t, gate_t=gate_t,
        mw1=mw1, mw2=mw2,
    )
    in_maps = []
    for core in range(8):
        b, p = core // 2, core % 2
        Sa = p * 4096
        lo = Sa - HALO
        xs = np.zeros((TOTC, DIM), f32)
        src_lo = max(lo, 0)
        xs[src_lo - lo:] = x[b, src_lo:Sa + 4096]
        tqc, tqs, tkc, tks = rot_tabs(Sa)
        dsh = slice(p * 1024, p * 1024 + 1024)
        m = dict(shared)
        m.update(
            xfm=cf(xs.T),
            tqc=tqc, tqs=tqs, tkc=tkc, tks=tks,
            m_w0=np.full((128, 512), NEGW0 if p == 0 else 0.0, f32),
            x1zv=np.full((128, 1), 0.0 if p == 0 else 1.0, f32),
            parv=np.array([[p]], np.int32),
            dtbv=cf(inputs["L0_attn_dt_proj_b"][dsh].reshape(8, 128).T
                    .astype(f32)),
            alogv=cf(inputs["L0_attn_A_log"][dsh].astype(f32)),
            dtproj_t=cf(dtp_t_full[:, dsh]),
        )
        in_maps.append(m)
    return in_maps


def kernel(**inputs):
    in_maps = _host_prep(inputs)
    nc = K().build()
    res = run_bass_kernel_spmd(nc, in_maps, core_ids=list(range(8)))
    out = np.zeros((4, 8192, DIM), np.float32)
    for core in range(8):
        b, p = core // 2, core % 2
        out[b, p * 4096:(p + 1) * 4096] = res.results[core]["out_fm"].T
    return out


# version 05
# speedup vs baseline: 1.1677x; 1.0489x over previous
import contextlib
import numpy as np
import ml_dtypes
import concourse.bass as bass
import concourse.mybir as mybir
from concourse.bass_utils import run_bass_kernel_spmd

F32 = mybir.dt.float32
BF16 = mybir.dt.bfloat16
I32 = mybir.dt.int32
AF = mybir.ActivationFunctionType
ALU = mybir.AluOpType
AX = mybir.AxisListType
BFNP = ml_dtypes.bfloat16

NEG = -3.0e37
NEGW0 = -1.0e37
DIM = 1024
DI = 2048
NST = 16
DTR = 64
HALO = 2560
TOK = 4096
TOTC = HALO + TOK
NCH = TOTC // 512
L1Q0 = HALO - 2048
HID2 = 2816
MHID = 4096
LN32 = float(np.log(32.0))

CKVALS = [0.0, 1.0, -1.0, 1e-5, -0.5, LN32, 1.0 / 1024.0, 1e-24, 8.0, -1e30]
CK = {v: i for i, v in enumerate(CKVALS)}
NCK = len(CKVALS)


class S:
    def __init__(self, h):
        self.h = h
        self.n = 0

    def inc(self, instr, k=1):
        instr.then_inc(self.h, k)
        self.n += k
        return self.n


class K:
    def __init__(self, dbg=(), phases=("p1", "ex1", "ffn", "scan", "ex2",
                                       "tail", "moe", "comb")):
        self.dbg = set(dbg)
        self.phases = set(phases)
        self.nc = bass.Bass("TRN2", target_bir_lowering=False, debug=False)
        self.st = contextlib.ExitStack()

    def dram(self, name, shape, dt, kind=None):
        kk = {}
        if kind is not None:
            kk["kind"] = kind
        elif name in self.dbg:
            kk["kind"] = "ExternalOutput"
        return self.nc.dram_tensor(name, shape, dt, **kk).ap()

    def sb(self, n, sh, dt):
        return self.st.enter_context(self.nc.sbuf_tensor(n, sh, dt))

    def psumt(self, n, sh, dt):
        return self.st.enter_context(self.nc.psum_tensor(n, sh, dt))

    def sem(self, n):
        return S(self.st.enter_context(self.nc.semaphore(n)))

    def w(self, eng, *deps):
        for s_, c_ in deps:
            if c_ > 0:
                eng.wait_ge(s_.h, c_)

    def ck(self, val, row=False, p=None):
        j = CK[val]
        if p is not None:
            return self.c_ck[0:p, j:j + 1]
        return self.c_ck[0:1, j:j + 1] if row else self.c_ck[:, j:j + 1]

    def bank_wait(self, eng, nm):
        for dep in self.bfree[nm]:
            self.w(eng, dep)
        self.bfree[nm] = []

    def bank_used(self, nm, dep):
        self.bfree[nm].append(dep)

    def enter_phase(self):
        deps = []
        for s_ in ("dS", "dG1", "dG2", "d_wt", "d_u", "d_bc", "d_ym", "d_sp2",
                   "dZg", "d_sl", "d_ug", "d_ffs", "d_tl", "d_mx", "d_cl",
                   "dZs", "d_ffs2", "d_mx2", "d_sl2", "d_ym2",
                   "d_xn", "d_xr", "d_res", "d_dt", "d_y", "d_xa",
                   "d_x1", "d_g", "d_xm", "d_sp", "d_out", "cC",
                   "pe", "ve", "ac", "po"):
            s = getattr(self, s_)
            if s.n > 0:
                deps.append((s, s.n))
        for eng in (self.T, self.V, self.A, self.G, self.SP):
            self.w(eng, *deps)

    def declare(self):
        d = self.dram
        EI = "ExternalInput"
        self.xfm = d("xfm", [DIM, TOTC], F32, EI)
        self.tqc = d("tqc", [128, 6144], BF16, EI)
        self.tqs = d("tqs", [128, 6144], BF16, EI)
        self.tkc = d("tkc", [128, 6144], BF16, EI)
        self.tks = d("tks", [128, 6144], BF16, EI)
        self.m_triu = d("m_triu", [128, 128], F32, EI)
        self.m_tril = d("m_tril", [128, 128], F32, EI)
        self.m_w0 = d("m_w0", [128, 512], F32, EI)
        self.x1zv = d("x1zv", [128, 1], F32, EI)
        self.parv = d("parv", [1, 1], I32, EI)
        self.identv = d("identv", [128, 128], BF16, EI)
        self.ident32v = d("ident32", [128, 128], F32, EI)
        self.onesrv = d("onesrv", [1, 128], BF16, EI)
        self.ind2v = d("ind2v", [128, 2], BF16, EI)
        self.rep2v = d("rep2v", [2, 128], BF16, EI)
        self.rotptv = d("rotptv", [128, 128], BF16, EI)
        self.ckv = d("ckv", [128, NCK], F32, EI)
        self.convwv = d("convwv", [128, 64], F32, EI)
        self.convbv = d("convbv", [128, 16], F32, EI)
        self.dvecv = d("dvecv", [128, 16], F32, EI)
        self.dtbv = d("dtbv", [128, 8], F32, EI)
        self.qscv = d("qscv", [128, 2], F32, EI)
        self.kscv = d("kscv", [128, 2], F32, EI)
        self.lngv = d("lngv", [128, 16], F32, EI)
        self.lnbv = d("lnbv", [128, 16], F32, EI)
        self.b1av = d("b1av", [128, 44], F32, EI)
        self.b1gv = d("b1gv", [128, 44], F32, EI)
        self.fb2v = d("fb2v", [128, 16], F32, EI)
        self.alogv = d("alogv", [1024, 16], F32, EI)
        self.wqkv = d("wqkv", [2, DIM, 1536], F32, EI)
        self.wo_w = d("wo_w", [2, 512, DIM], F32, EI)
        self.w1av = d("w1av", [2, DIM, HID2], F32, EI)
        self.w1gv = d("w1gv", [2, DIM, HID2], F32, EI)
        self.w2pv = d("w2pv", [2, HID2, DIM], F32, EI)
        self.inp_t = d("inp_t", [DIM, 2 * DI], F32, EI)
        self.xproj_t = d("xproj_t", [DI, 128], F32, EI)
        self.dtproj_t = d("dtproj_t", [DTR, 1024], F32, EI)
        self.outp_t = d("outp_t", [DI, DIM], F32, EI)
        self.gate_t = d("gate_t", [DIM, 8], F32, EI)
        self.mw1 = d("mw1", [8, DIM, MHID], F32, EI)
        self.mw2 = d("mw2", [8, MHID, DIM], F32, EI)
        self.xn_d = d("xn_d", [DIM, TOTC], BF16)
        self.u_d = d("u_d", [DI, TOK], BF16)
        self.res_d = d("res_d", [DI, TOK], BF16)
        self.xr1_d = d("xr1_d", [DIM, TOK], BF16)
        self.dt_own = d("dt_own", [DTR, TOK], F32)
        self.bc_own = d("bc_own", [32, TOK], BF16)
        self.uoth_c = d("uoth_c", [1024, TOK], BF16)
        self.uoth_g = d("uoth_g", [2048, TOK], BF16)
        self.dt_gt = d("dt_gt", [2 * DTR, TOK], F32)
        self.bc_gt = d("bc_gt", [64, TOK], BF16)
        self.u_sc = d("u_sc", [1024, 8192], BF16)
        self.dt_sc = d("dt_sc", [DTR, 8192], F32)
        self.bc_sc = d("bc_sc", [32, 8192], BF16)
        self.y_dd = d("y_dd", [1024, 8192], BF16)
        self.y_cc = d("y_cc", [1024, TOK], BF16)
        self.y_gg = d("y_gg", [2048, TOK], BF16)
        self.y_alld = d("y_alld", [DI, TOK], BF16)
        self.xatt_d = d("xatt_d", [DIM, 5632], BF16)
        self.x1_d = d("x1_d", [DIM, 5632], BF16)
        self.g_dd = d("g_dd", [DIM, TOK], BF16)
        self.xm_d = d("xm_d", [DIM, TOK], BF16)
        self.sp_d = d("sp_d", [8, TOK], F32)
        self.ymoe_d = d("ymoe_d", [DIM, TOK], BF16)
        self.out_fm = d("out_fm", [DIM, TOK], F32, "ExternalOutput")

    def alloc(self):
        sb, ps = self.sb, self.psumt
        self.c_id = sb("c_id", [128, 128], BF16)
        self.c_id32 = sb("c_id32", [128, 128], F32)
        self.c_ones = sb("c_ones", [1, 128], BF16)
        self.c_onec = sb("c_onec", [128, 1], BF16)
        self.c_ind2 = sb("c_ind2", [128, 2], BF16)
        self.c_rep2 = sb("c_rep2", [2, 128], BF16)
        self.c_rotp = sb("c_rotp", [128, 128], BF16)
        self.c_triu = sb("c_triu", [128, 128], F32)
        self.c_tril = sb("c_tril", [128, 128], F32)
        self.c_w0 = sb("c_w0", [128, 512], F32)
        self.c_x1z = sb("c_x1z", [128, 1], F32)
        self.c_ck = sb("c_ck", [128, NCK], F32)
        self.c_convw = sb("c_convw", [128, 64], F32)
        self.c_convb = sb("c_convb", [128, 16], F32)
        self.c_dvec = sb("c_dvec", [128, 16], F32)
        self.c_dtb = sb("c_dtb", [128, 8], F32)
        self.c_qsc = sb("c_qsc", [128, 2], F32)
        self.c_ksc = sb("c_ksc", [128, 2], F32)
        self.c_lng = sb("c_lng", [128, 16], F32)
        self.c_lnb = sb("c_lnb", [128, 16], F32)
        self.c_b1a = sb("c_b1a", [128, 44], F32)
        self.c_b1g = sb("c_b1g", [128, 44], F32)
        self.c_fb2 = sb("c_fb2", [128, 16], F32)
        self.c_A = sb("c_A", [128, 128], F32)
        self.c_par = sb("c_par", [1, 1], I32)
        self.g0 = ps("g0", [128, 512], F32)
        self.g1 = ps("g1", [128, 512], F32)
        self.g2 = ps("g2", [128, 512], F32)
        self.g3 = ps("g3", [128, 512], F32)
        self.g4 = ps("g4", [128, 1024], BF16)
        self.g5 = ps("g5", [128, 512], F32)
        self.g6 = ps("g6", [128, 512], F32)
        self.g7 = ps("g7", [128, 512], F32)
        self.bfree = {nm: [] for nm in
                      ("g0", "g1", "g2", "g3", "g4", "g5", "g6", "g7")}
        for nm in ("dS", "dG1", "dG2", "d_wt", "d_u", "d_bc", "d_ym",
                   "d_sp2", "dZg", "d_sl", "d_ug", "d_ffs", "d_tl", "d_mx",
                   "d_cl", "dZs", "d_ffs2", "d_mx2", "d_sl2", "d_ym2",
                   "d_xn", "d_xr", "d_res", "d_dt", "d_y",
                   "d_xa", "d_x1", "d_g", "d_xm", "d_sp", "d_out", "cC",
                   "pe", "ve", "ac", "po"):
            setattr(self, nm, self.sem(nm))
        nc = self.nc
        self.T, self.V, self.A, self.G, self.SP = (
            nc.tensor, nc.vector, nc.scalar, nc.gpsimd, nc.sync)

    def p0_consts(self):
        V, A, SP = self.V, self.A, self.SP
        w, dS, ve, ac = self.w, self.dS, self.ve, self.ac
        pairs = [
            (self.c_id, self.identv), (self.c_id32, self.ident32v),
            (self.c_ones, self.onesrv),
            (self.c_ind2, self.ind2v), (self.c_rep2, self.rep2v),
            (self.c_rotp, self.rotptv), (self.c_triu, self.m_triu),
            (self.c_tril, self.m_tril), (self.c_w0, self.m_w0),
            (self.c_x1z, self.x1zv), (self.c_ck, self.ckv),
            (self.c_convw, self.convwv), (self.c_convb, self.convbv),
            (self.c_dvec, self.dvecv), (self.c_dtb, self.dtbv),
            (self.c_qsc, self.qscv), (self.c_ksc, self.kscv),
            (self.c_lng, self.lngv), (self.c_lnb, self.lnbv),
            (self.c_b1a, self.b1av), (self.c_b1g, self.b1gv),
            (self.c_fb2, self.fb2v), (self.c_par, self.parv),
        ]
        for s_, d_ in pairs:
            dS.inc(SP.dma_start(out=s_[:], in_=d_), 16)
        for gi in range(8):
            dS.inc(SP.dma_start(out=self.c_A[:, gi * 16:(gi + 1) * 16],
                                in_=self.alogv[gi * 128:(gi + 1) * 128, :]), 16)
        ve.inc(self.V.memset(self.c_onec[:], 1.0))
        w(A, (dS, dS.n))
        ac.inc(A.activation(self.c_A[:], self.c_A[:], AF.Exp,
                            bias=self.ck(0.0), scale=self.ck(1.0)))
        w(V, (ac, ac.n))
        ve.inc(V.tensor_scalar_mul(self.c_A[:], self.c_A[:], -1.0))
        self.cr = (ve, ve.n)
        self.crd = (dS, dS.n)

    def p1_front(self):
        nc = self.nc
        T, V, A, G, SP = self.T, self.V, self.A, self.G, self.SP
        w = self.w
        dS = self.dS
        pe, ve, ac = self.pe, self.ve, self.ac
        g0, g1, g2, g3 = self.g0, self.g1, self.g2, self.g3
        self.enter_phase()
        est = contextlib.ExitStack()
        sbl = lambda n, sh, dt: est.enter_context(nc.sbuf_tensor(n, sh, dt))
        with est:
            xb = [sbl(f"p1xb{i}", [128, 8 * 512], BF16) for i in range(2)]
            sq = sbl("p1sq", [128, 8 * 512], BF16)
            rmu = sbl("p1rmu", [1, 512], F32)
            rva = sbl("p1rva", [1, 512], F32)
            rtm = sbl("p1rtm", [1, 512], F32)
            rmub = sbl("p1rmub", [1, 512], BF16)
            rvab = sbl("p1rvab", [1, 512], BF16)
            xnt = sbl("p1xn", [128, 8 * 512], BF16)
            xr1t = sbl("p1xr1", [128, 8 * 512], BF16)
            xmb = sbl("p1xm", [128, 16 * 515], F32)
            resb = [sbl(f"p1res{i}", [128, 512], BF16) for i in range(2)]
            xct = sbl("p1xc", [128, 16 * 512], F32)
            ubt = sbl("p1u", [128, 16 * 512], BF16)
            xdb = sbl("p1xdb", [128, 512], F32)
            wip = sbl("p1wip", [128, 8 * 4096], BF16)
            wxp = sbl("p1wxp", [128, 16 * 128], BF16)

            t8 = lambda buf, i: buf[:, i * 512:(i + 1) * 512]

            self.d_wt.inc(G.dma_start(
                out=wip.rearrange("p (a n) -> p a n", a=8),
                in_=self.inp_t.rearrange("(a p) n -> p a n", p=128)), 16)
            self.d_wt.inc(G.dma_start(
                out=wxp.rearrange("p (a n) -> p a n", a=16),
                in_=self.xproj_t.rearrange("(a p) n -> p a n", p=128)), 16)
            wready = (self.d_wt, self.d_wt.n)

            ldsem = [self.dG1, self.dG2]
            xb_ready = [(dS, 0), (dS, 0)]
            xb_free = [(ve, 0), (ve, 0)]
            sq_free = (pe, 0)
            xnt_free = (ac, 0)
            xct_free = (ac, 0)
            xm_conv_done = (ve, 0)

            for c in range(NCH):
                pb = c % 2
                w(G, xb_free[pb])
                ldsem[pb].inc(G.dma_start(
                    out=xb[pb].rearrange("p (a n) -> p a n", a=8),
                    in_=self.xfm[:, c * 512:(c + 1) * 512]
                        .rearrange("(a p) n -> p a n", p=128)), 16)
                xb_ready[pb] = (ldsem[pb], ldsem[pb].n)

                w(A, xb_ready[pb], sq_free, self.crd)
                for i in range(8):
                    ac.inc(A.activation(t8(sq, i), t8(xb[pb], i), AF.Square,
                                        bias=self.ck(0.0), scale=self.ck(1.0)))
                self.bank_wait(T, "g2")
                self.bank_wait(T, "g3")
                w(T, xb_ready[pb], (ac, ac.n), self.cr)
                for i in range(8):
                    T.matmul(g2[0:1, :], self.c_onec[:], t8(xb[pb], i),
                             start=(i == 0), stop=(i == 7))
                for i in range(8):
                    mm = T.matmul(g3[0:1, :], self.c_onec[:], t8(sq, i),
                                  start=(i == 0), stop=(i == 7))
                pe.inc(mm)
                sq_free = (pe, pe.n)
                w(V, (pe, pe.n), (ac, ac.n))
                ve.inc(V.tensor_scalar_mul(rmu[:], g2[0:1, :], 1.0 / 1024.0))
                ve.inc(V.tensor_scalar_mul(rva[:], g3[0:1, :], 1.0 / 1024.0))
                w(V, (ve, ve.n))
                ve.inc(V.tensor_mul(rtm[:], rmu[:], rmu[:]))
                w(V, (ve, ve.n))
                ve.inc(V.tensor_sub(rva[:], rva[:], rtm[:]))
                self.bank_used("g2", (ve, ve.n))
                self.bank_used("g3", (ve, ve.n))
                w(A, (ve, ve.n))
                ac.inc(A.activation(rva[:], rva[:], AF.Ln,
                                    bias=self.ck(1e-5, True), scale=self.ck(1.0, True)))
                w(A, (ac, ac.n))
                ac.inc(A.activation(rva[:], rva[:], AF.Exp,
                                    bias=self.ck(0.0, True), scale=self.ck(-0.5, True)))
                w(V, (ac, ac.n))
                ve.inc(V.tensor_copy(rmub[:], rmu[:]))
                ve.inc(V.tensor_copy(rvab[:], rva[:]))
                self.bank_wait(T, "g2")
                self.bank_wait(T, "g3")
                w(T, (ve, ve.n))
                m1 = T.matmul(g2[:], self.c_ones[:], rmub[:], start=True, stop=True)
                m2 = T.matmul(g3[:], self.c_ones[:], rvab[:], start=True, stop=True)
                pe.inc(m2)
                w(V, (pe, pe.n), (self.d_xn, self.d_xn.n), xnt_free)
                for i in range(8):
                    ve.inc(V.tensor_sub(t8(xnt, i), t8(xb[pb], i), g2[:]))
                w(V, (ve, ve.n))
                for i in range(8):
                    ve.inc(V.tensor_mul(t8(xnt, i), t8(xnt, i), g3[:]))
                xb_free[pb] = (ve, ve.n)
                self.bank_used("g2", (ve, ve.n))
                self.bank_used("g3", (ve, ve.n))
                xn_done = (ve, ve.n)
                w(A, xn_done)
                self.d_xn.inc(A.dma_start(
                    out=self.xn_d[:, c * 512:(c + 1) * 512]
                        .rearrange("(a p) n -> p a n", p=128),
                    in_=xnt.rearrange("p (a n) -> p a n", a=8)), 16)

                if c < 4:
                    continue

                w(A, xn_done, sq_free)
                for i in range(8):
                    ac.inc(A.activation(t8(sq, i), t8(xnt, i), AF.Square,
                                        bias=self.ck(0.0), scale=self.ck(1.0)))
                xnt_free = (ac, ac.n)
                self.bank_wait(T, "g2")
                w(T, (ac, ac.n))
                for i in range(8):
                    mm = T.matmul(g2[0:1, :], self.c_onec[:], t8(sq, i),
                                  start=(i == 0), stop=(i == 7))
                pe.inc(mm)
                sq_free = (pe, pe.n)
                w(V, (pe, pe.n))
                ve.inc(V.tensor_scalar_max(rtm[:], g2[0:1, :], 1e-24))
                self.bank_used("g2", (ve, ve.n))
                w(A, (ve, ve.n))
                ac.inc(A.activation(rtm[:], rtm[:], AF.Ln,
                                    bias=self.ck(0.0, True), scale=self.ck(1.0, True)))
                w(A, (ac, ac.n))
                ac.inc(A.activation(rtm[:], rtm[:], AF.Exp,
                                    bias=self.ck(LN32, True), scale=self.ck(-0.5, True)))
                w(V, (ac, ac.n), (pe, pe.n))
                ve.inc(V.tensor_copy(rvab[:], rtm[:]))
                self.bank_wait(T, "g2")
                w(T, (ve, ve.n))
                m1 = T.matmul(g2[:], self.c_ones[:], rvab[:], start=True, stop=True)
                pe.inc(m1)
                w(V, (pe, pe.n), (self.d_xr, self.d_xr.n))
                for i in range(8):
                    ve.inc(V.tensor_mul(t8(xr1t, i), t8(xnt, i), g2[:]))
                self.bank_used("g2", (ve, ve.n))
                xr1_done = (ve, ve.n)
                if c >= 5:
                    w(A, xr1_done)
                    self.d_xr.inc(A.dma_start(
                        out=self.xr1_d[:, (c - 5) * 512:(c - 4) * 512]
                            .rearrange("(a p) n -> p a n", p=128),
                        in_=xr1t.rearrange("p (a n) -> p a n", a=8)), 16)

                w(V, xm_conv_done, xct_free)
                if c >= 5:
                    ve.inc(V.tensor_copy(
                        xmb.rearrange("p (a n) -> p a n", a=16)[:, :, 0:3],
                        xmb.rearrange("p (a n) -> p a n", a=16)[:, :, 512:515]))
                carry_done = (ve, ve.n)

                banks = [g0, g1]
                bnames = ["g0", "g1"]
                w(T, xr1_done, wready)
                for mi in range(32):
                    bk = banks[mi % 2]
                    bn = bnames[mi % 2]
                    self.bank_wait(T, bn)
                    for kk in range(8):
                        mm = T.matmul(bk[:], wip[:, kk * 4096 + mi * 128:
                                                 kk * 4096 + (mi + 1) * 128],
                                      t8(xr1t, kk), start=(kk == 0), stop=(kk == 7))
                    pe.inc(mm)
                    mmdep = (pe, pe.n)
                    if mi < 16:
                        w(V, mmdep, carry_done)
                        ve.inc(V.tensor_copy(
                            xmb[:, mi * 515 + 3:mi * 515 + 515], bk[:]))
                        self.bank_used(bn, (ve, ve.n))
                    else:
                        rb = resb[mi % 2]
                        w(V, mmdep, (self.d_res, self.d_res.n))
                        ve.inc(V.tensor_copy(rb[:], bk[:]))
                        self.bank_used(bn, (ve, ve.n))
                        if c >= 5:
                            w(A, (ve, ve.n))
                            self.d_res.inc(A.dma_start(
                                out=self.res_d[(mi - 16) * 128:(mi - 15) * 128,
                                               (c - 5) * 512:(c - 4) * 512],
                                in_=rb[:]), 16)
                xm_ready = (ve, ve.n)

                if c < 5:
                    xm_conv_done = (ve, ve.n)
                    continue

                w(V, xm_ready, xct_free)
                for j in range(16):
                    base = j * 515
                    ve.inc(V.tensor_scalar(
                        xct[:, j * 512:(j + 1) * 512],
                        xmb[:, base:base + 512],
                        self.c_convw[:, j * 4 + 0:j * 4 + 1], None, ALU.mult))
                w(V, (ve, ve.n))
                for kk in range(1, 4):
                    for j in range(16):
                        base = j * 515
                        ve.inc(V.scalar_tensor_tensor(
                            xct[:, j * 512:(j + 1) * 512],
                            xmb[:, base + kk:base + kk + 512],
                            self.c_convw[:, j * 4 + kk:j * 4 + kk + 1],
                            xct[:, j * 512:(j + 1) * 512], ALU.mult, ALU.add))
                    w(V, (ve, ve.n))
                xm_conv_done = (ve, ve.n)
                w(A, (ve, ve.n), (pe, pe.n), (self.d_u, self.d_u.n))
                for j in range(16):
                    ac.inc(A.activation(ubt[:, j * 512:(j + 1) * 512],
                                        xct[:, j * 512:(j + 1) * 512], AF.Silu,
                                        bias=self.c_convb[:, j:j + 1],
                                        scale=self.ck(1.0)))
                u_done = (ac, ac.n)
                xct_free = (ac, ac.n)
                w(G, u_done)
                self.d_u.inc(G.dma_start(
                    out=self.u_d[:, (c - 5) * 512:(c - 4) * 512]
                        .rearrange("(a p) n -> p a n", p=128),
                    in_=ubt.rearrange("p (a n) -> p a n", a=16)), 16)

                self.bank_wait(T, "g3")
                w(T, u_done, wready)
                for kk in range(16):
                    mm = T.matmul(g3[:], wxp[:, kk * 128:(kk + 1) * 128],
                                  ubt[:, kk * 512:(kk + 1) * 512],
                                  start=(kk == 0), stop=(kk == 15))
                pe.inc(mm)
                w(V, (pe, pe.n), (self.d_dt, self.d_dt.n),
                  (self.d_bc, self.d_bc.n))
                ve.inc(V.tensor_copy(xdb[:], g3[:]))
                self.bank_used("g3", (ve, ve.n))
                w(A, (ve, ve.n))
                self.d_dt.inc(A.dma_start(
                    out=self.dt_own[:, (c - 5) * 512:(c - 4) * 512],
                    in_=xdb[0:64, :]), 16)
                w(G, (ve, ve.n))
                self.d_bc.inc(G.dma_start(
                    out=self.bc_own[:, (c - 5) * 512:(c - 4) * 512],
                    in_=xdb[64:96, :]), 16)
            self.p1_done = ((self.d_xn, self.d_xn.n), (self.d_xr, self.d_xr.n),
                            (self.d_res, self.d_res.n),
                            (self.d_u, self.d_u.n), (self.d_dt, self.d_dt.n),
                            (self.d_bc, self.d_bc.n))

    def ffn_branch(self):
        for lay in (0, 1):
            self.stream_dep = ((self.d_xn, self.d_xn.n) if lay == 0
                               else (self.d_x1, self.d_x1.n))
            self.ffn_passA(lay)
            self.xatt_dep = (self.d_xa, self.d_xa.n)
            self.ffn_passB(lay)
        self.ffn_done = (self.d_g, self.d_g.n)

    def ffn_passA(self, lay):
        nc = self.nc
        T, V, A, G, SP = self.T, self.V, self.A, self.G, self.SP
        w = self.w
        pe, ve, ac = self.pe, self.ve, self.ac
        g0, g1, g2, g3, g4, g5, g6, g7 = (self.g0, self.g1, self.g2, self.g3,
                                          self.g4, self.g5, self.g6, self.g7)
        stream = self.xn_d if lay == 0 else self.x1_d
        base = L1Q0 if lay == 0 else 512
        nqb = 12 if lay == 0 else 10
        tab0 = 0 if lay == 0 else 1024
        w0qb = 4 if lay == 0 else 2
        self.enter_phase()
        est = contextlib.ExitStack()
        sbl = lambda n, sh, dt: est.enter_context(
            nc.sbuf_tensor(f"{n}_{lay}", sh, dt))
        with est:
            wqb = sbl("a_wq", [128, 8 * 1536], BF16)
            wob = sbl("a_wo", [128, 4 * 1024], BF16)
            xin = [sbl(f"a_xi{i}", [128, 8 * 512], BF16) for i in range(2)]
            xsh = [sbl(f"a_xs{i}", [128, 4 * 512], BF16) for i in range(2)]
            tbq = sbl("a_tbq", [128, 2 * 512], BF16)
            tbk = sbl("a_tbk", [128, 2 * 512], BF16)
            qkvb = sbl("a_qkv", [128, 12 * 512], BF16)
            sq2 = sbl("a_sq2", [128, 512], BF16)
            rr2 = sbl("a_rr2", [2, 512], F32)
            rr2b = sbl("a_rr2b", [2, 512], BF16)
            qn = sbl("a_qn", [128, 512], BF16)
            t1b = sbl("a_t1", [128, 512], BF16)
            t2b = sbl("a_t2", [128, 512], BF16)
            qf = [sbl(f"a_qf{i}", [128, 8 * 512], BF16) for i in range(2)]
            vTc = [sbl(f"a_vT{i}", [128, 4 * 512], BF16) for i in range(2)]
            eAb = sbl("a_eA", [128, 512], BF16)
            eBb = sbl("a_eB", [128, 128], BF16)
            aT = sbl("a_aT", [128, 640], BF16)
            rA = sbl("a_rA", [128, 1], F32)
            rB = sbl("a_rB", [128, 1], F32)
            r1s = sbl("a_r1s", [128, 1], F32)
            sA = sbl("a_sA", [128, 1], F32)
            sB = sbl("a_sB", [128, 1], F32)
            osb = sbl("a_osb", [128, 4 * 512], BF16)
            xatt = sbl("a_xatt", [128, 8 * 512], BF16)

            pdone = getattr(self, "p1_done", ())
            w(G, *pdone)
            self.d_wt.inc(G.dma_start(
                out=wqb.rearrange("p (a n) -> p a n", a=8),
                in_=self.wqkv[lay].rearrange("(a p) n -> p a n", p=128)), 16)
            self.d_wt.inc(G.dma_start(
                out=wob.rearrange("p (a n) -> p a n", a=4),
                in_=self.wo_w[lay].rearrange("(a p) n -> p a n", p=128)), 16)
            wrdy = (self.d_wt, self.d_wt.n)

            xin_free = [(pe, 0), (pe, 0)]
            tb_free = (ve, 0)
            qf_free = [(pe, 0), (pe, 0)]
            vT_free = [(pe, 0), (pe, 0)]
            qkvb_free = (ve, 0)
            osb_free = (pe, 0)
            xatt_free = (self.d_xa, 0)
            eA_free = (pe, 0)
            eB_free = (pe, 0)
            aT_free = (pe, 0)
            t8 = lambda b, i: b[:, i * 512:(i + 1) * 512]

            for bi in range(nqb):
                cur = bi % 2
                prv = 1 - cur
                dffs = self.d_ffs if cur == 0 else self.d_ffs2
                sc0 = base + bi * 512
                w(SP, xin_free[cur], self.stream_dep, *pdone)
                dffs.inc(SP.dma_start(
                    out=xin[cur].rearrange("p (a n) -> p a n", a=8),
                    in_=stream[:, sc0:sc0 + 512]
                        .rearrange("(a p) n -> p a n", p=128)), 16)
                dffs.inc(SP.dma_start(
                    out=xsh[cur].rearrange("p (a n) -> p a n", a=4),
                    in_=stream[512:1024, sc0 - 1:sc0 + 511]
                        .rearrange("(a p) n -> p a n", p=128)), 16)
                w(SP, tb_free)
                dffs.inc(SP.dma_start(
                    out=tbq[:, 0:512], in_=self.tqc[:, tab0 + bi * 512:
                                                    tab0 + (bi + 1) * 512]), 16)
                dffs.inc(SP.dma_start(
                    out=tbq[:, 512:1024], in_=self.tqs[:, tab0 + bi * 512:
                                                       tab0 + (bi + 1) * 512]), 16)
                dffs.inc(SP.dma_start(
                    out=tbk[:, 0:512], in_=self.tkc[:, tab0 + bi * 512:
                                                    tab0 + (bi + 1) * 512]), 16)
                dffs.inc(SP.dma_start(
                    out=tbk[:, 512:1024], in_=self.tks[:, tab0 + bi * 512:
                                                       tab0 + (bi + 1) * 512]), 16)
                ldrdy = (dffs, dffs.n)
                banks = [g0, g1]
                bn = ["g0", "g1"]
                w(T, ldrdy, wrdy, qkvb_free)
                for mi in range(12):
                    bk = banks[mi % 2]
                    self.bank_wait(T, bn[mi % 2])
                    for kk in range(8):
                        rhs = t8(xin[cur], kk) if kk < 4 else t8(xsh[cur], kk - 4)
                        mm = T.matmul(bk[:], wqb[:, kk * 1536 + mi * 128:
                                                  kk * 1536 + (mi + 1) * 128],
                                      rhs, start=(kk == 0), stop=(kk == 7))
                    pe.inc(mm)
                    w(V, (pe, pe.n))
                    ve.inc(V.tensor_copy(t8(qkvb, mi), bk[:]))
                    self.bank_used(bn[mi % 2], (ve, ve.n))
                qkv_rdy = (ve, ve.n)
                for i in range(8):
                    w(A, qkv_rdy, (pe, pe.n))
                    ac.inc(A.activation(sq2[:], t8(qkvb, i), AF.Square,
                                        bias=self.ck(0.0), scale=self.ck(1.0)))
                    self.bank_wait(T, "g2")
                    w(T, (ac, ac.n))
                    mm = T.matmul(g2[0:2, :], self.c_ind2[:], sq2[:],
                                  start=True, stop=True)
                    pe.inc(mm)
                    w(V, (pe, pe.n), (ac, ac.n))
                    ve.inc(V.tensor_scalar_max(rr2[:], g2[0:2, :], 1e-24))
                    self.bank_used("g2", (ve, ve.n))
                    w(A, (ve, ve.n))
                    ac.inc(A.activation(rr2[:], rr2[:], AF.Ln,
                                        bias=self.ck(0.0, p=2),
                                        scale=self.ck(1.0, p=2)))
                    w(A, (ac, ac.n))
                    ac.inc(A.activation(rr2[:], rr2[:], AF.Exp,
                                        bias=self.ck(0.0, p=2),
                                        scale=self.ck(-0.5, p=2)))
                    w(V, (ac, ac.n), (pe, pe.n))
                    ve.inc(V.tensor_copy(rr2b[:], rr2[:]))
                    self.bank_wait(T, "g3")
                    w(T, (ve, ve.n))
                    mm = T.matmul(g3[:], self.c_rep2[:], rr2b[:],
                                  start=True, stop=True)
                    pe.inc(mm)
                    w(V, (pe, pe.n), (ac, ac.n))
                    ve.inc(V.tensor_mul(qn[:], t8(qkvb, i), g3[:]))
                    self.bank_used("g3", (ve, ve.n))
                    sc_col = self.c_qsc if i < 4 else self.c_ksc
                    w(V, (ve, ve.n))
                    ve.inc(V.tensor_scalar(qn[:], qn[:],
                                           sc_col[:, lay:lay + 1], None, ALU.mult))
                    self.bank_wait(T, "g2")
                    w(T, (ve, ve.n))
                    mm = T.matmul(g2[:], self.c_rotp[:], qn[:],
                                  start=True, stop=True)
                    pe.inc(mm)
                    tb = tbq if i < 4 else tbk
                    w(V, ldrdy, (ve, ve.n))
                    ve.inc(V.tensor_mul(t1b[:], qn[:], tb[:, 0:512]))
                    w(V, (pe, pe.n), (ve, ve.n))
                    ve.inc(V.tensor_mul(t2b[:], g2[:], tb[:, 512:1024]))
                    self.bank_used("g2", (ve, ve.n))
                    w(V, (ve, ve.n), qf_free[cur] if i == 0 else (ve, 0))
                    ve.inc(V.tensor_add(t8(qf[cur], i), t1b[:], t2b[:]))
                qf_rdy = (ve, ve.n)
                tb_free = (ve, ve.n)
                self.bank_wait(T, "g4")
                w(T, qkv_rdy, vT_free[cur])
                for vt in range(4):
                    for j in range(4):
                        mm = T.transpose(g4[:, j * 128:(j + 1) * 128],
                                         qkvb[:, (8 + vt) * 512 + j * 128:
                                              (8 + vt) * 512 + (j + 1) * 128],
                                         self.c_id[:])
                    pe.inc(mm)
                    w(V, (pe, pe.n))
                    ve.inc(V.tensor_copy(vTc[cur][:, vt * 512:(vt + 1) * 512],
                                         g4[:, 0:512]))
                    self.bank_used("g4", (ve, ve.n))
                    self.bank_wait(T, "g4")
                vT_rdy = (ve, ve.n)
                qkvb_free = (ve, ve.n)

                if bi == 0:
                    xin_free[cur] = (pe, pe.n)
                    qf_free[cur] = (pe, pe.n)
                    vT_free[cur] = (ve, ve.n)
                    continue

                for hp in range(4):
                    for hh in range(2):
                        h = 2 * hp + hh
                        avbank, avbn = (g7, "g7") if hh == 0 else (g3, "g3")
                        self.bank_wait(T, avbn)
                        for qt in range(4):
                            rowsl = slice(hh * 64, hh * 64 + 64)
                            lhsT = qf[cur][rowsl, (h // 2) * 512 + qt * 128:
                                           (h // 2) * 512 + (qt + 1) * 128]
                            self.bank_wait(T, "g5")
                            self.bank_wait(T, "g6")
                            w(T, qf_rdy)
                            for jt in range(5):
                                ja = qt + jt
                                slot = prv if ja < 4 else cur
                                rhs = qf[slot][rowsl,
                                               (4 + h // 2) * 512 + (ja % 4) * 128:
                                               (4 + h // 2) * 512 + (ja % 4 + 1) * 128]
                                if jt < 4:
                                    mm = T.matmul(g5[:, jt * 128:(jt + 1) * 128],
                                                  lhsT, rhs, start=True, stop=True)
                                else:
                                    mm = T.matmul(g6[:, 0:128], lhsT, rhs,
                                                  start=True, stop=True)
                            pe.inc(mm)
                            w(V, (pe, pe.n), (ac, ac.n))
                            ve.inc(V.tensor_add(g5[:, 0:128], g5[:, 0:128],
                                                self.c_triu[:]))
                            ve.inc(V.tensor_add(g6[:, 0:128], g6[:, 0:128],
                                                self.c_tril[:]))
                            if bi == w0qb:
                                cw = (4 - qt) * 128
                                w(V, (ve, ve.n))
                                ve.inc(V.tensor_add(g5[:, 0:cw], g5[:, 0:cw],
                                                    self.c_w0[:, 0:cw]))
                            w(V, (ve, ve.n))
                            ve.inc(V.reduce_max(rA[:], g5[:], AX.X, negate=True))
                            ve.inc(V.reduce_max(rB[:], g6[:, 0:128], AX.X,
                                                negate=True))
                            w(V, (ve, ve.n))
                            ve.inc(V.tensor_tensor(r1s[:], rA[:], rB[:], ALU.min))
                            w(V, (ve, ve.n))
                            ve.inc(V.tensor_scalar_mul(r1s[:], r1s[:], 8.0))
                            w(A, (ve, ve.n), eA_free, eB_free)
                            ac.inc(A.activation(eAb[:], g5[:], AF.Exp,
                                                bias=r1s[:, 0:1],
                                                scale=self.ck(8.0),
                                                accum_out=sA[:, 0:1]))
                            ac.inc(A.activation(eBb[:], g6[:, 0:128], AF.Exp,
                                                bias=r1s[:, 0:1],
                                                scale=self.ck(8.0),
                                                accum_out=sB[:, 0:1]))
                            self.bank_used("g5", (ac, ac.n))
                            self.bank_used("g6", (ac, ac.n))
                            w(V, (ac, ac.n))
                            ve.inc(V.tensor_add(sA[:], sA[:], sB[:]))
                            w(V, (ve, ve.n))
                            ve.inc(V.reciprocal(sB[:], sA[:]))
                            w(V, (ve, ve.n))
                            ve.inc(V.tensor_scalar(eAb[:], eAb[:], sB[:, 0:1],
                                                   None, ALU.mult))
                            ve.inc(V.tensor_scalar(eBb[:], eBb[:], sB[:, 0:1],
                                                   None, ALU.mult))
                            e_rdy = (ve, ve.n)
                            self.bank_wait(T, "g4")
                            w(T, e_rdy, aT_free)
                            for jt in range(4):
                                mm = T.transpose(g4[:, jt * 128:(jt + 1) * 128],
                                                 eAb[:, jt * 128:(jt + 1) * 128],
                                                 self.c_id[:])
                            mm = T.transpose(g4[:, 512:640], eBb[:], self.c_id[:])
                            pe.inc(mm)
                            eA_free = (pe, pe.n)
                            eB_free = (pe, pe.n)
                            w(V, (pe, pe.n))
                            ve.inc(V.tensor_copy(aT[:, 0:640], g4[:, 0:640]))
                            self.bank_used("g4", (ve, ve.n))
                            w(T, (ve, ve.n), vT_rdy)
                            for jt in range(5):
                                ja = qt + jt
                                slot = prv if ja < 4 else cur
                                lhsT = vTc[slot][:, (h // 2) * 512 +
                                                 (ja % 4) * 128 + hh * 64:
                                                 (h // 2) * 512 +
                                                 (ja % 4) * 128 + hh * 64 + 64]
                                mm = T.matmul(avbank[0:64, qt * 128:(qt + 1) * 128],
                                              lhsT, aT[:, jt * 128:(jt + 1) * 128],
                                              start=(jt == 0), stop=(jt == 4))
                            pe.inc(mm)
                            aT_free = (pe, pe.n)
                    w(V, (pe, pe.n), osb_free)
                    ve.inc(V.tensor_copy(osb[0:64, hp * 512:(hp + 1) * 512],
                                         g7[0:64, :]))
                    ve.inc(V.tensor_copy(osb[64:128, hp * 512:(hp + 1) * 512],
                                         g3[0:64, :]))
                    self.bank_used("g7", (ve, ve.n))
                    self.bank_used("g3", (ve, ve.n))
                o_rdy = (ve, ve.n)
                qf_free[prv] = (pe, pe.n)
                vT_free[prv] = (pe, pe.n)
                qf_free[cur] = (pe, pe.n)
                vT_free[cur] = (pe, pe.n)
                w(T, o_rdy)
                for mi in range(8):
                    bk = banks[mi % 2]
                    self.bank_wait(T, bn[mi % 2])
                    for kk in range(4):
                        mm = T.matmul(bk[:], wob[:, kk * 1024 + mi * 128:
                                                  kk * 1024 + (mi + 1) * 128],
                                      t8(osb, kk), start=(kk == 0), stop=(kk == 3))
                    pe.inc(mm)
                    w(V, (pe, pe.n), xatt_free)
                    ve.inc(V.tensor_add(t8(xatt, mi), bk[:], t8(xin[cur], mi)))
                    self.bank_used(bn[mi % 2], (ve, ve.n))
                osb_free = (pe, pe.n)
                xin_free[cur] = (ve, ve.n)
                xin_free[prv] = (pe, pe.n)
                w(A, (ve, ve.n))
                self.d_xa.inc(A.dma_start(
                    out=self.xatt_d[:, (bi - 1) * 512:bi * 512]
                        .rearrange("(a p) n -> p a n", p=128),
                    in_=xatt.rearrange("p (a n) -> p a n", a=8)), 16)
                xatt_free = (self.d_xa, self.d_xa.n)

    def ffn_passB(self, lay):
        nc = self.nc
        T, V, A, G, SP = self.T, self.V, self.A, self.G, self.SP
        w = self.w
        pe, ve, ac = self.pe, self.ve, self.ac
        g0, g1, g2, g3 = self.g0, self.g1, self.g2, self.g3
        nfb = 11 if lay == 0 else 9
        self.enter_phase()
        est = contextlib.ExitStack()
        sbl = lambda n, sh, dt: est.enter_context(
            nc.sbuf_tensor(f"{n}_{lay}", sh, dt))
        with est:
            w1ab = sbl("b_w1a", [128, 8 * HID2], BF16)
            w1gb = sbl("b_w1g", [128, 8 * HID2], BF16)
            w2pb = sbl("b_w2p", [128, 22 * 1024], BF16)
            xat = sbl("b_xat", [128, 8 * 512], BF16)
            xshb = sbl("b_xsh", [128, 4 * 512], BF16)
            sqf = sbl("b_sqf", [128, 8 * 512], BF16)
            rmu2 = sbl("b_rmu", [1, 512], F32)
            rva2 = sbl("b_rva", [1, 512], F32)
            rtm2 = sbl("b_rtm", [1, 512], F32)
            rmub2 = sbl("b_rmub", [1, 512], BF16)
            rvab2 = sbl("b_rvab", [1, 512], BF16)
            xh = sbl("b_xh", [128, 8 * 512], BF16)
            ab = sbl("b_ab", [128, 22 * 512], BF16)
            ggt = sbl("b_ggt", [128, 512], BF16)
            x2b = sbl("b_x2", [128, 8 * 512], BF16)
            t8 = lambda b, i: b[:, i * 512:(i + 1) * 512]

            w(G, (self.d_xa, self.d_xa.n))
            self.d_wt.inc(G.dma_start(
                out=w1ab.rearrange("p (a n) -> p a n", a=8),
                in_=self.w1av[lay].rearrange("(a p) n -> p a n", p=128)), 16)
            self.d_wt.inc(G.dma_start(
                out=w1gb.rearrange("p (a n) -> p a n", a=8),
                in_=self.w1gv[lay].rearrange("(a p) n -> p a n", p=128)), 16)
            self.d_wt.inc(G.dma_start(
                out=w2pb.rearrange("p (a n) -> p a n", a=22),
                in_=self.w2pv[lay].rearrange("(a p) n -> p a n", p=128)), 16)
            wrdy = (self.d_wt, self.d_wt.n)

            ld_free = (pe, 0)
            ab_free = (pe, 0)
            x2_free = (self.d_x1, 0)
            xh_free = (pe, 0)
            for fb in range(nfb):
                dffs = self.d_ffs if fb % 2 == 0 else self.d_ffs2
                c0 = fb * 512
                csh = max(c0 - 1, 0)
                w(SP, ld_free, self.xatt_dep)
                dffs.inc(SP.dma_start(
                    out=xat.rearrange("p (a n) -> p a n", a=8),
                    in_=self.xatt_d[:, c0:c0 + 512]
                        .rearrange("(a p) n -> p a n", p=128)), 16)
                dffs.inc(SP.dma_start(
                    out=xshb.rearrange("p (a n) -> p a n", a=4),
                    in_=self.xatt_d[512:1024, csh:csh + 512]
                        .rearrange("(a p) n -> p a n", p=128)), 16)
                ldrdy = (dffs, dffs.n)
                hhs = lambda i: t8(xat, i) if i < 4 else t8(xshb, i - 4)
                w(A, ldrdy, (pe, pe.n))
                for i in range(8):
                    ac.inc(A.activation(t8(sqf, i), hhs(i), AF.Square,
                                        bias=self.ck(0.0), scale=self.ck(1.0)))
                self.bank_wait(T, "g2")
                self.bank_wait(T, "g3")
                w(T, ldrdy, (ac, ac.n))
                for i in range(8):
                    T.matmul(g2[0:1, :], self.c_onec[:], hhs(i),
                             start=(i == 0), stop=(i == 7))
                for i in range(8):
                    mm = T.matmul(g3[0:1, :], self.c_onec[:], t8(sqf, i),
                                  start=(i == 0), stop=(i == 7))
                pe.inc(mm)
                w(V, (pe, pe.n), (ac, ac.n))
                ve.inc(V.tensor_scalar_mul(rmu2[:], g2[0:1, :], 1.0 / 1024.0))
                ve.inc(V.tensor_scalar_mul(rva2[:], g3[0:1, :], 1.0 / 1024.0))
                w(V, (ve, ve.n))
                ve.inc(V.tensor_mul(rtm2[:], rmu2[:], rmu2[:]))
                w(V, (ve, ve.n))
                ve.inc(V.tensor_sub(rva2[:], rva2[:], rtm2[:]))
                w(A, (ve, ve.n))
                ac.inc(A.activation(rva2[:], rva2[:], AF.Ln,
                                    bias=self.ck(1e-5, True),
                                    scale=self.ck(1.0, True)))
                w(A, (ac, ac.n))
                ac.inc(A.activation(rva2[:], rva2[:], AF.Exp,
                                    bias=self.ck(0.0, True),
                                    scale=self.ck(-0.5, True)))
                w(V, (ac, ac.n), (pe, pe.n))
                ve.inc(V.tensor_copy(rmub2[:], rmu2[:]))
                ve.inc(V.tensor_copy(rvab2[:], rva2[:]))
                w(T, (ve, ve.n))
                T.matmul(g2[:], self.c_ones[:], rmub2[:], start=True, stop=True)
                mm = T.matmul(g3[:], self.c_ones[:], rvab2[:],
                              start=True, stop=True)
                pe.inc(mm)
                w(V, (pe, pe.n), xh_free)
                for i in range(8):
                    ve.inc(V.tensor_sub(t8(xh, i), hhs(i), g2[:]))
                w(V, (ve, ve.n))
                for i in range(8):
                    ve.inc(V.tensor_mul(t8(xh, i), t8(xh, i), g3[:]))
                w(V, (ve, ve.n))
                for i in range(8):
                    ve.inc(V.tensor_scalar(
                        t8(xh, i), t8(xh, i), self.c_lng[:, lay * 8 + i:
                                                         lay * 8 + i + 1],
                        self.c_lnb[:, lay * 8 + i:lay * 8 + i + 1],
                        ALU.mult, ALU.add))
                self.bank_used("g2", (ve, ve.n))
                self.bank_used("g3", (ve, ve.n))
                xh_rdy = (ve, ve.n)
                banks = [g0, g1]
                bn = ["g0", "g1"]
                w(T, xh_rdy, wrdy, ab_free)
                for hc in range(22):
                    bk = banks[hc % 2]
                    self.bank_wait(T, bn[hc % 2])
                    for kk in range(8):
                        mm = T.matmul(bk[:], w1ab[:, kk * HID2 + hc * 128:
                                                  kk * HID2 + (hc + 1) * 128],
                                      t8(xh, kk), start=(kk == 0), stop=(kk == 7))
                    pe.inc(mm)
                    w(V, (pe, pe.n))
                    ve.inc(V.tensor_scalar(
                        ab[:, hc * 512:(hc + 1) * 512], bk[:],
                        self.c_b1a[:, lay * 22 + hc:lay * 22 + hc + 1],
                        None, ALU.add))
                    self.bank_used(bn[hc % 2], (ve, ve.n))
                for hc in range(22):
                    bk = banks[hc % 2]
                    self.bank_wait(T, bn[hc % 2])
                    for kk in range(8):
                        mm = T.matmul(bk[:], w1gb[:, kk * HID2 + hc * 128:
                                                  kk * HID2 + (hc + 1) * 128],
                                      t8(xh, kk), start=(kk == 0), stop=(kk == 7))
                    pe.inc(mm)
                    w(A, (pe, pe.n), (ve, ve.n))
                    ac.inc(A.activation(
                        ggt[:], bk[:], AF.Gelu,
                        bias=self.c_b1g[:, lay * 22 + hc:lay * 22 + hc + 1],
                        scale=self.ck(1.0)))
                    self.bank_used(bn[hc % 2], (ac, ac.n))
                    w(V, (ac, ac.n))
                    ve.inc(V.tensor_mul(ab[:, hc * 512:(hc + 1) * 512],
                                        ab[:, hc * 512:(hc + 1) * 512], ggt[:]))
                hf_rdy = (ve, ve.n)
                w(T, hf_rdy, x2_free)
                for mi in range(8):
                    bk = banks[mi % 2]
                    self.bank_wait(T, bn[mi % 2])
                    for kk in range(22):
                        mm = T.matmul(bk[:], w2pb[:, kk * 1024 + mi * 128:
                                                  kk * 1024 + (mi + 1) * 128],
                                      ab[:, kk * 512:(kk + 1) * 512],
                                      start=(kk == 0), stop=(kk == 21))
                    pe.inc(mm)
                    w(V, (pe, pe.n))
                    ve.inc(V.scalar_tensor_tensor(
                        t8(x2b, mi), bk[:],
                        self.c_fb2[:, lay * 8 + mi:lay * 8 + mi + 1],
                        t8(xat, mi), ALU.add, ALU.add))
                    self.bank_used(bn[mi % 2], (ve, ve.n))
                ab_free = (pe, pe.n)
                xh_free = (pe, pe.n)
                ld_free = (ve, ve.n)
                if lay == 0:
                    if fb < 3:
                        w(V, (ve, ve.n))
                        for i in range(8):
                            ve.inc(V.tensor_scalar(
                                t8(x2b, i), t8(x2b, i),
                                self.c_x1z[:, 0:1], None, ALU.mult))
                    w(A, (ve, ve.n))
                    self.d_x1.inc(A.dma_start(
                        out=self.x1_d[:, c0:c0 + 512]
                            .rearrange("(a p) n -> p a n", p=128),
                        in_=x2b.rearrange("p (a n) -> p a n", a=8)), 16)
                    x2_free = (self.d_x1, self.d_x1.n)
                else:
                    w(A, (ve, ve.n))
                    for i in range(8):
                        ac.inc(A.activation(t8(x2b, i), t8(x2b, i), AF.Silu,
                                            bias=self.ck(0.0),
                                            scale=self.ck(1.0)))
                    if fb >= 1:
                        w(A, (ac, ac.n))
                        self.d_g.inc(A.dma_start(
                            out=self.g_dd[:, (fb - 1) * 512:fb * 512]
                                .rearrange("(a p) n -> p a n", p=128),
                            in_=x2b.rearrange("p (a n) -> p a n", a=8)), 16)
                        x2_free = (self.d_g, self.d_g.n)
                    else:
                        x2_free = (ac, ac.n)

    def ex1(self):
        G, w = self.G, self.w
        w(G, (self.d_u, self.d_u.n), (self.d_dt, self.d_dt.n),
          (self.d_bc, self.d_bc.n), (self.dS, self.dS.n))
        with G.register("r0") as r0:
            G.reg_load(r0, self.c_par[0:1, 0:1])
            G.reg_mul(r0, r0, -1024 * TOK)
            G.reg_add(r0, r0, 1024 * TOK)
            srcap = bass.AP(self.u_d.tensor, r0, [[TOK, 1024], [1, TOK]])
            self.dZg.inc(G.dma_start(out=self.uoth_c, in_=srcap), 16)
        w(G, (self.dZg, self.dZg.n))
        for cin, cout in ((self.uoth_c, self.uoth_g),
                          (self.dt_own, self.dt_gt),
                          (self.bc_own, self.bc_gt)):
            self.cC.inc(G.collective_compute(
                "AllGather", ALU.bypass,
                replica_groups=[[0, 1], [2, 3], [4, 5], [6, 7]],
                ins=[cin.opt()], outs=[cout.opt()]))

    def assemble1(self):
        SP, w = self.SP, self.w
        w(SP, (self.cC, self.cC.n))
        with SP.register("rp") as rp, SP.register("rt") as rt, \
                SP.register("rs") as rs:
            SP.reg_load(rp, self.c_par[0:1, 0:1])
            SP.reg_mul(rt, rp, 1024 * TOK)
            SP.reg_mul(rs, rp, TOK)
            self.dZs.inc(SP.dma_start(
                out=bass.AP(self.u_sc.tensor, rs, [[8192, 1024], [1, TOK]]),
                in_=bass.AP(self.u_d.tensor, rt, [[TOK, 1024], [1, TOK]])), 16)
            SP.reg_mul(rt, rp, -1024 * TOK)
            SP.reg_add(rt, rt, 1024 * TOK)
            SP.reg_mul(rs, rp, -TOK)
            SP.reg_add(rs, rs, TOK)
            self.dZs.inc(SP.dma_start(
                out=bass.AP(self.u_sc.tensor, rs, [[8192, 1024], [1, TOK]]),
                in_=bass.AP(self.uoth_g.tensor, rt, [[TOK, 1024], [1, TOK]])), 16)
            SP.reg_mul(rs, rp, TOK)
            self.dZs.inc(SP.dma_start(
                out=bass.AP(self.dt_sc.tensor, rs, [[8192, DTR], [1, TOK]]),
                in_=self.dt_own), 16)
            SP.reg_mul(rt, rp, -DTR * TOK)
            SP.reg_add(rt, rt, DTR * TOK)
            SP.reg_mul(rs, rp, -TOK)
            SP.reg_add(rs, rs, TOK)
            self.dZs.inc(SP.dma_start(
                out=bass.AP(self.dt_sc.tensor, rs, [[8192, DTR], [1, TOK]]),
                in_=bass.AP(self.dt_gt.tensor, rt, [[TOK, DTR], [1, TOK]])), 16)
            SP.reg_mul(rs, rp, TOK)
            self.dZs.inc(SP.dma_start(
                out=bass.AP(self.bc_sc.tensor, rs, [[8192, 32], [1, TOK]]),
                in_=self.bc_own), 16)
            SP.reg_mul(rt, rp, -32 * TOK)
            SP.reg_add(rt, rt, 32 * TOK)
            SP.reg_mul(rs, rp, -TOK)
            SP.reg_add(rs, rs, TOK)
            self.dZs.inc(SP.dma_start(
                out=bass.AP(self.bc_sc.tensor, rs, [[8192, 32], [1, TOK]]),
                in_=bass.AP(self.bc_gt.tensor, rt, [[TOK, 32], [1, TOK]])), 16)
        self.scan_in_ready = (self.dZs, self.dZs.n)

    def scan_phase(self):
        nc = self.nc
        T, V, A, G, SP = self.T, self.V, self.A, self.G, self.SP
        w = self.w
        pe, ve, ac, po = self.pe, self.ve, self.ac, self.po
        g0, g1 = self.g0, self.g1
        self.enter_phase()
        est = contextlib.ExitStack()
        sbl = lambda n, sh, dt: est.enter_context(nc.sbuf_tensor(n, sh, dt))
        with est:
            wdt = sbl("s_wdt", [DTR, 1024], BF16)
            brep = sbl("s_brep", [128, 16 * 1024], BF16)
            crep = sbl("s_crep", [128, 16 * 1024], BF16)
            sdt = sbl("s_sdt", [DTR, 1024], F32)
            sdtb = sbl("s_sdtb", [DTR, 1024], BF16)
            sdel = sbl("s_sdel", [128, 1024], F32)
            t_a = sbl("s_ta", [128, 1024], F32)
            t_b = sbl("s_tb", [128, 1024], F32)
            sdu = [sbl(f"s_sdu{i}", [128, 1024], BF16) for i in range(2)]
            sda = [sbl(f"s_sda{i}", [128, 1024], F32) for i in range(2)]
            sdb = [sbl(f"s_sdb{i}", [128, 1024], BF16) for i in range(2)]
            sH = [sbl(f"s_sH{i}", [128, 1024], BF16) for i in range(2)]
            ytm = [sbl(f"s_ytm{i}", [128, 1024], BF16) for i in range(2)]
            ypar = sbl("s_ypar", [128, 12 * 1024], BF16)
            yg = sbl("s_yg", [128, 1024], BF16)
            ug = sbl("s_ug", [128, 1024], BF16)
            Hc = sbl("s_Hc", [128, 128], F32)

            rdy = self.scan_in_ready if hasattr(self, "scan_in_ready") else (self.dZs, 0)
            w(V, rdy)
            ve.inc(V.memset(Hc[:], 0.0))
            w(G, rdy)
            self.d_wt.inc(G.dma_start(out=wdt[:], in_=self.dtproj_t), 16)
            wdt_ready = (self.d_wt, self.d_wt.n)

            bc_free = [(po, 0), (ve, 0)]
            sdt_free = (ve, 0)
            ug_free = (ve, 0)
            sdu_free = [(po, 0), (po, 0)]
            sdel_free = [(ac, 0), (po, 0), (ve, 0)]
            sda_free = [(ve, 0), (ve, 0)]
            sdb_free = [(ve, 0), (ve, 0)]
            sH_free = [(ve, 0), (ve, 0)]
            ytm_free = [(po, 0), (po, 0)]
            ypar_free = (ve, 0)
            yg_free = (self.d_y, 0)

            for s in range(8):
                cols = slice(s * 1024, (s + 1) * 1024)
                w(SP, rdy, sdt_free)
                self.d_sl2.inc(SP.dma_start(out=sdt[:], in_=self.dt_sc[:, cols]), 16)
                w(SP, *bc_free)
                for n in range(16):
                    self.d_sl.inc(SP.dma_start(
                        out=brep[:, n * 1024:(n + 1) * 1024],
                        in_=self.bc_sc[n:n + 1, cols]
                            .partition_broadcast(128).squeeze(1)), 16)
                    self.d_sl.inc(SP.dma_start(
                        out=crep[:, n * 1024:(n + 1) * 1024],
                        in_=self.bc_sc[16 + n:17 + n, cols]
                            .partition_broadcast(128).squeeze(1)), 16)
                bc_ready = (self.d_sl, self.d_sl.n)
                w(V, (self.d_sl2, self.d_sl2.n))
                ve.inc(V.tensor_copy(sdtb[:], sdt[:]))
                sdt_free = (ve, ve.n)
                sdtb_ready = (ve, ve.n)

                for g in range(8):
                    self.bank_wait(T, "g0")
                    self.bank_wait(T, "g1")
                    w(T, sdtb_ready, wdt_ready)
                    m1 = T.matmul(g0[:], wdt[:, g * 128:(g + 1) * 128],
                                  sdtb[:, 0:512], start=True, stop=True)
                    m2 = T.matmul(g1[:], wdt[:, g * 128:(g + 1) * 128],
                                  sdtb[:, 512:1024], start=True, stop=True)
                    pe.inc(m2)
                    w(V, (pe, pe.n), *sdel_free)
                    ve.inc(V.tensor_scalar(sdel[:, 0:512], g0[:],
                                           self.c_dtb[:, g:g + 1], None, ALU.add))
                    ve.inc(V.tensor_scalar(sdel[:, 512:1024], g1[:],
                                           self.c_dtb[:, g:g + 1], None, ALU.add))
                    self.bank_used("g0", (ve, ve.n))
                    self.bank_used("g1", (ve, ve.n))
                    w(A, (ve, ve.n), (ac, ac.n))
                    ac.inc(A.activation(t_a[:], sdel[:], AF.Abs,
                                        bias=self.ck(0.0), scale=self.ck(1.0)))
                    w(A, (ac, ac.n))
                    ac.inc(A.activation(t_b[:], t_a[:], AF.Exp,
                                        bias=self.ck(0.0), scale=self.ck(-1.0)))
                    w(A, (ac, ac.n))
                    ac.inc(A.activation(t_b[:], t_b[:], AF.Ln,
                                        bias=self.ck(1.0), scale=self.ck(1.0)))
                    w(V, (ac, ac.n))
                    ve.inc(V.tensor_scalar(t_a[:], sdel[:], 0.0, None, ALU.max))
                    w(V, (ve, ve.n))
                    ve.inc(V.tensor_add(sdel[:], t_a[:], t_b[:]))
                    sdel_ready = (ve, ve.n)
                    gp = g % 2
                    w(SP, ug_free)
                    self.d_ug.inc(SP.dma_start(
                        out=ug[:], in_=self.u_sc[g * 128:(g + 1) * 128, cols]), 16)
                    w(V, (self.d_ug, self.d_ug.n), sdel_ready, sdu_free[gp])
                    ve.inc(V.tensor_mul(sdu[gp][:], sdel[:], ug[:]))
                    ug_free = (ve, ve.n)
                    sdu_ready = (ve, ve.n)

                    for n in range(16):
                        pa = n % 2
                        col = g * 16 + n
                        w(A, sdel_ready, sda_free[pa])
                        ac.inc(A.activation(sda[pa][:], sdel[:], AF.Exp,
                                            bias=self.ck(0.0),
                                            scale=self.c_A[:, col:col + 1]))
                        w(G, sdu_ready, bc_ready, sdb_free[pa])
                        po.inc(G.tensor_mul(sdb[pa][:], sdu[gp][:],
                                            brep[:, n * 1024:(n + 1) * 1024]))
                        w(V, (ac, ac.n), (po, po.n), sH_free[pa])
                        ve.inc(V.tensor_tensor_scan(
                            sH[pa][:], sda[pa][:], sdb[pa][:],
                            Hc[:, col:col + 1], ALU.mult, ALU.add))
                        sda_free[pa] = (ve, ve.n)
                        sdb_free[pa] = (ve, ve.n)
                        w(V, (ve, ve.n))
                        ve.inc(V.tensor_copy(Hc[:, col:col + 1],
                                             sH[pa][:, 1023:1024]))
                        w(V, ytm_free[pa])
                        ve.inc(V.tensor_mul(ytm[pa][:], sH[pa][:],
                                            crep[:, n * 1024:(n + 1) * 1024]))
                        sH_free[pa] = (ve, ve.n)
                        if pa == 1:
                            w(G, (ve, ve.n), ypar_free)
                            po.inc(G.tensor_add(
                                ypar[:, (n // 2) * 1024:(n // 2 + 1) * 1024],
                                ytm[0][:], ytm[1][:]))
                            ytm_free = [(po, po.n), (po, po.n)]
                    w(V, (po, po.n), yg_free)
                    for i2 in range(4):
                        ve.inc(V.tensor_add(
                            ypar[:, (8 + i2) * 1024:(9 + i2) * 1024],
                            ypar[:, 2 * i2 * 1024:(2 * i2 + 1) * 1024],
                            ypar[:, (2 * i2 + 1) * 1024:(2 * i2 + 2) * 1024]))
                    w(V, (ve, ve.n))
                    for i2 in range(2):
                        ve.inc(V.tensor_add(
                            ypar[:, i2 * 1024:(i2 + 1) * 1024],
                            ypar[:, (8 + 2 * i2) * 1024:(9 + 2 * i2) * 1024],
                            ypar[:, (9 + 2 * i2) * 1024:(10 + 2 * i2) * 1024]))
                    w(V, (ve, ve.n))
                    ve.inc(V.tensor_add(yg[:], ypar[:, 0:1024],
                                        ypar[:, 1024:2048]))
                    ypar_free = (ve, ve.n)
                    sdel_free = [(ac, ac.n), (ve, ve.n)]
                    sdu_free[gp] = (po, po.n)
                    w(A, (ve, ve.n))
                    self.d_y.inc(A.dma_start(
                        out=self.y_dd[g * 128:(g + 1) * 128, cols], in_=yg[:]), 16)
                    yg_free = (self.d_y, self.d_y.n)
                bc_free = [(po, po.n), (ve, ve.n)]
        self.scan_done = (self.d_y, self.d_y.n)

    def ex2(self):
        G, w = self.G, self.w
        w(G, self.scan_done if hasattr(self, "scan_done") else (self.d_y, 0))
        with G.register("r1") as r1:
            G.reg_load(r1, self.c_par[0:1, 0:1])
            G.reg_mul(r1, r1, -TOK)
            G.reg_add(r1, r1, TOK)
            self.dZg.inc(G.dma_start(
                out=self.y_cc,
                in_=bass.AP(self.y_dd.tensor, r1, [[8192, 1024], [1, TOK]])), 16)
        w(G, (self.dZg, self.dZg.n))
        self.cC.inc(G.collective_compute(
            "AllGather", ALU.bypass,
            replica_groups=[[0, 1], [2, 3], [4, 5], [6, 7]],
            ins=[self.y_cc.opt()], outs=[self.y_gg.opt()]))

    def assemble2(self):
        SP, w = self.SP, self.w
        w(SP, (self.cC, self.cC.n))
        with SP.register("rp2") as rp, SP.register("rt2") as rt, \
                SP.register("rs2") as rs:
            SP.reg_load(rp, self.c_par[0:1, 0:1])
            SP.reg_mul(rt, rp, TOK)
            SP.reg_mul(rs, rp, 1024 * TOK)
            self.dZs.inc(SP.dma_start(
                out=bass.AP(self.y_alld.tensor, rs, [[TOK, 1024], [1, TOK]]),
                in_=bass.AP(self.y_dd.tensor, rt, [[8192, 1024], [1, TOK]])), 16)
            SP.reg_mul(rt, rp, -1024 * TOK)
            SP.reg_add(rt, rt, 1024 * TOK)
            self.dZs.inc(SP.dma_start(
                out=bass.AP(self.y_alld.tensor, rt, [[TOK, 1024], [1, TOK]]),
                in_=bass.AP(self.y_gg.tensor, rt, [[TOK, 1024], [1, TOK]])), 16)
        self.y_ready = (self.dZs, self.dZs.n)

    def tail_phase(self):
        nc = self.nc
        T, V, A, G, SP = self.T, self.V, self.A, self.G, self.SP
        w = self.w
        pe, ve, ac = self.pe, self.ve, self.ac
        g0, g1, g2, g3, g5 = self.g0, self.g1, self.g2, self.g3, self.g5
        self.enter_phase()
        est = contextlib.ExitStack()
        sbl = lambda n, sh, dt: est.enter_context(nc.sbuf_tensor(n, sh, dt))
        with est:
            wop = sbl("t_wop", [128, 16 * 1024], BF16)
            wgt = sbl("t_wgt", [128, 8 * 8], BF16)
            ytb = sbl("t_yt", [128, 16 * 512], BF16)
            utb = sbl("t_ut", [128, 16 * 512], BF16)
            rsb = sbl("t_rs", [128, 16 * 512], BF16)
            xrb = sbl("t_xr", [128, 8 * 512], BF16)
            srs = sbl("t_srs", [128, 16 * 512], BF16)
            ytg = sbl("t_ytg", [128, 16 * 512], BF16)
            mot = sbl("t_mot", [128, 8 * 512], F32)
            sqb = sbl("t_sq", [128, 8 * 512], BF16)
            rr = sbl("t_rr", [1, 512], F32)
            rrb = sbl("t_rrb", [1, 512], BF16)
            xmt = sbl("t_xmt", [128, 8 * 512], BF16)
            lgt = sbl("t_lgt", [128, 128], F32)
            ltm = sbl("t_ltm", [128, 32], F32)
            spt = sbl("t_spt", [128, 128], F32)
            spf = sbl("t_spf", [8, 512], F32)
            r1 = sbl("t_r1", [128, 1], F32)
            r2 = sbl("t_r2", [128, 1], F32)
            r3 = sbl("t_r3", [128, 1], F32)
            egt = sbl("t_egt", [128, 8], F32)
            selt = sbl("t_selt", [128, 8], F32)
            mskt = sbl("t_mskt", [128, 8], F32)
            wtt = sbl("t_wtt", [128, 8], F32)

            yrdy = self.y_ready if hasattr(self, "y_ready") else (self.dZs, 0)
            w(G, yrdy, (self.dS, self.dS.n))
            self.d_wt.inc(G.dma_start(
                out=wop.rearrange("p (a n) -> p a n", a=16),
                in_=self.outp_t.rearrange("(a p) n -> p a n", p=128)), 16)
            self.d_wt.inc(G.dma_start(
                out=wgt.rearrange("p (a n) -> p a n", a=8),
                in_=self.gate_t.rearrange("(a p) n -> p a n", p=128)), 16)
            wrdy = (self.d_wt, self.d_wt.n)
            w(V, (self.dS, self.dS.n))
            ve.inc(V.memset(lgt[:], 0.0))
            ve.inc(V.memset(spt[:], 0.0))

            lds_free = [(pe, 0)]
            mot_free = (ac, 0)
            for c in range(8):
                cols = slice(c * 512, (c + 1) * 512)
                w(SP, yrdy, *lds_free)
                self.d_tl.inc(SP.dma_start(
                    out=ytb.rearrange("p (a n) -> p a n", a=16),
                    in_=self.y_alld[:, cols].rearrange("(a p) n -> p a n", p=128)), 16)
                self.d_tl.inc(SP.dma_start(
                    out=utb.rearrange("p (a n) -> p a n", a=16),
                    in_=self.u_d[:, cols].rearrange("(a p) n -> p a n", p=128)), 16)
                self.d_tl.inc(SP.dma_start(
                    out=rsb.rearrange("p (a n) -> p a n", a=16),
                    in_=self.res_d[:, cols].rearrange("(a p) n -> p a n", p=128)), 16)
                self.d_tl.inc(SP.dma_start(
                    out=xrb.rearrange("p (a n) -> p a n", a=8),
                    in_=self.xr1_d[:, cols].rearrange("(a p) n -> p a n", p=128)), 16)
                ldrdy = (self.d_tl, self.d_tl.n)
                t16 = lambda b, i: b[:, i * 512:(i + 1) * 512]
                w(A, ldrdy, (ve, ve.n))
                for j in range(16):
                    ac.inc(A.activation(t16(srs, j), t16(rsb, j), AF.Silu,
                                        bias=self.ck(0.0), scale=self.ck(1.0)))
                w(V, ldrdy, (pe, pe.n))
                for j in range(16):
                    ve.inc(V.scalar_tensor_tensor(
                        t16(ytg, j), t16(utb, j), self.c_dvec[:, j:j + 1],
                        t16(ytb, j), ALU.mult, ALU.add))
                w(V, (ve, ve.n), (ac, ac.n))
                for j in range(16):
                    ve.inc(V.tensor_mul(t16(ytg, j), t16(ytg, j), t16(srs, j)))
                ytg_rdy = (ve, ve.n)
                banks = [g0, g1]
                bn = ["g0", "g1"]
                w(T, ytg_rdy, wrdy)
                for mi in range(8):
                    bk = banks[mi % 2]
                    self.bank_wait(T, bn[mi % 2])
                    for kk in range(16):
                        mm = T.matmul(bk[:], wop[:, kk * 1024 + mi * 128:
                                                 kk * 1024 + (mi + 1) * 128],
                                      t16(ytg, kk), start=(kk == 0), stop=(kk == 15))
                    pe.inc(mm)
                    w(V, (pe, pe.n), mot_free)
                    ve.inc(V.tensor_add(t16(mot, mi), bk[:],
                                        xrb[:, mi * 512:(mi + 1) * 512]))
                    self.bank_used(bn[mi % 2], (ve, ve.n))
                mot_rdy = (ve, ve.n)
                lds_free = [mot_rdy]
                w(A, mot_rdy, (self.d_xm, self.d_xm.n))
                for i in range(8):
                    ac.inc(A.activation(t16(sqb, i), t16(mot, i), AF.Square,
                                        bias=self.ck(0.0), scale=self.ck(1.0)))
                self.bank_wait(T, "g2")
                w(T, (ac, ac.n))
                for i in range(8):
                    mm = T.matmul(g2[0:1, :], self.c_onec[:], t16(sqb, i),
                                  start=(i == 0), stop=(i == 7))
                pe.inc(mm)
                w(V, (pe, pe.n), (ac, ac.n))
                ve.inc(V.tensor_scalar_max(rr[:], g2[0:1, :], 1e-24))
                self.bank_used("g2", (ve, ve.n))
                w(A, (ve, ve.n))
                ac.inc(A.activation(rr[:], rr[:], AF.Ln,
                                    bias=self.ck(0.0, True), scale=self.ck(1.0, True)))
                w(A, (ac, ac.n))
                ac.inc(A.activation(rr[:], rr[:], AF.Exp,
                                    bias=self.ck(LN32, True), scale=self.ck(-0.5, True)))
                w(V, (ac, ac.n), (pe, pe.n))
                ve.inc(V.tensor_copy(rrb[:], rr[:]))
                self.bank_wait(T, "g3")
                w(T, (ve, ve.n))
                mm = T.matmul(g3[:], self.c_ones[:], rrb[:], start=True, stop=True)
                pe.inc(mm)
                w(V, (pe, pe.n), (self.d_xm, self.d_xm.n))
                for i in range(8):
                    ve.inc(V.tensor_mul(t16(xmt, i), t16(mot, i), g3[:]))
                self.bank_used("g3", (ve, ve.n))
                mot_free = (ve, ve.n)
                xmt_rdy = (ve, ve.n)
                w(A, xmt_rdy)
                self.d_xm.inc(A.dma_start(
                    out=self.xm_d[:, cols].rearrange("(a p) n -> p a n", p=128),
                    in_=xmt.rearrange("p (a n) -> p a n", a=8)), 16)
                self.bank_wait(T, "g2")
                w(T, xmt_rdy, wrdy)
                for kk in range(8):
                    mm = T.matmul(g2[0:8, :], wgt[:, kk * 8:(kk + 1) * 8],
                                  t16(xmt, kk), start=(kk == 0), stop=(kk == 7))
                pe.inc(mm)
                for tt in range(4):
                    w(V, (pe, pe.n), (ve, ve.n))
                    ve.inc(V.tensor_copy(lgt[0:8, 0:128],
                                         g2[0:8, tt * 128:(tt + 1) * 128]))
                    if tt == 3:
                        self.bank_used("g2", (ve, ve.n))
                    self.bank_wait(T, "g5")
                    w(T, (ve, ve.n))
                    mm = T.transpose(g5[:, 0:128], lgt[:], self.c_id32[:])
                    pe.inc(mm)
                    w(V, (pe, pe.n))
                    ve.inc(V.tensor_copy(ltm[:, 0:8], g5[:, 0:8]))
                    self.bank_used("g5", (ve, ve.n))
                    lt = ltm[:, 0:8]
                    w(V, (ve, ve.n))
                    ve.inc(V.reduce_max(r1[:], lt, AX.X, negate=True))
                    w(V, (ve, ve.n))
                    ve.inc(V.tensor_scalar_mul(r2[:], r1[:], -1.0))
                    w(A, (ve, ve.n))
                    ac.inc(A.activation(egt[:], lt, AF.Exp,
                                        bias=r1[:, 0:1], scale=self.ck(1.0)))
                    w(V, (ve, ve.n))
                    ve.inc(V.tensor_scalar(selt[:], lt, r2[:, 0:1], None,
                                           ALU.is_ge))
                    w(V, (ve, ve.n))
                    ve.inc(V.scalar_tensor_tensor(mskt[:], selt[:], -1e30,
                                                  lt, ALU.mult, ALU.add))
                    w(V, (ve, ve.n))
                    ve.inc(V.reduce_max(r3[:], mskt[:], AX.X))
                    w(V, (ve, ve.n))
                    ve.inc(V.tensor_scalar(selt[:], lt, r3[:, 0:1], None,
                                           ALU.is_ge))
                    w(V, (ve, ve.n), (ac, ac.n))
                    ve.inc(V.tensor_mul(wtt[:], egt[:], selt[:]))
                    w(V, (ve, ve.n))
                    ve.inc(V.reduce_sum(r3[:], wtt[:], AX.X))
                    w(V, (ve, ve.n))
                    ve.inc(V.reciprocal(r2[:], r3[:]))
                    w(V, (ve, ve.n))
                    ve.inc(V.tensor_scalar(spt[:, 0:8], wtt[:], r2[:, 0:1],
                                           None, ALU.mult))
                    self.bank_wait(T, "g5")
                    w(T, (ve, ve.n))
                    mm = T.transpose(g5[:, 0:128], spt[:], self.c_id32[:])
                    pe.inc(mm)
                    w(V, (pe, pe.n), (self.d_sp, self.d_sp.n))
                    ve.inc(V.tensor_copy(spf[0:8, tt * 128:(tt + 1) * 128],
                                         g5[0:8, 0:128]))
                    self.bank_used("g5", (ve, ve.n))
                w(A, (ve, ve.n))
                self.d_sp.inc(A.dma_start(out=self.sp_d[:, cols], in_=spf[0:8, :]), 16)
        self.tail_done = ((self.d_xm, self.d_xm.n), (self.d_sp, self.d_sp.n))

    def moe_phase(self):
        nc = self.nc
        T, V, A, G, SP = self.T, self.V, self.A, self.G, self.SP
        w = self.w
        pe, ve, ac = self.pe, self.ve, self.ac
        g0, g1, g2, g3 = self.g0, self.g1, self.g2, self.g3
        self.enter_phase()
        est = contextlib.ExitStack()
        sbl = lambda n, sh, dt: est.enter_context(nc.sbuf_tensor(n, sh, dt))
        with est:
            w1b = [sbl(f"m_w1{i}", [128, 8 * 2048], BF16) for i in range(2)]
            w2b = [sbl(f"m_w2{i}", [128, 16 * 1024], BF16) for i in range(2)]
            xmb = [sbl(f"m_xm{i}", [128, 8 * 512], BF16) for i in range(2)]
            hfb = sbl("m_hf", [128, 16 * 512], BF16)
            sprep = sbl("m_sp", [128, TOK], BF16)
            ysb = [sbl(f"m_ysb{i}", [128, 512], BF16) for i in range(2)]

            tdone = self.tail_done if hasattr(self, "tail_done") else ()
            w(G, *tdone)
            wl = [self.dG1, self.dG2]
            wrdy = [(self.dG1, 0), (self.dG2, 0)]
            wfree = [(pe, 0), (pe, 0)]

            def load_w(e, h, slot):
                self.w(G, wfree[slot], *tdone)
                wl[slot].inc(G.dma_start(
                    out=w1b[slot].rearrange("p (a n) -> p a n", a=8),
                    in_=self.mw1[e, :, h * 2048:(h + 1) * 2048]
                        .rearrange("(a p) n -> p a n", p=128)), 16)
                wl[slot].inc(G.dma_start(
                    out=w2b[slot].rearrange("p (a n) -> p a n", a=16),
                    in_=self.mw2[e, h * 2048:(h + 1) * 2048, :]
                        .rearrange("(a p) n -> p a n", p=128)), 16)
                wrdy[slot] = (wl[slot], wl[slot].n)

            load_w(0, 0, 0)
            sprep_rdy = (self.d_sp2, 0)
            sprep_free = (ve, 0)
            xm_free = [(pe, 0), (pe, 0)]
            hf_free = [(pe, 0)] * 16
            ysb_free = [(self.d_ym, 0), (self.d_ym2, 0)]
            accum_guard = [(self.d_ym, 0), (self.d_ym2, 0)]

            step = 0
            for e in range(8):
                w(G, sprep_free, *tdone)
                self.d_sp2.inc(G.dma_start(
                    out=sprep[:],
                    in_=self.sp_d[e:e + 1, :].partition_broadcast(128)
                        .squeeze(1)), 16)
                sprep_rdy = (self.d_sp2, self.d_sp2.n)
                for h in range(2):
                    slot = step % 2
                    nstep = step + 1
                    if nstep < 16:
                        ne, nh = nstep // 2, nstep % 2
                        load_w(ne, nh, nstep % 2)
                    for nt in range(8):
                        xs = nt % 2
                        dmx = self.d_mx if xs == 0 else self.d_mx2
                        w(SP, xm_free[xs], *tdone)
                        dmx.inc(SP.dma_start(
                            out=xmb[xs].rearrange("p (a n) -> p a n", a=8),
                            in_=self.xm_d[:, nt * 512:(nt + 1) * 512]
                                .rearrange("(a p) n -> p a n", p=128)), 16)
                        xmrdy = (dmx, dmx.n)
                        bk = [g0, g1]
                        bn = ["g0", "g1"]
                        w(T, xmrdy, wrdy[slot])
                        for hc in range(16):
                            b_ = bk[hc % 2]
                            self.bank_wait(T, bn[hc % 2])
                            for dc in range(8):
                                mm = T.matmul(
                                    b_[:], w1b[slot][:, dc * 2048 + hc * 128:
                                                     dc * 2048 + (hc + 1) * 128],
                                    xmb[xs][:, dc * 512:(dc + 1) * 512],
                                    start=(dc == 0), stop=(dc == 7))
                            pe.inc(mm)
                            w(A, (pe, pe.n), hf_free[hc])
                            ac.inc(A.activation(
                                hfb[:, hc * 512:(hc + 1) * 512], b_[:], AF.Gelu,
                                bias=self.ck(0.0), scale=self.ck(1.0)))
                            self.bank_used(bn[hc % 2], (ac, ac.n))
                        hf_rdy = (ac, ac.n)
                        bk2 = [g2, g3]
                        bn2 = ["g2", "g3"]
                        w(T, hf_rdy)
                        for dc in range(8):
                            b_ = bk2[dc % 2]
                            self.bank_wait(T, bn2[dc % 2])
                            for hc in range(16):
                                mm = T.matmul(
                                    b_[:], w2b[slot][:, hc * 1024 + dc * 128:
                                                     hc * 1024 + (dc + 1) * 128],
                                    hfb[:, hc * 512:(hc + 1) * 512],
                                    start=(hc == 0), stop=(hc == 15))
                            pe.inc(mm)
                            hf_free = [(pe, pe.n)] * 16
                            ys = dc % 2
                            w(V, (pe, pe.n), sprep_rdy, ysb_free[ys])
                            ve.inc(V.tensor_mul(
                                ysb[ys][:], b_[:],
                                sprep[:, nt * 512:(nt + 1) * 512]))
                            self.bank_used(bn2[dc % 2], (ve, ve.n))
                            dym = self.d_ym if ys == 0 else self.d_ym2
                            w(G, (ve, ve.n), *accum_guard)
                            op = ALU.bypass if step == 0 else ALU.add
                            dym.inc(G.dma_start(
                                out=self.ymoe_d[dc * 128:(dc + 1) * 128,
                                                nt * 512:(nt + 1) * 512],
                                in_=ysb[ys][:], accum_op=op), 16)
                            ysb_free[ys] = (dym, dym.n)
                        xm_free[xs] = (pe, pe.n)
                    wfree[slot] = (pe, pe.n)
                    accum_guard = [(self.d_ym, self.d_ym.n),
                                   (self.d_ym2, self.d_ym2.n)]
                    step += 1
                sprep_free = (ve, ve.n)
        self.moe_done = (self.d_ym, self.d_ym.n)
        self.moe_done2 = (self.d_ym2, self.d_ym2.n)

    def combine_phase(self):
        nc = self.nc
        V, A, SP, G = self.V, self.A, self.SP, self.G
        w = self.w
        ve = self.ve
        self.enter_phase()
        est = contextlib.ExitStack()
        sbl = lambda n, sh, dt: est.enter_context(nc.sbuf_tensor(n, sh, dt))
        with est:
            ymb = sbl("c_ym", [128, 8 * 512], BF16)
            xnb = sbl("c_xn", [128, 8 * 512], BF16)
            ggb = sbl("c_gg", [128, 8 * 512], BF16)
            xxb = sbl("c_xx", [128, 8 * 512], F32)
            sgb = sbl("c_sg", [128, 8 * 512], F32)
            otb = sbl("c_ot", [128, 8 * 512], F32)
            mdone = self.moe_done if hasattr(self, "moe_done") else (self.d_ym, 0)
            mdone2 = getattr(self, "moe_done2", (self.d_ym2, 0))
            ld_free = (ve, 0)
            ot_free = (self.d_out, 0)
            for c in range(8):
                cols = slice(c * 512, (c + 1) * 512)
                xcols = slice(HALO + c * 512, HALO + (c + 1) * 512)
                w(SP, mdone, mdone2, ld_free,
                  self.ffn_done if hasattr(self, "ffn_done")
                  else (self.d_g, 0))
                self.d_cl.inc(SP.dma_start(
                    out=ymb.rearrange("p (a n) -> p a n", a=8),
                    in_=self.ymoe_d[:, cols].rearrange("(a p) n -> p a n", p=128)), 16)
                self.d_cl.inc(SP.dma_start(
                    out=xnb.rearrange("p (a n) -> p a n", a=8),
                    in_=self.xn_d[:, xcols].rearrange("(a p) n -> p a n", p=128)), 16)
                self.d_cl.inc(SP.dma_start(
                    out=ggb.rearrange("p (a n) -> p a n", a=8),
                    in_=self.g_dd[:, cols].rearrange("(a p) n -> p a n", p=128)), 16)
                self.d_cl.inc(SP.dma_start(
                    out=xxb.rearrange("p (a n) -> p a n", a=8),
                    in_=self.xfm[:, xcols].rearrange("(a p) n -> p a n", p=128)), 16)
                ldrdy = (self.d_cl, self.d_cl.n)
                t8 = lambda b, i: b[:, i * 512:(i + 1) * 512]
                w(V, ldrdy, ot_free)
                for i in range(8):
                    ve.inc(V.tensor_add(t8(sgb, i), t8(ymb, i), t8(xnb, i)))
                w(V, (ve, ve.n))
                for i in range(8):
                    ve.inc(V.tensor_mul(t8(sgb, i), t8(sgb, i), t8(ggb, i)))
                w(V, (ve, ve.n))
                for i in range(8):
                    ve.inc(V.tensor_add(t8(otb, i), t8(sgb, i), t8(xxb, i)))
                ld_free = (ve, ve.n)
                w(A, (ve, ve.n))
                self.d_out.inc(A.dma_start(
                    out=self.out_fm[:, cols].rearrange("(a p) n -> p a n", p=128),
                    in_=otb.rearrange("p (a n) -> p a n", a=8)), 16)
                ot_free = (self.d_out, self.d_out.n)

    def build(self):
        self.declare()
        with self.st:
            self.alloc()
            with self.nc.Block():
                self.p0_consts()
                if "p1" in self.phases:
                    self.p1_front()
                if "ex1" in self.phases:
                    self.ex1(); self.assemble1()
                if "ffn" in self.phases:
                    self.ffn_branch()
                if "scan" in self.phases:
                    self.scan_phase()
                if "ex2" in self.phases:
                    self.ex2(); self.assemble2()
                if "tail" in self.phases:
                    self.tail_phase()
                if "moe" in self.phases:
                    self.moe_phase()
                if "comb" in self.phases:
                    self.combine_phase()
                for s_ in ("d_xn", "d_xr", "d_res", "d_dt", "d_y", "d_xa",
                           "d_x1", "d_g", "d_xm", "d_sp", "d_out"):
                    self.w(self.A, (getattr(self, s_), getattr(self, s_).n))
                for s_ in ("dG1", "dG2", "d_wt", "d_u", "d_bc", "d_ym",
                           "d_ym2", "d_sp2", "dZg"):
                    self.w(self.G, (getattr(self, s_), getattr(self, s_).n))
                for s_ in ("dS", "d_sl", "d_sl2", "d_ug", "d_ffs", "d_ffs2",
                           "d_tl", "d_mx", "d_mx2", "d_cl", "dZs"):
                    self.w(self.SP, (getattr(self, s_), getattr(self, s_).n))
        return self.nc


def _host_prep(inputs):
    x = inputs["L0_attn_x"]
    f32 = np.float32
    cf = np.ascontiguousarray

    def bft(a):
        return cf(a.astype(BFNP))

    def rot_tabs(S):
        t = np.arange(S - 2048, S + 4096, dtype=np.float64)
        hi = np.arange(0, 64, 2, dtype=np.float64)
        inv = 1.0 / 10000.0 ** (hi / 64.0)
        fr = t[:, None] * inv[None, :]
        fr = np.concatenate([fr, fr], -1)
        base = (hi + 0.4 * 64) / (1.4 * 64)
        power = (t - 4096.0) / 256.0
        sc = base[None, :] ** power[:, None]
        sc = np.concatenate([sc, sc], -1)
        cos, sin = np.cos(fr), np.sin(fr)
        r = lambda a: bft(np.tile(a.T.astype(f32), (2, 1)))
        return r(cos * sc), r(sin * sc), r(cos / sc), r(sin / sc)

    ii = np.arange(128)[:, None]
    jj = np.arange(128)[None, :]
    m_triu = np.where(jj >= ii, 0.0, NEG).astype(f32)
    m_tril = np.where(jj <= ii, 0.0, NEG).astype(f32)
    ident = np.eye(128, dtype=BFNP)
    ident32 = np.eye(128, dtype=f32)
    ones_r = np.ones((1, 128), BFNP)
    ind2 = np.zeros((128, 2), BFNP)
    ind2[0:64, 0] = 1
    ind2[64:128, 1] = 1
    rep2 = np.zeros((2, 128), BFNP)
    rep2[0, 0:64] = 1
    rep2[1, 64:128] = 1
    P64 = np.zeros((64, 64), f32)
    P64[0:32, 32:64] = -np.eye(32)
    P64[32:64, 0:32] = np.eye(32)
    rotp = np.zeros((128, 128), f32)
    rotp[0:64, 0:64] = P64
    rotp[64:128, 64:128] = P64
    rot_pt = bft(rotp.T)
    ckarr = np.tile(np.array(CKVALS, f32)[None, :], (128, 1))

    convw = inputs["L0_attn_conv_w"][:, 0, :]
    convw_t = cf(convw.reshape(16, 128, 4).transpose(1, 0, 2)
                 .reshape(128, 64).astype(f32))
    convb_t = cf(inputs["L0_attn_conv_b"].reshape(16, 128).T.astype(f32))
    dvec_t = cf(inputs["L0_attn_D"].reshape(16, 128).T.astype(f32))
    qsc = inputs["L0_ffn_q_scale"]
    qsc_t = cf(np.tile(qsc.T, (2, 1)).astype(f32))
    ksc_t = cf(np.tile(inputs["L0_ffn_k_scale"].T, (2, 1)).astype(f32))
    lng_t = cf(inputs["L0_ffn_ln_g"].reshape(2, 8, 128)
               .transpose(2, 0, 1).reshape(128, 16).astype(f32))
    lnb_t = cf(inputs["L0_ffn_ln_b"].reshape(2, 8, 128)
               .transpose(2, 0, 1).reshape(128, 16).astype(f32))
    fw1 = inputs["L0_ffn_w1"]
    fb1 = inputs["L0_ffn_b1"]
    w1a = np.zeros((2, DIM, HID2), f32)
    w1g = np.zeros((2, DIM, HID2), f32)
    w1a[:, :, 0:2730] = fw1[:, :, 0:2730]
    w1g[:, :, 0:2730] = fw1[:, :, 2730:5460]
    b1a = np.zeros((2, HID2), f32)
    b1g = np.zeros((2, HID2), f32)
    b1a[:, 0:2730] = fb1[:, 0:2730]
    b1g[:, 0:2730] = fb1[:, 2730:5460]
    b1a_t = cf(b1a.reshape(2, 22, 128).transpose(2, 0, 1).reshape(128, 44))
    b1g_t = cf(b1g.reshape(2, 22, 128).transpose(2, 0, 1).reshape(128, 44))
    w2pad = np.zeros((2, HID2, DIM), f32)
    w2pad[:, 0:2730, :] = inputs["L0_ffn_w2"]
    fb2_t = cf(inputs["L0_ffn_b2"].reshape(2, 8, 128)
               .transpose(2, 0, 1).reshape(128, 16).astype(f32))
    inp_t = cf(inputs["L0_attn_in_proj_w"].T.astype(f32))
    xproj_t = np.zeros((DI, 128), f32)
    xproj_t[:, 0:96] = inputs["L0_attn_x_proj_w"].T
    outp_t = cf(inputs["L0_attn_out_proj_w"].T.astype(f32))
    gate_t = cf(inputs["L0_attn_gate_w"].T.astype(f32))
    mw1 = cf(inputs["L0_attn_w1"].transpose(0, 2, 1).astype(f32))
    mw2 = cf(inputs["L0_attn_w2"].transpose(0, 2, 1).astype(f32))
    dtp_t_full = inputs["L0_attn_dt_proj_w"].T.astype(f32)

    shared = dict(
        m_triu=m_triu, m_tril=m_tril, identv=ident, onesrv=ones_r,
        ind2v=ind2, rep2v=rep2, rotptv=rot_pt, ckv=ckarr, convwv=convw_t,
        convbv=convb_t, dvecv=dvec_t, qscv=qsc_t, kscv=ksc_t, lngv=lng_t,
        lnbv=lnb_t, b1av=b1a_t, b1gv=b1g_t, fb2v=fb2_t,
        ident32=ident32,
        wqkv=cf(inputs["L0_ffn_wqkv"].astype(f32)),
        wo_w=cf(inputs["L0_ffn_wo"].astype(f32)),
        w1av=cf(w1a), w1gv=cf(w1g), w2pv=cf(w2pad),
        inp_t=inp_t, xproj_t=xproj_t, outp_t=outp_t, gate_t=gate_t,
        mw1=mw1, mw2=mw2,
    )
    in_maps = []
    for core in range(8):
        b, p = core // 2, core % 2
        Sa = p * 4096
        lo = Sa - HALO
        xs = np.zeros((TOTC, DIM), f32)
        src_lo = max(lo, 0)
        xs[src_lo - lo:] = x[b, src_lo:Sa + 4096]
        tqc, tqs, tkc, tks = rot_tabs(Sa)
        dsh = slice(p * 1024, p * 1024 + 1024)
        m = dict(shared)
        m.update(
            xfm=cf(xs.T),
            tqc=tqc, tqs=tqs, tkc=tkc, tks=tks,
            m_w0=np.full((128, 512), NEGW0 if p == 0 else 0.0, f32),
            x1zv=np.full((128, 1), 0.0 if p == 0 else 1.0, f32),
            parv=np.array([[p]], np.int32),
            dtbv=cf(inputs["L0_attn_dt_proj_b"][dsh].reshape(8, 128).T
                    .astype(f32)),
            alogv=cf(inputs["L0_attn_A_log"][dsh].astype(f32)),
            dtproj_t=cf(dtp_t_full[:, dsh]),
        )
        in_maps.append(m)
    return in_maps


def kernel(**inputs):
    in_maps = _host_prep(inputs)
    nc = K().build()
    res = run_bass_kernel_spmd(nc, in_maps, core_ids=list(range(8)))
    out = np.zeros((4, 8192, DIM), np.float32)
    for core in range(8):
        b, p = core // 2, core % 2
        out[b, p * 4096:(p + 1) * 4096] = res.results[core]["out_fm"].T
    return out
